# version 02
# baseline (speedup 1.0000x reference)
import math
import os
import numpy as np
import concourse.bass as bass
import concourse.mybir as mybir
from concourse.bass_utils import run_bass_kernel_spmd
from concourse.alu_op_type import AluOpType
from contextlib import ExitStack

F32 = mybir.dt.float32
BF16 = mybir.dt.bfloat16
AF = mybir.ActivationFunctionType
AX = mybir.AxisListType.X
OP = AluOpType

T, D, NT, ND = 2048, 1024, 16, 8
KPH = int(os.environ.get("KPH", "99"))

TH = [0.24419428096993973, 0.4639691338428854, 0.6861859295255306,
      0.9059607823984763, 1.0839784132255623]

AF_W = 19456
XO = 0
FS = 16384
ST = 18432

AB_W = 58368
XTO = 0
RB = 16384
CB = 54784
C_ID = CB
C_ONE = CB + 128
C_BV = CB + 256
C_M0 = CB + 1280
C_M1 = CB + 1536
C_UPQ = CB + 1792
C_UPQS = CB + 1920
C_UPK = CB + 2048
C_UPKS = CB + 2176
C_UPV1 = CB + 2304
XBF = 57344

S_NEGMAX = ST + 0
S_R = ST + 2
S_RREC = ST + 4
S_GREC = ST + 6
S_SUM = ST + 8
S_SS = ST + 10
S_NMEAN = ST + 12
S_M2 = ST + 14
S_VAR = ST + 16
S_STD = ST + 18
S_RSTD = ST + 20
S_EPS = ST + 22
S_JUNK = ST + 23
S_GB = ST + 24
S_HPI = ST + 25
S_BQA = ST + 26
S_BQB = ST + 27
S_BKA = ST + 28
S_BKB = ST + 29


def build_nc():
    nc = bass.Bass("TRN2", target_bir_lowering=False)
    dr = {}

    def din(name, shape, dt=F32):
        dr[name] = nc.dram_tensor(name, shape, dt, kind="ExternalInput").ap()
        return dr[name]

    x_d = din("x", [T, D])
    wq_d = din("wq_t", [D, 256])
    bq_d = din("bq_pad", [256, 1])
    wk_d = din("wk_t", [D, 256])
    bk_d = din("bk_pad", [256, 1])
    wv_d = din("wv_t", [D, D])
    bv_d = din("bv_r", [1, D])
    gam_d = din("gam", [128, 1])
    wfc_d = din("wfc_t", [D, 4096])
    wpj_d = din("wpj_t", [4096, D])
    wqd_d = din("wqd", [D, D])
    wkd_d = din("wkd", [D, D])
    wvd_d = din("wvd", [D, D])
    upq_d = din("upq", [128, 128])
    upqs_d = din("upqs", [128, 128])
    upk_d = din("upk", [128, 128])
    upks_d = din("upks", [128, 128])
    wvu_d = din("wvu", [64, 64])
    wc_d = din("wc", [D, D])
    th_d = din("th", [D, D])
    tn_d = din("tn", [D, D])
    ph_d = din("ph", [D, D])
    pn_d = din("pn", [D, D])
    id_d = din("identM", [128, 128])
    rc_d = din("ropec", [128, T])
    rs_d = din("ropes", [128, T])
    m0_d = din("mask0", [128, 256])
    m1_d = din("mask1", [128, 256])
    qr_sp = nc.dram_tensor("qr_sp", [1024, T], BF16, kind="Internal").ap()
    kr_sp = nc.dram_tensor("kr_sp", [1024, T], BF16, kind="Internal").ap()
    lv_sp = nc.dram_tensor("lv_sp", [1024, T], BF16, kind="Internal").ap()
    out_d = nc.dram_tensor("out", [T, D], F32, kind="ExternalOutput").ap()

    with ExitStack() as stk:
        E = stk.enter_context
        Af = E(nc.sbuf_tensor("arena_f", [128, AF_W], F32))
        Ab = E(nc.sbuf_tensor("arena_b", [128, AB_W], BF16))
        P = [E(nc.psum_tensor(f"p{i}", [128, 512], F32)) for i in range(6)]
        PT = [E(nc.psum_tensor(f"pt{i}", [128, 1024], BF16)) for i in range(2)]
        sems = {k: E(nc.semaphore("sem_" + k))
                for k in ("pe", "ac", "dv", "gp", "sy", "st")}
        block = E(nc.Block())

        ops = {k: [] for k in ("pe", "ac", "dv", "gp", "sy")}
        cnt = {k: 0 for k in ops}
        waited = {}
        ready = {}
        readers = {}

        def W(eng, prod, val):
            if val <= 0:
                return
            key = (eng, prod)
            if waited.get(key, -1) >= val:
                return
            waited[key] = val
            s = sems[prod]
            ops[eng].append(lambda e, s=s, v=val: e.wait_ge(s, v))

        def need(res, eng):
            r = ready.get(res)
            if r and r[0] != eng:
                W(eng, r[0], r[1])

        def acquire(res, eng):
            r = ready.get(res)
            if r and r[0] != eng:
                W(eng, r[0], r[1])
            for e2, v in readers.get(res, {}).items():
                if e2 != eng:
                    W(eng, e2, v)

        def EMIT(eng, fn, reads=(), writes=(), inc=1, sem=None):
            for r in reads:
                need(r, eng)
            for w in writes:
                acquire(w, eng)
            cnt[eng] += inc
            if inc:
                s = sems[sem if sem else eng]
                ops[eng].append(lambda e, f=fn, s=s, n=inc: f(e).then_inc(s, n))
            else:
                ops[eng].append(fn)
            for r in reads:
                readers.setdefault(r, {})[eng] = cnt[eng]
            for w in writes:
                ready[w] = (eng, cnt[eng])
                readers[w] = {}

        def MARK(eng, *names):
            for nm in names:
                ready[nm] = (eng, cnt[eng])
                readers[nm] = {}

        def barrier():
            for c in ops:
                for p in ops:
                    if p != c:
                        W(c, p, cnt[p])

        def af(off, w, p0=0, pn=128):
            return Af[p0:p0 + pn, off:off + w]

        def ab(off, w, p0=0, pn=128):
            return Ab[p0:p0 + pn, off:off + w]

        def Xt(t, c0=0, w=1024):
            return Af[:, XO + t * 1024 + c0: XO + t * 1024 + c0 + w]

        def XTk(k, c0, w):
            return Ab[:, XTO + k * 2048 + c0: XTO + k * 2048 + c0 + w]

        def col(o, n=1, p0=0, pn=128):
            return Af[p0:p0 + pn, o:o + n]

        def xtr(c0, w):
            t0, t1 = c0 // 128, (c0 + w - 1) // 128
            return [f"XT{t}" for t in range(t0, t1 + 1)]

        for t in range(NT):
            EMIT("sy", lambda e, t=t: e.dma_start(Xt(t), x_d[t * 128:(t + 1) * 128, :]),
                 writes=[f"X{t}"], inc=16)
        for (off, w, d_, nm, p0, pn) in (
                (C_ID, 128, id_d, "CID", 0, 128), (C_BV, 1024, bv_d, "CBV", 0, 1),
                (C_M0, 256, m0_d, "CM0", 0, 128), (C_M1, 256, m1_d, "CM1", 0, 128),
                (C_UPQ, 128, upq_d, "CUPQ", 0, 128), (C_UPQS, 128, upqs_d, "CUPQS", 0, 128),
                (C_UPK, 128, upk_d, "CUPK", 0, 128), (C_UPKS, 128, upks_d, "CUPKS", 0, 128),
                (C_UPV1, 64, wvu_d, "CUPV", 0, 64)):
            EMIT("gp", lambda e, off=off, w=w, d_=d_, p0=p0, pn=pn: e.dma_start(
                ab(off, w, p0, pn), d_), writes=[nm], inc=16)
        for (o, d_, nm) in ((S_GB, gam_d, "SGB"),):
            EMIT("sy", lambda e, o=o, d_=d_: e.dma_start(col(o), d_), writes=[nm], inc=16)
        for (o, d_, sl, nm) in ((S_BQA, bq_d, 0, "SBQA"), (S_BQB, bq_d, 1, "SBQB"),
                                (S_BKA, bk_d, 0, "SBKA"), (S_BKB, bk_d, 1, "SBKB")):
            EMIT("sy", lambda e, o=o, d_=d_, sl=sl: e.dma_start(
                col(o), d_[sl * 128:(sl + 1) * 128, :]), writes=[nm], inc=16)
        EMIT("dv", lambda e: e.memset(col(S_EPS), 1e-5), writes=["SEPS"])
        EMIT("dv", lambda e: e.memset(col(S_HPI), float(np.pi / 2)), writes=["SHPI"])
        EMIT("dv", lambda e: e.memset(ab(C_ONE, 128, 0, 1), 1.0), writes=["CONE"])

        def xt_build(t):
            EMIT("ac", lambda e, t=t: e.activation(ab(XBF, 1024), Xt(t), AF.Copy),
                 reads=[f"X{t}"], writes=["XBF"])
            for f in range(ND):
                EMIT("pe", lambda e, f=f: e.transpose(
                    PT[0][:, f * 128:(f + 1) * 128], ab(XBF + f * 128, 128),
                    ab(C_ID, 128)),
                    reads=["XBF", "CID"], writes=["pt0"])
            for f in range(ND):
                EMIT("dv", lambda e, f=f, t=t: e.tensor_copy(
                    XTk(f, t * 128, 128), PT[0][:, f * 128:(f + 1) * 128]),
                    reads=["pt0"], writes=[f"XT{t}"])

        def layernorm(t, s, rebuild=True):
            sq = af(FS + 1024, 1024)
            EMIT("ac", lambda e, t=t, s=s, sq=sq: e.activation(
                sq, Xt(t), AF.Square, accum_out=col(S_SS + s)),
                reads=[f"X{t}"], writes=["LND", f"SSS{s}"])
            EMIT("dv", lambda e, t=t, s=s: e.reduce_sum(
                col(S_SUM + s), Xt(t), axis=AX, opt_output=False),
                reads=[f"X{t}"], writes=[f"SSUM{s}"], inc=0)
            EMIT("dv", lambda e: e.drain(), inc=0)
            EMIT("dv", lambda e, s=s: e.tensor_scalar(
                col(S_NMEAN + s), col(S_SUM + s), -1.0 / 1024.0, None, OP.mult),
                writes=[f"SNM{s}"])
            EMIT("ac", lambda e, s=s: e.activation(
                col(S_M2 + s), col(S_NMEAN + s), AF.Square),
                reads=[f"SNM{s}"], writes=[f"SM2{s}"])
            EMIT("dv", lambda e, s=s: e.scalar_tensor_tensor(
                col(S_VAR + s), col(S_SS + s), 1.0 / 1024.0, col(S_M2 + s),
                OP.mult, OP.subtract),
                reads=[f"SSS{s}", f"SM2{s}"], writes=[f"SVAR{s}"])
            EMIT("ac", lambda e, s=s: e.activation(
                col(S_STD + s), col(S_VAR + s), AF.Sqrt, bias=col(S_EPS)),
                reads=[f"SVAR{s}", "SEPS"], writes=[f"SSTD{s}"])
            EMIT("dv", lambda e, s=s: e.reciprocal(col(S_RSTD + s), col(S_STD + s)),
                 reads=[f"SSTD{s}"], writes=[f"SRSTD{s}"], inc=0)
            EMIT("dv", lambda e: e.drain(), inc=0)
            EMIT("dv", lambda e, t=t, s=s: e.tensor_scalar(
                Xt(t), Xt(t), col(S_NMEAN + s), col(S_RSTD + s), OP.add, OP.mult),
                reads=[f"SNM{s}", f"SRSTD{s}"], writes=[f"X{t}"])
            if rebuild:
                xt_build(t)

        for t in range(NT):
            xt_build(t)

        _dummy = [None]
        VTOK = RB
        QTA, QTB, KTA, KTB = RB + 16384, RB + 18432, RB + 20480, RB + 22528
        WVS = RB + 24576
        WQS = RB + 28672
        WKS = RB + 30720
        PTP = [RB + 24576, RB + 26624]
        PTOK = RB + 28672
        OSB = RB + 30720

        for dvh in range(2 if KPH >= 1 else 0):
            for k in range(ND):
                EMIT("gp", lambda e, k=k, dvh=dvh: e.dma_start(
                    ab(WVS + k * 512, 512),
                    wv_d[k * 128:(k + 1) * 128, dvh * 512:(dvh + 1) * 512]),
                    writes=(["WVS"] if k == 0 else []), inc=16)
            MARK("gp", "WVS")
            for nt in range(NT):
                pb = f"p{nt % 2}"
                for k in range(ND):
                    EMIT("pe", lambda e, k=k, nt=nt: e.matmul(
                        P[nt % 2][:, :], XTk(k, nt * 128, 128), ab(WVS + k * 512, 512),
                        start=(k == 0), stop=False),
                        reads=["WVS"] + xtr(nt * 128, 128), writes=[pb])
                EMIT("pe", lambda e, nt=nt, dvh=dvh: e.matmul(
                    P[nt % 2][:, :], ab(C_ONE, 128, 0, 1),
                    ab(C_BV + dvh * 512, 512, 0, 1), start=False, stop=True),
                    reads=["CONE", "CBV"], writes=[pb])
                EMIT("ac", lambda e, nt=nt, dvh=dvh: e.activation(
                    ab(VTOK + nt * 1024 + dvh * 512, 512), P[nt % 2][:, :], AF.Copy),
                    reads=[pb], writes=[f"VT{nt}_{dvh}"])

        for k in range(ND if KPH >= 1 else 0):
            EMIT("gp", lambda e, k=k: e.dma_start(
                ab(WQS + k * 256, 256), wq_d[k * 128:(k + 1) * 128, :]),
                writes=(["WQS"] if k == 0 else []), inc=16)
        MARK("gp", "WQS")
        for k in range(ND if KPH >= 1 else 0):
            EMIT("gp", lambda e, k=k: e.dma_start(
                ab(WKS + k * 256, 256), wk_d[k * 128:(k + 1) * 128, :]),
                writes=(["WKS"] if k == 0 else []), inc=16)
        MARK("gp", "WKS")
        for (dst, wsrc, wnm, g, bq, nm) in (() if KPH < 1 else (
                (QTA, WQS, "WQS", 0, S_BQA, "QTA"), (QTB, WQS, "WQS", 1, S_BQB, "QTB"),
                (KTA, WKS, "WKS", 0, S_BKA, "KTA"), (KTB, WKS, "WKS", 1, S_BKB, "KTB"))):
            for mc in range(4):
                pb = f"p{mc % 2}"
                for k in range(ND):
                    EMIT("pe", lambda e, k=k, mc=mc, wsrc=wsrc, g=g: e.matmul(
                        P[mc % 2][:, :], ab(wsrc + k * 256 + g * 128, 128),
                        XTk(k, mc * 512, 512), start=(k == 0), stop=(k == ND - 1)),
                        reads=[wnm] + xtr(mc * 512, 512), writes=[pb])
                EMIT("dv", lambda e, mc=mc, dst=dst, bq=bq: e.tensor_scalar(
                    ab(dst + mc * 512, 512), P[mc % 2][:, :], col(bq), None, OP.add),
                    reads=[pb, "SBQA", "SBQB", "SBKA", "SBKB"], writes=[nm])

        barrier()

        for h in range(8 if KPH >= 2 else 0):
            g = h // 4
            base = (h % 4) * 32
            qsrc = QTA if g == 0 else QTB
            ksrc = KTA if g == 0 else KTB
            qnm = "QTA" if g == 0 else "QTB"
            knm = "KTA" if g == 0 else "KTB"
            for mt in range(NT):
                s = mt % 2
                for c in range(4):
                    pb = f"p{c % 2}"
                    tp = (96, 0) if base == 96 else None
                    EMIT("pe", lambda e, c=c, mt=mt, base=base, qsrc=qsrc, ksrc=ksrc,
                         tp=tp: e.matmul(
                             P[c % 2][:, :],
                             Ab[base:base + 32, qsrc + mt * 128:qsrc + mt * 128 + 128],
                             Ab[base:base + 32, ksrc + c * 512:ksrc + c * 512 + 512],
                             start=True, stop=True, tile_position=tp),
                         reads=[qnm, knm], writes=[pb])
                    EMIT("ac", lambda e, c=c: e.activation(
                        af(FS + c * 512, 512), P[c % 2][:, :], AF.Prelu,
                        scale=0.25, alpha=0.2),
                        reads=[pb], writes=["SLK"])
                EMIT("dv", lambda e, s=s: e.reduce_max(
                    col(S_NEGMAX + s), af(FS, 2048), axis=AX, negate=True,
                    opt_output=False),
                    reads=["SLK"], writes=[f"SNX{s}"], inc=0)
                EMIT("dv", lambda e: e.drain(), inc=0)
                EMIT("dv", lambda e: e.memset(col(S_JUNK), 0.0), writes=[f"SNX{s}"])
                EMIT("ac", lambda e, s=s: e.activation(
                    ab(PTOK, 2048), af(FS, 2048), AF.Exp, bias=col(S_NEGMAX + s),
                    scale=1.0, accum_out=col(S_R + s)),
                    reads=["SLK", f"SNX{s}"], writes=["PTOK", f"SR{s}"])
                EMIT("dv", lambda e, s=s: e.reciprocal(col(S_RREC + s), col(S_R + s)),
                     reads=[f"SR{s}"], writes=[f"SRC{s}"], inc=0)
                EMIT("dv", lambda e: e.drain(), inc=0)
                EMIT("dv", lambda e, s=s: e.tensor_tensor(
                    col(S_GREC + s), col(S_RREC + s), col(S_GB), OP.mult),
                    reads=["SGB", f"SRC{s}"], writes=[f"SGR{s}"])
                EMIT("dv", lambda e: e.drain(), inc=0)
                for bk in range(2):
                    pt = f"pt{bk}"
                    for i in range(8):
                        EMIT("pe", lambda e, bk=bk, i=i: e.transpose(
                            PT[bk][:, i * 128:(i + 1) * 128],
                            ab(PTOK + (bk * 8 + i) * 128, 128), ab(C_ID, 128)),
                            reads=["PTOK", "CID"], writes=[pt])
                    EMIT("dv", lambda e, bk=bk, s=s: e.tensor_copy(
                        ab(PTP[s] + bk * 1024, 1024), PT[bk][:, :]),
                        reads=[pt], writes=[f"PTP{s}"])
                for nt in range(NT):
                    EMIT("pe", lambda e, nt=nt, h=h, s=s: e.matmul(
                        P[2][:, s * 128:s * 128 + 128],
                        ab(VTOK + nt * 1024 + h * 128, 128),
                        ab(PTP[s] + nt * 128, 128),
                        start=(nt == 0), stop=(nt == NT - 1)),
                        reads=[f"PTP{s}", f"VT{nt}_0", f"VT{nt}_1"], writes=["p2"])
                EMIT("ac", lambda e, s=s: e.activation(
                    ab(OSB + s * 128, 128), P[2][:, s * 128:s * 128 + 128], AF.Copy),
                    reads=["p2"], writes=[f"OSB{s}"])
                EMIT("pe", lambda e, s=s: e.transpose(
                    PT[0][:, 0:128], ab(OSB + s * 128, 128), ab(C_ID, 128)),
                    reads=[f"OSB{s}", "CID"], writes=["pt0"])
                EMIT("dv", lambda e, mt=mt, h=h, s=s: e.scalar_tensor_tensor(
                    Xt(mt, h * 128, 128), PT[0][:, 0:128], col(S_GREC + s),
                    Xt(mt, h * 128, 128), OP.mult, OP.add),
                    reads=["pt0", f"SGR{s}", f"X{mt}"], writes=[f"X{mt}"])

        barrier()
        for t in range(NT if KPH >= 3 else 0):
            layernorm(t, t % 2)

        barrier()
        HT = RB
        WFS = [RB + 16384, RB + 20480]
        WPS = [RB + 24576, RB + 25088]
        TMB = [RB + 25600, RB + 26112]

        for mcq in range(4 if KPH >= 4 else 0):
            for fch in range(32):
                buf = (fch // 4) % 2
                if fch % 4 == 0:
                    for k in range(ND):
                        EMIT("gp", lambda e, k=k, fch=fch, buf=buf: e.dma_start(
                            ab(WFS[buf] + k * 512, 512),
                            wfc_d[k * 128:(k + 1) * 128, fch * 128:fch * 128 + 512]),
                            writes=([f"WFS{buf}"] if k == 0 else []), inc=16)
                    MARK("gp", f"WFS{buf}")
                pb = f"p{fch % 2}"
                for k in range(ND):
                    EMIT("pe", lambda e, k=k, fch=fch, mcq=mcq, buf=buf: e.matmul(
                        P[fch % 2][:, :],
                        ab(WFS[buf] + k * 512 + (fch % 4) * 128, 128),
                        XTk(k, mcq * 512, 512), start=(k == 0), stop=(k == ND - 1)),
                        reads=[f"WFS{buf}"] + xtr(mcq * 512, 512), writes=[pb])
                tb = fch % 2
                EMIT("dv", lambda e, fch=fch, tb=tb: e.tensor_scalar(
                    ab(TMB[tb], 512), P[fch % 2][:, :], 0.0, None, OP.max),
                    reads=[pb], writes=[f"TMB{tb}"])
                EMIT("ac", lambda e, fch=fch, tb=tb: e.activation(
                    ab(HT + fch * 512, 512), ab(TMB[tb], 512), AF.Square),
                    reads=[f"TMB{tb}"], writes=[f"HT{fch}"])
            for dc in range(2):
                for ft in range(32):
                    buf = ft % 2
                    EMIT("gp", lambda e, ft=ft, dc=dc, buf=buf: e.dma_start(
                        ab(WPS[buf], 512),
                        wpj_d[ft * 128:(ft + 1) * 128, dc * 512:(dc + 1) * 512]),
                        writes=[f"WPS{buf}"], inc=16)
                    for mt4 in range(4):
                        EMIT("pe", lambda e, ft=ft, mt4=mt4, buf=buf: e.matmul(
                            P[2 + mt4][:, :], ab(HT + ft * 512 + mt4 * 128, 128),
                            ab(WPS[buf], 512), start=(ft == 0), stop=(ft == 31)),
                            reads=[f"WPS{buf}", f"HT{ft}"], writes=[f"p{2 + mt4}"])
                for mt4 in range(4):
                    mt = mcq * 4 + mt4
                    pb = f"p{2 + mt4}"
                    a_, b_ = af(FS, 512), af(FS + 512, 512)
                    EMIT("dv", lambda e, mt4=mt4: e.tensor_scalar(
                        af(FS, 512), P[2 + mt4][:, :], TH[0], None, OP.is_gt),
                        reads=[pb], writes=["ACCA"])
                    for i, (src, dst, rs_, ws) in enumerate(
                            ((a_, b_, "ACCA", "ACCB"), (b_, a_, "ACCB", "ACCA"),
                             (a_, b_, "ACCA", "ACCB"), (b_, a_, "ACCB", "ACCA"))):
                        EMIT("dv", lambda e, mt4=mt4, i=i, src=src, dst=dst:
                             e.scalar_tensor_tensor(
                                 dst, P[2 + mt4][:, :], TH[i + 1], src,
                                 OP.is_gt, OP.add),
                             reads=[pb, rs_], writes=[ws])
                    EMIT("dv", lambda e, mt=mt, dc=dc: e.scalar_tensor_tensor(
                        Xt(mt, dc * 512, 512), af(FS, 512), 0.2,
                        Xt(mt, dc * 512, 512), OP.mult, OP.add),
                        reads=["ACCA", f"X{mt}"], writes=[f"X{mt}"])
            for mt4 in range(4):
                mt = mcq * 4 + mt4
                layernorm(mt, mt % 2)

        barrier()
        LATQ = [RB, RB + 2048]
        LATK = [RB + 4096, RB + 6144]
        LATV = [RB + 8192, RB + 10240]
        QTJ = [RB + 12288, RB + 14336]
        KTJ = [RB + 16384, RB + 18432]
        WDQ = [RB + 20480, RB + 21504]
        WDK = [RB + 22528, RB + 23552]
        WDV = [RB + 24576, RB + 25600]
        RPC = RB + 26624
        RPS = RB + 28672
        TP1 = RB + 30720
        TP2 = RB + 31232

        EMIT("gp", lambda e: e.dma_start(ab(RPC, 2048), rc_d), writes=["RPC"], inc=16)
        EMIT("gp", lambda e: e.dma_start(ab(RPS, 2048), rs_d), writes=["RPS"], inc=16)

        for j in range(8 if KPH >= 5 else 0):
            par = j % 2
            for (wd, dram, nm) in ((WDQ, wqd_d, "WDQ"), (WDK, wkd_d, "WDK"),
                                   (WDV, wvd_d, "WDV")):
                for k in range(ND):
                    EMIT("gp", lambda e, k=k, j=j, wd=wd, dram=dram, par=par:
                         e.dma_start(ab(wd[par] + k * 128, 128),
                                     dram[k * 128:(k + 1) * 128, j * 128:(j + 1) * 128]),
                         writes=([f"{nm}{par}"] if k == 0 else []), inc=16)
                MARK("gp", f"{nm}{par}")
            for (wd, lat, nm, lnm) in ((WDQ, LATQ, "WDQ", "LATQ"),
                                       (WDK, LATK, "WDK", "LATK"),
                                       (WDV, LATV, "WDV", "LATV")):
                for mc in range(4):
                    pb = f"p{mc % 2}"
                    for k in range(ND):
                        EMIT("pe", lambda e, k=k, mc=mc, wd=wd, par=par: e.matmul(
                            P[mc % 2][:, :], ab(wd[par] + k * 128, 128),
                            XTk(k, mc * 512, 512), start=(k == 0), stop=(k == ND - 1)),
                            reads=[f"{nm}{par}"] + xtr(mc * 512, 512), writes=[pb])
                    EMIT("ac", lambda e, mc=mc, lat=lat, par=par: e.activation(
                        ab(lat[par] + mc * 512, 512), P[mc % 2][:, :], AF.Copy),
                        reads=[pb], writes=[f"{lnm}{par}"])
            EMIT("sy", lambda e, j=j, par=par: e.dma_start(
                lv_sp[j * 128:(j + 1) * 128, :], ab(LATV[par], 2048)),
                reads=[f"LATV{par}"], inc=16)
            for (lat, lnm, upo, unm, upso, usnm, qt, qnm, spill) in (
                    (LATQ, "LATQ", C_UPQ, "CUPQ", C_UPQS, "CUPQS", QTJ, "QTJ", qr_sp),
                    (LATK, "LATK", C_UPK, "CUPK", C_UPKS, "CUPKS", KTJ, "KTJ", kr_sp)):
                for mc in range(4):
                    EMIT("pe", lambda e, mc=mc, lat=lat, upo=upo, par=par: e.matmul(
                        P[2][:, :], ab(upo, 128), ab(lat[par] + mc * 512, 512),
                        start=True, stop=True),
                        reads=[f"{lnm}{par}", unm], writes=["p2"])
                    EMIT("pe", lambda e, mc=mc, lat=lat, upso=upso, par=par: e.matmul(
                        P[3][:, :], ab(upso, 128), ab(lat[par] + mc * 512, 512),
                        start=True, stop=True),
                        reads=[f"{lnm}{par}", usnm], writes=["p3"])
                    EMIT("dv", lambda e, mc=mc: e.tensor_tensor(
                        ab(TP1, 512), P[2][:, :], ab(RPC + mc * 512, 512), OP.mult),
                        reads=["p2", "RPC"], writes=["TP1"])
                    EMIT("dv", lambda e, mc=mc: e.tensor_tensor(
                        ab(TP2, 512), P[3][:, :], ab(RPS + mc * 512, 512), OP.mult),
                        reads=["p3", "RPS"], writes=["TP2"])
                    EMIT("dv", lambda e, mc=mc, qt=qt, par=par: e.tensor_tensor(
                        ab(qt[par] + mc * 512, 512), ab(TP1, 512), ab(TP2, 512),
                        OP.add),
                        reads=["TP1", "TP2"], writes=[f"{qnm}{par}"])
                EMIT("sy", lambda e, j=j, qt=qt, par=par, spill=spill: e.dma_start(
                    spill[j * 128:(j + 1) * 128, :], ab(qt[par], 2048)),
                    reads=[f"{qnm}{par}"], inc=16)

        barrier()
        KTH = [XTO, XTO + 2048]
        QTH = [XTO + 4096, XTO + 6144]
        LVH = [XTO + 8192, XTO + 10240]
        PTL = XTO + 12288
        YT = RB
        VH = [RB + 16384, RB + 17424]
        PTMP = [RB + 18464, RB + 18720]
        YSTG = RB + 18976
        WC = RB + 19232
        RB1 = RB + 27424
        RB2 = RB + 27680

        for k in range(ND if KPH >= 6 else 0):
            EMIT("gp", lambda e, k=k: e.dma_start(
                ab(WC + k * 1024, 1024), wc_d[k * 128:(k + 1) * 128, :]),
                writes=(["WC"] if k == 0 else []), inc=16)
        MARK("gp", "WC")
        for par in range(2 if KPH >= 6 else 0):
            for nt in range(NT):
                EMIT("dv", lambda e, par=par, nt=nt: e.memset(
                    ab(VH[par] + nt * 65 + 64, 1), 1.0), writes=[f"VHO{par}"])

        for h in range(16 if KPH >= 6 else 0):
            par = h % 2
            EMIT("sy", lambda e, h=h, par=par: e.dma_start(
                ab(LVH[par], 2048, 0, 64), lv_sp[h * 64:(h + 1) * 64, :]),
                writes=[f"LVH{par}"], inc=16)
            EMIT("sy", lambda e, h=h, par=par: e.dma_start(
                ab(KTH[par], 2048, 0, 64), kr_sp[h * 64:(h + 1) * 64, :]),
                writes=[f"KTH{par}"], inc=16)
            EMIT("sy", lambda e, h=h, par=par: e.dma_start(
                ab(QTH[par], 2048, 0, 64), qr_sp[h * 64:(h + 1) * 64, :]),
                writes=[f"QTH{par}"], inc=16)
            for nt in range(NT):
                pb = f"p{nt % 2}"
                EMIT("pe", lambda e, nt=nt, par=par: e.matmul(
                    P[nt % 2][:, 0:64], ab(LVH[par] + nt * 128, 128, 0, 64),
                    ab(C_UPV1, 64, 0, 64), start=True, stop=True),
                    reads=[f"LVH{par}", "CUPV"], writes=[pb])
                EMIT("dv", lambda e, nt=nt, par=par: e.tensor_copy(
                    ab(VH[par] + nt * 65, 64), P[nt % 2][:, 0:64]),
                    reads=[pb, f"VHO{par}"], writes=[f"VH{par}"])
            for mcq in range(8):
                maxnt = 2 * mcq + 2
                for nt in range(maxnt):
                    pb = f"p{nt % 2}"
                    EMIT("pe", lambda e, nt=nt, mcq=mcq, par=par: e.matmul(
                        P[nt % 2][:, 0:256], ab(KTH[par] + nt * 128, 128, 0, 64),
                        ab(QTH[par] + mcq * 256, 256, 0, 64),
                        start=True, stop=True),
                        reads=[f"KTH{par}", f"QTH{par}"], writes=[pb])
                    if nt < maxnt - 2:
                        EMIT("ac", lambda e, nt=nt: e.activation(
                            ab(PTL + nt * 256, 256), P[nt % 2][:, 0:256], AF.Exp,
                            scale=0.125),
                            reads=[pb], writes=[f"PTL{nt}"])
                    else:
                        v = nt - 2 * mcq
                        pp = nt % 2
                        EMIT("ac", lambda e, pp=pp, nt=nt: e.activation(
                            ab(PTMP[pp], 256), P[nt % 2][:, 0:256], AF.Exp,
                            scale=0.125),
                            reads=[pb], writes=[f"PTMP{pp}"])
                        EMIT("dv", lambda e, nt=nt, v=v, pp=pp: e.tensor_tensor(
                            ab(PTL + nt * 256, 256), ab(PTMP[pp], 256),
                            ab(C_M0 if v == 0 else C_M1, 256), OP.mult),
                            reads=[f"PTMP{pp}", "CM0", "CM1"], writes=[f"PTL{nt}"])
                for nt in range(maxnt):
                    EMIT("pe", lambda e, nt=nt, par=par, maxnt=maxnt: e.matmul(
                        P[2][0:65, 0:256], ab(VH[par] + nt * 65, 65),
                        ab(PTL + nt * 256, 256),
                        start=(nt == 0), stop=(nt == maxnt - 1)),
                        reads=[f"VH{par}", f"PTL{nt}"], writes=["p2"])
                EMIT("dv", lambda e: e.tensor_copy(af(FS + 768, 256, 0, 1),
                                                   P[2][64:65, 0:256]),
                     reads=["p2"], writes=["RRAW"])
                EMIT("dv", lambda e: e.reciprocal(
                    af(FS + 256, 256, 0, 1), af(FS + 768, 256, 0, 1)),
                    reads=["RRAW"], writes=["RROW"], inc=0)
                EMIT("dv", lambda e: e.drain(), inc=0)
                EMIT("dv", lambda e: e.tensor_copy(ab(RB1, 256, 0, 1),
                                                   af(FS + 256, 256, 0, 1)),
                     reads=["RROW"], writes=["RB1"])
                EMIT("dv", lambda e: e.tensor_tensor(
                    af(FS + 512, 256, 0, 1), af(FS + 256, 256, 0, 1),
                    ab(RB1, 256, 0, 1), OP.subtract),
                    reads=["RROW", "RB1"], writes=["RRES"])
                EMIT("dv", lambda e: e.tensor_copy(ab(RB2, 256, 0, 1),
                                                   af(FS + 512, 256, 0, 1)),
                     reads=["RRES"], writes=["RB2"])
                EMIT("pe", lambda e: e.matmul(
                    P[3][0:64, 0:256], ab(C_ONE, 64, 0, 1), ab(RB1, 256, 0, 1),
                    start=True, stop=False),
                    reads=["CONE", "RB1"], writes=["p3"])
                EMIT("pe", lambda e: e.matmul(
                    P[3][0:64, 0:256], ab(C_ONE, 64, 0, 1), ab(RB2, 256, 0, 1),
                    start=False, stop=True),
                    reads=["CONE", "RB2"], writes=["p3"])
                EMIT("dv", lambda e: e.tensor_copy(af(FS, 256, 0, 64),
                                                   P[3][0:64, 0:256]),
                     reads=["p3"], writes=["BCF"])
                pt_ = h // 2
                if h % 2 == 0:
                    EMIT("dv", lambda e, pt_=pt_, mcq=mcq: e.tensor_tensor(
                        ab(YT + pt_ * 2048 + mcq * 256, 256, 0, 64),
                        P[2][0:64, 0:256], af(FS, 256, 0, 64), OP.mult),
                        reads=["p2", "BCF"], writes=[f"YT{pt_}_{mcq}"])
                else:
                    EMIT("dv", lambda e: e.tensor_tensor(
                        ab(YSTG, 256, 0, 64), P[2][0:64, 0:256], af(FS, 256, 0, 64),
                        OP.mult),
                        reads=["p2", "BCF"], writes=["YSTG"])
                    EMIT("sy", lambda e, pt_=pt_, mcq=mcq: e.dma_start(
                        Ab[64:128, YT + pt_ * 2048 + mcq * 256:
                           YT + pt_ * 2048 + mcq * 256 + 256],
                        ab(YSTG, 256, 0, 64)),
                        reads=["YSTG"], writes=[f"YT{pt_}_{mcq}"], inc=16)

        barrier()
        for t in range(NT if KPH >= 7 else 0):
            readies = [f"YT{p_}_{t // 2}" for p_ in range(ND)]
            for dc in range(2):
                pb = f"p{4 + dc}"
                for k in range(ND):
                    EMIT("pe", lambda e, k=k, t=t, dc=dc: e.matmul(
                        P[4 + dc][:, :], ab(YT + k * 2048 + t * 128, 128),
                        ab(WC + k * 1024 + dc * 512, 512),
                        start=(k == 0), stop=(k == ND - 1)),
                        reads=readies + ["WC"], writes=[pb])
                EMIT("dv", lambda e, t=t, dc=dc: e.tensor_tensor(
                    Xt(t, dc * 512, 512), P[4 + dc][:, :], Xt(t, dc * 512, 512),
                    OP.add),
                    reads=[pb, f"X{t}"], writes=[f"X{t}"])
            layernorm(t, t % 2)

        barrier()
        CST = RB
        SNT = RB + 8192
        for (src1, src2, dst, dnm, useb) in (() if KPH < 8 else ((th_d, tn_d, CST, "CST", True),
                                             (ph_d, pn_d, SNT, "SNT", False))):
            for k in range(ND):
                for cc in range(2):
                    EMIT("sy", lambda e, k=k, cc=cc, src1=src1: e.dma_start(
                        af(FS, 512),
                        src1[k * 128:(k + 1) * 128, cc * 512:(cc + 1) * 512]),
                        writes=["TS1"], inc=16)
                    EMIT("sy", lambda e, k=k, cc=cc, src2=src2: e.dma_start(
                        af(FS + 512, 512),
                        src2[k * 128:(k + 1) * 128, cc * 512:(cc + 1) * 512]),
                        writes=["TS2"], inc=16)
                    EMIT("dv", lambda e: e.tensor_tensor(
                        af(FS, 512), af(FS, 512), af(FS + 512, 512), OP.add),
                        reads=["TS1", "TS2"], writes=["TS1"])
                    if useb:
                        EMIT("ac", lambda e, k=k, cc=cc, dst=dst: e.activation(
                            ab(dst + k * 1024 + cc * 512, 512), af(FS, 512),
                            AF.Sin, bias=col(S_HPI)),
                            reads=["TS1", "SHPI"], writes=[dnm])
                    else:
                        EMIT("ac", lambda e, k=k, cc=cc, dst=dst: e.activation(
                            ab(dst + k * 1024 + cc * 512, 512), af(FS, 512),
                            AF.Sin),
                            reads=["TS1"], writes=[dnm])
        for t in range(NT):
            for dc in range(2 if KPH >= 8 else 0):
                for k in range(ND):
                    EMIT("pe", lambda e, k=k, t=t, dc=dc: e.matmul(
                        P[0][:, :], XTk(k, t * 128, 128),
                        ab(CST + k * 1024 + dc * 512, 512),
                        start=(k == 0), stop=(k == ND - 1)),
                        reads=["CST"] + xtr(t * 128, 128), writes=["p0"])
                for k in range(ND):
                    EMIT("pe", lambda e, k=k, t=t, dc=dc: e.matmul(
                        P[1][:, :], XTk(k, t * 128, 128),
                        ab(SNT + k * 1024 + dc * 512, 512),
                        start=(k == 0), stop=(k == ND - 1)),
                        reads=["SNT"] + xtr(t * 128, 128), writes=["p1"])
                EMIT("ac", lambda e: e.activation(af(FS, 512), P[0][:, :], AF.Square),
                     reads=["p0"], writes=["RE2"])
                EMIT("ac", lambda e: e.activation(af(FS + 512, 512), P[1][:, :],
                                                  AF.Square),
                     reads=["p1"], writes=["IM2"])
                EMIT("dv", lambda e: e.tensor_tensor(
                    af(FS, 512), af(FS, 512), af(FS + 512, 512), OP.add),
                    reads=["RE2", "IM2"], writes=["RE2"])
                EMIT("ac", lambda e: e.activation(af(FS + 512, 512), af(FS, 512),
                                                  AF.Sqrt),
                     reads=["RE2"], writes=["IM2"])
                EMIT("dv", lambda e, t=t, dc=dc: e.tensor_tensor(
                    Xt(t, dc * 512, 512), af(FS + 512, 512), Xt(t, dc * 512, 512),
                    OP.add),
                    reads=["IM2", f"X{t}"], writes=[f"X{t}"])
            if KPH >= 8:
                layernorm(t, t % 2, rebuild=False)
            EMIT("sy", lambda e, t=t: e.dma_start(
                out_d[t * 128:(t + 1) * 128, :], Xt(t)),
                reads=[f"X{t}"], inc=16, sem="st")

        ops["sy"].append(lambda e: e.wait_ge(sems["st"], 16 * NT))

        @block.tensor
        def _(t_):
            for fn in ops["pe"]:
                fn(t_)

        @block.scalar
        def _(s_):
            for fn in ops["ac"]:
                fn(s_)

        @block.vector
        def _(v_):
            for fn in ops["dv"]:
                fn(v_)

        @block.gpsimd
        def _(g_):
            for fn in ops["gp"]:
                fn(g_)

        @block.sync
        def _(s_):
            for fn in ops["sy"]:
                fn(s_)

    return nc


def _host_prep(inp):
    f32 = np.float32
    d = {}
    Wq = np.asarray(inp["L0_attn_Wq"], f32)
    Wk = np.asarray(inp["L0_attn_Wk"], f32)
    wq_t = np.zeros((1024, 256), f32)
    wk_t = np.zeros((1024, 256), f32)
    bq_pad = np.zeros((256, 1), f32)
    bk_pad = np.zeros((256, 1), f32)
    bq = np.asarray(inp["L0_attn_bq"], f32)
    bk = np.asarray(inp["L0_attn_bk"], f32)
    for h in range(8):
        g, hh = h // 4, h % 4
        c0 = g * 128 + hh * 32
        wq_t[:, c0:c0 + 16] = Wq[h * 16:(h + 1) * 16, :].T
        wk_t[:, c0:c0 + 16] = Wk[h * 16:(h + 1) * 16, :].T
        bq_pad[c0:c0 + 16, 0] = bq[h * 16:(h + 1) * 16]
        bk_pad[c0:c0 + 16, 0] = bk[h * 16:(h + 1) * 16]
    d["wq_t"], d["wk_t"] = wq_t, wk_t
    d["bq_pad"], d["bk_pad"] = bq_pad, bk_pad
    d["wv_t"] = np.ascontiguousarray(np.asarray(inp["L0_attn_Wv"], f32).T)
    d["bv_r"] = np.asarray(inp["L0_attn_bv"], f32).reshape(1, 1024)
    d["gam"] = np.broadcast_to(
        np.asarray(inp["L0_attn_gamma"], f32).reshape(1, 1), (128, 1)).copy()
    d["wfc_t"] = np.ascontiguousarray(np.asarray(inp["L0_ffn_W_fc"], f32).T)
    d["wpj_t"] = np.ascontiguousarray(np.asarray(inp["L0_ffn_W_proj"], f32).T)
    d["wqd"] = np.asarray(inp["L1_attn_Wq_down"], f32)
    d["wkd"] = np.asarray(inp["L1_attn_Wk_down"], f32)
    d["wvd"] = np.asarray(inp["L1_attn_Wv_down"], f32)

    def upblock(c, e):
        U = np.concatenate([c, e], axis=1)
        out = np.zeros((128, 128), f32)
        out[0:64, 0:64] = U
        out[64:128, 64:128] = U
        return out

    def upswap(e):
        perm = np.arange(32)
        perm = perm + 1 - 2 * (perm % 2)
        Us = np.zeros((64, 64), f32)
        Us[:, 32:64] = e[:, perm]
        out = np.zeros((128, 128), f32)
        out[0:64, 0:64] = Us
        out[64:128, 64:128] = Us
        return out

    qc = np.asarray(inp["L1_attn_Wq_up_c"], f32)
    qe = np.asarray(inp["L1_attn_Wq_up_e"], f32)
    kc = np.asarray(inp["L1_attn_Wk_up_c"], f32)
    ke = np.asarray(inp["L1_attn_Wk_up_e"], f32)
    d["upq"] = upblock(qc, qe)
    d["upqs"] = upswap(qe)
    d["upk"] = upblock(kc, ke)
    d["upks"] = upswap(ke)
    d["wvu"] = np.asarray(inp["L1_attn_Wv_up"], f32)
    d["wc"] = np.asarray(inp["L1_attn_Wc"], f32)
    d["th"] = np.asarray(inp["L1_ffn_theta"], f32)
    d["tn"] = np.asarray(inp["L1_ffn_theta_noise"], f32)
    d["ph"] = np.asarray(inp["L1_ffn_phi"], f32)
    d["pn"] = np.asarray(inp["L1_ffn_phi_noise"], f32)
    d["identM"] = np.eye(128, dtype=f32)
    tpos = np.arange(T, dtype=np.float64)
    invf = 10000.0 ** (-np.arange(16, dtype=np.float64) / 16.0)
    ang = tpos[None, :] * invf[:, None]
    cosv = np.cos(ang)
    sinv = np.sin(ang)
    rc = np.zeros((128, T), np.float64)
    rs = np.zeros((128, T), np.float64)
    for r in range(128):
        j = r % 64
        if j < 32:
            rc[r] = 1.0
        else:
            i = j - 32
            rc[r] = cosv[i % 16]
            rs[r] = (-1.0 if i % 2 == 0 else 1.0) * sinv[i % 16]
    d["ropec"] = rc.astype(f32)
    d["ropes"] = rs.astype(f32)
    m = np.zeros((2, 128, 256), f32)
    for v in range(2):
        for nl in range(128):
            m[v, nl, :] = (128 * v + nl <= np.arange(256)).astype(f32)
    d["mask0"], d["mask1"] = m[0], m[1]
    return d


def kernel(**inputs):
    shared = _host_prep(inputs)
    x = np.ascontiguousarray(np.asarray(inputs["L0_attn_x"], np.float32))
    nc = build_nc()
    in_maps = [dict(shared, x=np.ascontiguousarray(x[i])) for i in range(8)]
    res = run_bass_kernel_spmd(nc, in_maps, core_ids=list(range(8)))
    out = np.stack([np.asarray(res.results[i]["out"], np.float32)
                    for i in range(8)])
    return out


# version 04
# speedup vs baseline: 1.0750x; 1.0750x over previous
import math
import os
import numpy as np
import concourse.bass as bass
import concourse.mybir as mybir
from concourse.bass_utils import run_bass_kernel_spmd
from concourse.alu_op_type import AluOpType
from contextlib import ExitStack

F32 = mybir.dt.float32
BF16 = mybir.dt.bfloat16
AF = mybir.ActivationFunctionType
AX = mybir.AxisListType.X
OP = AluOpType

T, D, NT, ND = 2048, 1024, 16, 8
KPH = int(os.environ.get("KPH", "99"))

TH = [0.24419428096993973, 0.4639691338428854, 0.6861859295255306,
      0.9059607823984763, 1.0839784132255623]

AF_W = 19456
XO = 0
FS = 16384
ST = 18432

AB_W = 58368
XTO = 0
RB = 16384
CB = 54784
C_ID = CB
C_ONE = CB + 128
C_BV = CB + 256
C_M0 = CB + 1280
C_M1 = CB + 1536
C_UPQ = CB + 1792
C_UPQS = CB + 1920
C_UPK = CB + 2048
C_UPKS = CB + 2176
C_UPV1 = CB + 2304
XBF = 57344

S_NEGMAX = ST + 0
S_R = ST + 2
S_RREC = ST + 4
S_GREC = ST + 6
S_SUM = ST + 8
S_SS = ST + 10
S_NMEAN = ST + 12
S_M2 = ST + 14
S_VAR = ST + 16
S_STD = ST + 18
S_RSTD = ST + 20
S_EPS = ST + 22
S_JUNK = ST + 23
S_GB = ST + 24
S_HPI = ST + 25
S_BQA = ST + 26
S_BQB = ST + 27
S_BKA = ST + 28
S_BKB = ST + 29
S_IDF = ST + 512
S_RCL = ST + 640


def build_nc():
    nc = bass.Bass("TRN2", target_bir_lowering=False)
    dr = {}

    def din(name, shape, dt=F32):
        dr[name] = nc.dram_tensor(name, shape, dt, kind="ExternalInput").ap()
        return dr[name]

    x_d = din("x", [T, D])
    wq_d = din("wq_t", [D, 256])
    bq_d = din("bq_pad", [256, 1])
    wk_d = din("wk_t", [D, 256])
    bk_d = din("bk_pad", [256, 1])
    wv_d = din("wv_t", [D, D])
    bv_d = din("bv_r", [1, D])
    gam_d = din("gam", [128, 1])
    wfc_d = din("wfc_t", [D, 4096])
    wpj_d = din("wpj_t", [4096, D])
    wqd_d = din("wqd", [D, D])
    wkd_d = din("wkd", [D, D])
    wvd_d = din("wvd", [D, D])
    upq_d = din("upq", [128, 128])
    upqs_d = din("upqs", [128, 128])
    upk_d = din("upk", [128, 128])
    upks_d = din("upks", [128, 128])
    wvu_d = din("wvu", [64, 64])
    wc_d = din("wc", [D, D])
    th_d = din("th", [D, D])
    tn_d = din("tn", [D, D])
    ph_d = din("ph", [D, D])
    pn_d = din("pn", [D, D])
    id_d = din("identM", [128, 128])
    rc_d = din("ropec", [128, T])
    rs_d = din("ropes", [128, T])
    m0_d = din("mask0", [128, 256])
    m1_d = din("mask1", [128, 256])
    qr_sp = nc.dram_tensor("qr_sp", [1024, T], BF16, kind="Internal").ap()
    kr_sp = nc.dram_tensor("kr_sp", [1024, T], BF16, kind="Internal").ap()
    lv_sp = nc.dram_tensor("lv_sp", [1024, T], BF16, kind="Internal").ap()
    wfcb_sp = nc.dram_tensor("wfcb_sp", [D, 4096], BF16, kind="Internal").ap()
    wpjb_sp = nc.dram_tensor("wpjb_sp", [4096, D], BF16, kind="Internal").ap()
    out_d = nc.dram_tensor("out", [T, D], F32, kind="ExternalOutput").ap()

    with ExitStack() as stk:
        E = stk.enter_context
        Af = E(nc.sbuf_tensor("arena_f", [128, AF_W], F32))
        Ab = E(nc.sbuf_tensor("arena_b", [128, AB_W], BF16))
        P = [E(nc.psum_tensor(f"p{i}", [128, 512], F32)) for i in range(6)]
        PT = [E(nc.psum_tensor(f"pt{i}", [128, 1024], BF16)) for i in range(2)]
        sems = {k: E(nc.semaphore("sem_" + k))
                for k in ("pe", "ac", "dv", "gp", "sy", "st")}
        block = E(nc.Block())

        ops = {k: [] for k in ("pe", "ac", "dv", "gp", "sy")}
        cnt = {k: 0 for k in ops}
        waited = {}
        ready = {}
        readers = {}

        def W(eng, prod, val):
            if val <= 0:
                return
            key = (eng, prod)
            if waited.get(key, -1) >= val:
                return
            waited[key] = val
            s = sems[prod]
            ops[eng].append(lambda e, s=s, v=val: e.wait_ge(s, v))

        def need(res, eng):
            r = ready.get(res)
            if r and r[0] != eng:
                W(eng, r[0], r[1])

        def acquire(res, eng):
            r = ready.get(res)
            if r and r[0] != eng:
                W(eng, r[0], r[1])
            for e2, v in readers.get(res, {}).items():
                if e2 != eng:
                    W(eng, e2, v)

        def EMIT(eng, fn, reads=(), writes=(), inc=1, sem=None):
            for r in reads:
                need(r, eng)
            for w in writes:
                acquire(w, eng)
            cnt[eng] += inc
            if inc:
                s = sems[sem if sem else eng]
                ops[eng].append(lambda e, f=fn, s=s, n=inc: f(e).then_inc(s, n))
            else:
                ops[eng].append(fn)
            for r in reads:
                readers.setdefault(r, {})[eng] = cnt[eng]
            for w in writes:
                ready[w] = (eng, cnt[eng])
                readers[w] = {}

        def MARK(eng, *names):
            for nm in names:
                ready[nm] = (eng, cnt[eng])
                readers[nm] = {}

        def barrier():
            for c in ops:
                for p in ops:
                    if p != c:
                        W(c, p, cnt[p])

        def af(off, w, p0=0, pn=128):
            return Af[p0:p0 + pn, off:off + w]

        def ab(off, w, p0=0, pn=128):
            return Ab[p0:p0 + pn, off:off + w]

        def Xt(t, c0=0, w=1024):
            return Af[:, XO + t * 1024 + c0: XO + t * 1024 + c0 + w]

        def XTk(k, c0, w):
            return Ab[:, XTO + k * 2048 + c0: XTO + k * 2048 + c0 + w]

        def col(o, n=1, p0=0, pn=128):
            return Af[p0:p0 + pn, o:o + n]

        def xtr(c0, w):
            t0, t1 = c0 // 128, (c0 + w - 1) // 128
            return [f"XT{t}" for t in range(t0, t1 + 1)]

        for t in range(NT):
            EMIT("sy", lambda e, t=t: e.dma_start(Xt(t), x_d[t * 128:(t + 1) * 128, :]),
                 writes=[f"X{t}"], inc=16)
        for (off, w, d_, nm, p0, pn) in (
                (C_ID, 128, id_d, "CID", 0, 128), (C_BV, 1024, bv_d, "CBV", 0, 1),
                (C_M0, 256, m0_d, "CM0", 0, 128), (C_M1, 256, m1_d, "CM1", 0, 128),
                (C_UPQ, 128, upq_d, "CUPQ", 0, 128), (C_UPQS, 128, upqs_d, "CUPQS", 0, 128),
                (C_UPK, 128, upk_d, "CUPK", 0, 128), (C_UPKS, 128, upks_d, "CUPKS", 0, 128),
                (C_UPV1, 64, wvu_d, "CUPV", 0, 64)):
            EMIT("gp", lambda e, off=off, w=w, d_=d_, p0=p0, pn=pn: e.dma_start(
                ab(off, w, p0, pn), d_), writes=[nm], inc=16)
        for (o, d_, nm) in ((S_GB, gam_d, "SGB"),):
            EMIT("sy", lambda e, o=o, d_=d_: e.dma_start(col(o), d_), writes=[nm], inc=16)
        EMIT("sy", lambda e: e.dma_start(af(S_IDF, 128), id_d), writes=["SIDF"], inc=16)
        for (o, d_, sl, nm) in ((S_BQA, bq_d, 0, "SBQA"), (S_BQB, bq_d, 1, "SBQB"),
                                (S_BKA, bk_d, 0, "SBKA"), (S_BKB, bk_d, 1, "SBKB")):
            EMIT("sy", lambda e, o=o, d_=d_, sl=sl: e.dma_start(
                col(o), d_[sl * 128:(sl + 1) * 128, :]), writes=[nm], inc=16)
        EMIT("dv", lambda e: e.memset(col(S_EPS), 1e-5), writes=["SEPS"])
        EMIT("dv", lambda e: e.memset(col(S_HPI), float(np.pi / 2)), writes=["SHPI"])
        EMIT("dv", lambda e: e.memset(ab(C_ONE, 128, 0, 1), 1.0), writes=["CONE"])

        def xt_build(t):
            EMIT("ac", lambda e, t=t: e.activation(ab(XBF, 1024), Xt(t), AF.Copy),
                 reads=[f"X{t}"], writes=["XBF"])
            for f in range(ND):
                EMIT("pe", lambda e, f=f: e.transpose(
                    PT[0][:, f * 128:(f + 1) * 128], ab(XBF + f * 128, 128),
                    ab(C_ID, 128)),
                    reads=["XBF", "CID"], writes=["pt0"])
            for f in range(ND):
                EMIT("dv", lambda e, f=f, t=t: e.tensor_copy(
                    XTk(f, t * 128, 128), PT[0][:, f * 128:(f + 1) * 128]),
                    reads=["pt0"], writes=[f"XT{t}"])

        def layernorm(t, s, rebuild=True):
            sq = af(FS + 1024, 1024)
            EMIT("ac", lambda e, t=t, s=s, sq=sq: e.activation(
                sq, Xt(t), AF.Square, accum_out=col(S_SS + s)),
                reads=[f"X{t}"], writes=["LND", f"SSS{s}"])
            EMIT("ac", lambda e, t=t, s=s: e.activation(
                sq, Xt(t), AF.Copy, accum_out=col(S_SUM + s)),
                reads=[f"X{t}"], writes=["LND", f"SSUM{s}"])
            EMIT("dv", lambda e, s=s: e.tensor_scalar(
                col(S_NMEAN + s), col(S_SUM + s), -1.0 / 1024.0, None, OP.mult),
                reads=[f"SSUM{s}"], writes=[f"SNM{s}"])
            EMIT("ac", lambda e, s=s: e.activation(
                col(S_M2 + s), col(S_NMEAN + s), AF.Square),
                reads=[f"SNM{s}"], writes=[f"SM2{s}"])
            EMIT("dv", lambda e, s=s: e.scalar_tensor_tensor(
                col(S_VAR + s), col(S_SS + s), 1.0 / 1024.0, col(S_M2 + s),
                OP.mult, OP.subtract),
                reads=[f"SSS{s}", f"SM2{s}"], writes=[f"SVAR{s}"])
            EMIT("ac", lambda e, s=s: e.activation(
                col(S_STD + s), col(S_VAR + s), AF.Sqrt, bias=col(S_EPS)),
                reads=[f"SVAR{s}", "SEPS"], writes=[f"SSTD{s}"])
            EMIT("dv", lambda e, s=s: e.reciprocal(col(S_RSTD + s), col(S_STD + s)),
                 reads=[f"SSTD{s}"], writes=[f"SRSTD{s}"], inc=0)
            EMIT("dv", lambda e: e.drain(), inc=0)
            EMIT("dv", lambda e, t=t, s=s: e.tensor_scalar(
                Xt(t), Xt(t), col(S_NMEAN + s), col(S_RSTD + s), OP.add, OP.mult),
                reads=[f"SNM{s}", f"SRSTD{s}"], writes=[f"X{t}"])
            if rebuild:
                xt_build(t)

        for t in range(NT):
            xt_build(t)

        _dummy = [None]
        VTOK = RB
        QTA, QTB, KTA, KTB = RB + 16384, RB + 18432, RB + 20480, RB + 22528
        WVS = RB + 24576
        WQS = RB + 28672
        WKS = RB + 30720
        PTP = [RB + 24576, RB + 26624]
        PTOK = RB + 28672
        OSB = RB + 30720

        for dvh in range(2 if KPH >= 1 else 0):
            for k in range(ND):
                EMIT("gp", lambda e, k=k, dvh=dvh: e.dma_start(
                    ab(WVS + k * 512, 512),
                    wv_d[k * 128:(k + 1) * 128, dvh * 512:(dvh + 1) * 512]),
                    writes=(["WVS"] if k == 0 else []), inc=16)
            MARK("gp", "WVS")
            for nt in range(NT):
                pb = f"p{nt % 2}"
                for k in range(ND):
                    EMIT("pe", lambda e, k=k, nt=nt: e.matmul(
                        P[nt % 2][:, :], XTk(k, nt * 128, 128), ab(WVS + k * 512, 512),
                        start=(k == 0), stop=False),
                        reads=["WVS"] + xtr(nt * 128, 128), writes=[pb])
                EMIT("pe", lambda e, nt=nt, dvh=dvh: e.matmul(
                    P[nt % 2][:, :], ab(C_ONE, 128, 0, 1),
                    ab(C_BV + dvh * 512, 512, 0, 1), start=False, stop=True),
                    reads=["CONE", "CBV"], writes=[pb])
                EMIT("ac", lambda e, nt=nt, dvh=dvh: e.activation(
                    ab(VTOK + nt * 1024 + dvh * 512, 512), P[nt % 2][:, :], AF.Copy),
                    reads=[pb], writes=[f"VT{nt}_{dvh}"])

        for k in range(ND if KPH >= 1 else 0):
            EMIT("gp", lambda e, k=k: e.dma_start(
                ab(WQS + k * 256, 256), wq_d[k * 128:(k + 1) * 128, :]),
                writes=(["WQS"] if k == 0 else []), inc=16)
        MARK("gp", "WQS")
        for k in range(ND if KPH >= 1 else 0):
            EMIT("gp", lambda e, k=k: e.dma_start(
                ab(WKS + k * 256, 256), wk_d[k * 128:(k + 1) * 128, :]),
                writes=(["WKS"] if k == 0 else []), inc=16)
        MARK("gp", "WKS")
        for (dst, wsrc, wnm, g, bq, nm) in (() if KPH < 1 else (
                (QTA, WQS, "WQS", 0, S_BQA, "QTA"), (QTB, WQS, "WQS", 1, S_BQB, "QTB"),
                (KTA, WKS, "WKS", 0, S_BKA, "KTA"), (KTB, WKS, "WKS", 1, S_BKB, "KTB"))):
            for mc in range(4):
                pb = f"p{mc % 2}"
                for k in range(ND):
                    EMIT("pe", lambda e, k=k, mc=mc, wsrc=wsrc, g=g: e.matmul(
                        P[mc % 2][:, :], ab(wsrc + k * 256 + g * 128, 128),
                        XTk(k, mc * 512, 512), start=(k == 0), stop=(k == ND - 1)),
                        reads=[wnm] + xtr(mc * 512, 512), writes=[pb])
                EMIT("dv", lambda e, mc=mc, dst=dst, bq=bq: e.tensor_scalar(
                    ab(dst + mc * 512, 512), P[mc % 2][:, :], col(bq), None, OP.add),
                    reads=[pb, "SBQA", "SBQB", "SBKA", "SBKB"], writes=[nm])

        barrier()

        units = [(h, mt) for h in range(8 if KPH >= 2 else 0)
                 for mt in range(NT)]

        def l0a_scores(u):
            h, mt = units[u]
            g = h // 4
            base = (h % 4) * 32
            qsrc = QTA if g == 0 else QTB
            ksrc = KTA if g == 0 else KTB
            qnm = "QTA" if g == 0 else "QTB"
            knm = "KTA" if g == 0 else "KTB"
            for c in range(4):
                pb = f"p{c % 2}"
                tp = (96, 0) if base == 96 else None
                EMIT("pe", lambda e, c=c, mt=mt, base=base, qsrc=qsrc, ksrc=ksrc,
                     tp=tp: e.matmul(
                         P[c % 2][:, :],
                         Ab[base:base + 32, qsrc + mt * 128:qsrc + mt * 128 + 128],
                         Ab[base:base + 32, ksrc + c * 512:ksrc + c * 512 + 512],
                         start=True, stop=True, tile_position=tp),
                     reads=[qnm, knm], writes=[pb])
                EMIT("ac", lambda e, c=c: e.activation(
                    af(FS + c * 512, 512), P[c % 2][:, :], AF.Prelu,
                    scale=0.25, alpha=0.2),
                    reads=[pb], writes=["SLK"])

        def l0a_tail(u):
            h, mt = units[u]
            s = mt % 2
            EMIT("dv", lambda e, s=s: e.reduce_max(
                col(S_NEGMAX + s), af(FS, 2048), axis=AX, negate=True,
                opt_output=False),
                reads=["SLK"], writes=[f"SNX{s}"], inc=0)
            EMIT("dv", lambda e: e.drain(), inc=0)
            EMIT("dv", lambda e: e.memset(col(S_JUNK), 0.0), writes=[f"SNX{s}"])
            EMIT("ac", lambda e, s=s: e.activation(
                ab(PTOK, 2048), af(FS, 2048), AF.Exp, bias=col(S_NEGMAX + s),
                scale=1.0, accum_out=col(S_R + s)),
                reads=["SLK", f"SNX{s}"], writes=["PTOK", f"SR{s}"])
            EMIT("dv", lambda e, s=s: e.reciprocal(col(S_RREC + s), col(S_R + s)),
                 reads=[f"SR{s}"], writes=[f"SRC{s}"], inc=0)
            EMIT("dv", lambda e, s=s: e.tensor_tensor(
                col(S_GREC + s), col(S_RREC + s), col(S_GB), OP.mult),
                reads=["SGB", f"SRC{s}"], writes=[f"SGR{s}"])
            EMIT("dv", lambda e: e.drain(), inc=0)

        def l0a_back(u):
            h, mt = units[u]
            s = mt % 2
            for bk in range(2):
                pt = f"pt{bk}"
                for i in range(8):
                    EMIT("pe", lambda e, bk=bk, i=i: e.transpose(
                        PT[bk][:, i * 128:(i + 1) * 128],
                        ab(PTOK + (bk * 8 + i) * 128, 128), ab(C_ID, 128)),
                        reads=["PTOK", "CID"], writes=[pt])
            for bk in range(2):
                EMIT("dv", lambda e, bk=bk, s=s: e.tensor_copy(
                    ab(PTP[s] + bk * 1024, 1024), PT[bk][:, :]),
                    reads=[f"pt{bk}"], writes=[f"PTP{s}"])
            for nt in range(NT):
                EMIT("pe", lambda e, nt=nt, h=h, s=s: e.matmul(
                    P[2][:, s * 128:s * 128 + 128],
                    ab(VTOK + nt * 1024 + h * 128, 128),
                    ab(PTP[s] + nt * 128, 128),
                    start=(nt == 0), stop=(nt == NT - 1)),
                    reads=[f"PTP{s}", f"VT{nt}_0", f"VT{nt}_1"], writes=["p2"])
            EMIT("dv", lambda e, s=s: e.tensor_copy(
                ab(OSB + s * 128, 128), P[2][:, s * 128:s * 128 + 128]),
                reads=["p2"], writes=[f"OSB{s}"])
            EMIT("pe", lambda e, s=s: e.transpose(
                PT[0][:, 0:128], ab(OSB + s * 128, 128), ab(C_ID, 128)),
                reads=[f"OSB{s}", "CID"], writes=["pt0"])
            EMIT("dv", lambda e, mt=mt, h=h, s=s: e.scalar_tensor_tensor(
                Xt(mt, h * 128, 128), PT[0][:, 0:128], col(S_GREC + s),
                Xt(mt, h * 128, 128), OP.mult, OP.add),
                reads=["pt0", f"SGR{s}", f"X{mt}"], writes=[f"X{mt}"])

        for u in range(len(units)):
            l0a_scores(u)
            if u >= 1:
                l0a_back(u - 1)
            l0a_tail(u)
        if units:
            l0a_back(len(units) - 1)

        barrier()
        for t in range(NT if KPH >= 3 else 0):
            layernorm(t, t % 2)

        barrier()
        HT = RB
        WFS = [RB + 16384, RB + 20480]
        WPS = [RB + 24576, RB + 25088]
        TMB = [RB + 25600, RB + 26112]

        for mcq in range(4 if KPH >= 4 else 0):
            for fch in range(32):
                buf = (fch // 4) % 2
                if fch % 4 == 0:
                    if mcq == 0:
                        for k in range(ND):
                            EMIT("gp", lambda e, k=k, fch=fch, buf=buf: e.dma_start(
                                ab(WFS[buf] + k * 512, 512),
                                wfc_d[k * 128:(k + 1) * 128,
                                      fch * 128:fch * 128 + 512]),
                                writes=([f"WFS{buf}"] if k == 0 else []), inc=16)
                        MARK("gp", f"WFS{buf}")
                        for k in range(ND):
                            EMIT("sy", lambda e, k=k, fch=fch, buf=buf: e.dma_start(
                                wfcb_sp[k * 128:(k + 1) * 128,
                                        fch * 128:fch * 128 + 512],
                                ab(WFS[buf] + k * 512, 512)),
                                reads=[f"WFS{buf}"], inc=16)
                    else:
                        for k in range(ND):
                            EMIT("sy", lambda e, k=k, fch=fch, buf=buf: e.dma_start(
                                ab(WFS[buf] + k * 512, 512),
                                wfcb_sp[k * 128:(k + 1) * 128,
                                        fch * 128:fch * 128 + 512]),
                                writes=([f"WFS{buf}"] if k == 0 else []), inc=16)
                        MARK("sy", f"WFS{buf}")
                pb = f"p{fch % 2}"
                for k in range(ND):
                    EMIT("pe", lambda e, k=k, fch=fch, mcq=mcq, buf=buf: e.matmul(
                        P[fch % 2][:, :],
                        ab(WFS[buf] + k * 512 + (fch % 4) * 128, 128),
                        XTk(k, mcq * 512, 512), start=(k == 0), stop=(k == ND - 1)),
                        reads=[f"WFS{buf}"] + xtr(mcq * 512, 512), writes=[pb])
                tb = fch % 2
                EMIT("dv", lambda e, fch=fch, tb=tb: e.tensor_scalar(
                    ab(TMB[tb], 512), P[fch % 2][:, :], 0.0, None, OP.max),
                    reads=[pb], writes=[f"TMB{tb}"])
                EMIT("ac", lambda e, fch=fch, tb=tb: e.activation(
                    ab(HT + fch * 512, 512), ab(TMB[tb], 512), AF.Square),
                    reads=[f"TMB{tb}"], writes=[f"HT{fch}"])
            for dc in range(2):
                for ft in range(32):
                    buf = ft % 2
                    if mcq == 0:
                        EMIT("gp", lambda e, ft=ft, dc=dc, buf=buf: e.dma_start(
                            ab(WPS[buf], 512),
                            wpj_d[ft * 128:(ft + 1) * 128, dc * 512:(dc + 1) * 512]),
                            writes=[f"WPS{buf}"], inc=16)
                        EMIT("sy", lambda e, ft=ft, dc=dc, buf=buf: e.dma_start(
                            wpjb_sp[ft * 128:(ft + 1) * 128, dc * 512:(dc + 1) * 512],
                            ab(WPS[buf], 512)),
                            reads=[f"WPS{buf}"], inc=16)
                    else:
                        EMIT("sy", lambda e, ft=ft, dc=dc, buf=buf: e.dma_start(
                            ab(WPS[buf], 512),
                            wpjb_sp[ft * 128:(ft + 1) * 128, dc * 512:(dc + 1) * 512]),
                            writes=[f"WPS{buf}"], inc=16)
                    for mt4 in range(4):
                        EMIT("pe", lambda e, ft=ft, mt4=mt4, buf=buf: e.matmul(
                            P[2 + mt4][:, :], ab(HT + ft * 512 + mt4 * 128, 128),
                            ab(WPS[buf], 512), start=(ft == 0), stop=(ft == 31)),
                            reads=[f"WPS{buf}", f"HT{ft}"], writes=[f"p{2 + mt4}"])
                for mt4 in range(4):
                    mt = mcq * 4 + mt4
                    pb = f"p{2 + mt4}"
                    a_, b_ = af(FS, 512), af(FS + 512, 512)
                    EMIT("dv", lambda e, mt4=mt4: e.tensor_scalar(
                        af(FS, 512), P[2 + mt4][:, :], TH[0], None, OP.is_gt),
                        reads=[pb], writes=["ACCA"])
                    for i, (src, dst, rs_, ws) in enumerate(
                            ((a_, b_, "ACCA", "ACCB"), (b_, a_, "ACCB", "ACCA"),
                             (a_, b_, "ACCA", "ACCB"), (b_, a_, "ACCB", "ACCA"))):
                        EMIT("dv", lambda e, mt4=mt4, i=i, src=src, dst=dst:
                             e.scalar_tensor_tensor(
                                 dst, P[2 + mt4][:, :], TH[i + 1], src,
                                 OP.is_gt, OP.add),
                             reads=[pb, rs_], writes=[ws])
                    EMIT("dv", lambda e, mt=mt, dc=dc: e.scalar_tensor_tensor(
                        Xt(mt, dc * 512, 512), af(FS, 512), 0.2,
                        Xt(mt, dc * 512, 512), OP.mult, OP.add),
                        reads=["ACCA", f"X{mt}"], writes=[f"X{mt}"])
            for mt4 in range(4):
                mt = mcq * 4 + mt4
                layernorm(mt, mt % 2)

        barrier()
        LATQ = [RB, RB + 2048]
        LATK = [RB + 4096, RB + 6144]
        LATV = [RB + 8192, RB + 10240]
        QTJ = [RB + 12288, RB + 14336]
        KTJ = [RB + 16384, RB + 18432]
        WDQ = [RB + 20480, RB + 21504]
        WDK = [RB + 22528, RB + 23552]
        WDV = [RB + 24576, RB + 25600]
        RPC = RB + 26624
        RPS = RB + 28672
        TP1 = RB + 30720
        TP2 = RB + 31232

        EMIT("gp", lambda e: e.dma_start(ab(RPC, 2048), rc_d), writes=["RPC"], inc=16)
        EMIT("gp", lambda e: e.dma_start(ab(RPS, 2048), rs_d), writes=["RPS"], inc=16)

        for j in range(8 if KPH >= 5 else 0):
            par = j % 2
            for (wd, dram, nm) in ((WDQ, wqd_d, "WDQ"), (WDK, wkd_d, "WDK"),
                                   (WDV, wvd_d, "WDV")):
                for k in range(ND):
                    EMIT("gp", lambda e, k=k, j=j, wd=wd, dram=dram, par=par:
                         e.dma_start(ab(wd[par] + k * 128, 128),
                                     dram[k * 128:(k + 1) * 128, j * 128:(j + 1) * 128]),
                         writes=([f"{nm}{par}"] if k == 0 else []), inc=16)
                MARK("gp", f"{nm}{par}")
            for (wd, lat, nm, lnm) in ((WDQ, LATQ, "WDQ", "LATQ"),
                                       (WDK, LATK, "WDK", "LATK"),
                                       (WDV, LATV, "WDV", "LATV")):
                for mc in range(4):
                    pb = f"p{mc % 2}"
                    for k in range(ND):
                        EMIT("pe", lambda e, k=k, mc=mc, wd=wd, par=par: e.matmul(
                            P[mc % 2][:, :], ab(wd[par] + k * 128, 128),
                            XTk(k, mc * 512, 512), start=(k == 0), stop=(k == ND - 1)),
                            reads=[f"{nm}{par}"] + xtr(mc * 512, 512), writes=[pb])
                    EMIT("ac", lambda e, mc=mc, lat=lat, par=par: e.activation(
                        ab(lat[par] + mc * 512, 512), P[mc % 2][:, :], AF.Copy),
                        reads=[pb], writes=[f"{lnm}{par}"])
            EMIT("sy", lambda e, j=j, par=par: e.dma_start(
                lv_sp[j * 128:(j + 1) * 128, :], ab(LATV[par], 2048)),
                reads=[f"LATV{par}"], inc=16)
            for (lat, lnm, upo, unm, upso, usnm, qt, qnm, spill) in (
                    (LATQ, "LATQ", C_UPQ, "CUPQ", C_UPQS, "CUPQS", QTJ, "QTJ", qr_sp),
                    (LATK, "LATK", C_UPK, "CUPK", C_UPKS, "CUPKS", KTJ, "KTJ", kr_sp)):
                for mc in range(4):
                    EMIT("pe", lambda e, mc=mc, lat=lat, upo=upo, par=par: e.matmul(
                        P[2][:, :], ab(upo, 128), ab(lat[par] + mc * 512, 512),
                        start=True, stop=True),
                        reads=[f"{lnm}{par}", unm], writes=["p2"])
                    EMIT("pe", lambda e, mc=mc, lat=lat, upso=upso, par=par: e.matmul(
                        P[3][:, :], ab(upso, 128), ab(lat[par] + mc * 512, 512),
                        start=True, stop=True),
                        reads=[f"{lnm}{par}", usnm], writes=["p3"])
                    EMIT("dv", lambda e, mc=mc: e.tensor_tensor(
                        ab(TP1, 512), P[2][:, :], ab(RPC + mc * 512, 512), OP.mult),
                        reads=["p2", "RPC"], writes=["TP1"])
                    EMIT("dv", lambda e, mc=mc: e.tensor_tensor(
                        ab(TP2, 512), P[3][:, :], ab(RPS + mc * 512, 512), OP.mult),
                        reads=["p3", "RPS"], writes=["TP2"])
                    EMIT("dv", lambda e, mc=mc, qt=qt, par=par: e.tensor_tensor(
                        ab(qt[par] + mc * 512, 512), ab(TP1, 512), ab(TP2, 512),
                        OP.add),
                        reads=["TP1", "TP2"], writes=[f"{qnm}{par}"])
                EMIT("sy", lambda e, j=j, qt=qt, par=par, spill=spill: e.dma_start(
                    spill[j * 128:(j + 1) * 128, :], ab(qt[par], 2048)),
                    reads=[f"{qnm}{par}"], inc=16)

        barrier()
        KTH = [XTO, XTO + 2048]
        QTH = [XTO + 4096, XTO + 6144]
        LVH = [XTO + 8192, XTO + 10240]
        PTL = XTO + 12288
        YT = RB
        VH = [RB + 16384, RB + 17424]
        PTMP = [RB + 18464, RB + 18720]
        YSTG = RB + 18976
        WC = RB + 19232
        RB1 = RB + 27424
        RB2 = RB + 27680

        for k in range(ND if KPH >= 6 else 0):
            EMIT("gp", lambda e, k=k: e.dma_start(
                ab(WC + k * 1024, 1024), wc_d[k * 128:(k + 1) * 128, :]),
                writes=(["WC"] if k == 0 else []), inc=16)
        MARK("gp", "WC")
        for par in range(2 if KPH >= 6 else 0):
            for nt in range(NT):
                EMIT("dv", lambda e, par=par, nt=nt: e.memset(
                    ab(VH[par] + nt * 65 + 64, 1), 1.0), writes=[f"VHO{par}"])

        units2 = []
        for h in range(16 if KPH >= 6 else 0):
            for mcq in range(8):
                units2.append((h, mcq))

        def l1_front(i):
            h, mcq = units2[i]
            par = h % 2
            avp = f"p{2 + i % 2}"
            if mcq == 0:
                EMIT("sy", lambda e, h=h, par=par: e.dma_start(
                    ab(LVH[par], 2048, 0, 64), lv_sp[h * 64:(h + 1) * 64, :]),
                    writes=[f"LVH{par}"], inc=16)
                EMIT("sy", lambda e, h=h, par=par: e.dma_start(
                    ab(KTH[par], 2048, 0, 64), kr_sp[h * 64:(h + 1) * 64, :]),
                    writes=[f"KTH{par}"], inc=16)
                EMIT("sy", lambda e, h=h, par=par: e.dma_start(
                    ab(QTH[par], 2048, 0, 64), qr_sp[h * 64:(h + 1) * 64, :]),
                    writes=[f"QTH{par}"], inc=16)
                for nt in range(NT):
                    pb = f"p{nt % 2}"
                    EMIT("pe", lambda e, nt=nt, par=par: e.matmul(
                        P[nt % 2][:, 0:64], ab(LVH[par] + nt * 128, 128, 0, 64),
                        ab(C_UPV1, 64, 0, 64), start=True, stop=True),
                        reads=[f"LVH{par}", "CUPV"], writes=[pb])
                    EMIT("dv", lambda e, nt=nt, par=par: e.tensor_copy(
                        ab(VH[par] + nt * 65, 64), P[nt % 2][:, 0:64]),
                        reads=[pb, f"VHO{par}"], writes=[f"VH{par}"])
            maxnt = 2 * mcq + 2
            for nt in range(maxnt):
                pb = f"p{nt % 2}"
                EMIT("pe", lambda e, nt=nt, mcq=mcq, par=par: e.matmul(
                    P[nt % 2][:, 0:256], ab(KTH[par] + nt * 128, 128, 0, 64),
                    ab(QTH[par] + mcq * 256, 256, 0, 64),
                    start=True, stop=True),
                    reads=[f"KTH{par}", f"QTH{par}"], writes=[pb])
                if nt < maxnt - 2:
                    EMIT("ac", lambda e, nt=nt: e.activation(
                        ab(PTL + nt * 256, 256), P[nt % 2][:, 0:256], AF.Exp,
                        scale=0.125),
                        reads=[pb], writes=[f"PTL{nt}"])
                else:
                    v = nt - 2 * mcq
                    pp = nt % 2
                    EMIT("ac", lambda e, pp=pp, nt=nt: e.activation(
                        ab(PTMP[pp], 256), P[nt % 2][:, 0:256], AF.Exp,
                        scale=0.125),
                        reads=[pb], writes=[f"PTMP{pp}"])
                    EMIT("dv", lambda e, nt=nt, v=v, pp=pp: e.tensor_tensor(
                        ab(PTL + nt * 256, 256), ab(PTMP[pp], 256),
                        ab(C_M0 if v == 0 else C_M1, 256), OP.mult),
                        reads=[f"PTMP{pp}", "CM0", "CM1"], writes=[f"PTL{nt}"])
            for nt in range(maxnt):
                EMIT("pe", lambda e, nt=nt, par=par, maxnt=maxnt, i=i: e.matmul(
                    P[2 + i % 2][0:65, 0:256], ab(VH[par] + nt * 65, 65),
                    ab(PTL + nt * 256, 256),
                    start=(nt == 0), stop=(nt == maxnt - 1)),
                    reads=[f"VH{par}", f"PTL{nt}"], writes=[avp])

        def l1_back(i):
            h, mcq = units2[i]
            avp = f"p{2 + i % 2}"
            AVP = P[2 + i % 2]
            EMIT("dv", lambda e, AVP=AVP: e.tensor_copy(
                af(FS + 768, 256, 0, 1), AVP[64:65, 0:256]),
                reads=[avp], writes=["RRAW"])
            for half in range(2):
                EMIT("pe", lambda e, half=half: e.transpose(
                    P[4][:, half:half + 1], af(FS + 768 + half * 128, 128, 0, 1),
                    af(S_IDF, 1, 0, 1)),
                    reads=["RRAW", "SIDF"], writes=["p4"])
            EMIT("dv", lambda e: e.reciprocal(af(S_RCL, 2), P[4][:, 0:2]),
                 reads=["p4"], writes=["SRCL"])
            for half in range(2):
                EMIT("pe", lambda e, half=half: e.transpose(
                    P[4][0:1, 4 + half * 128:132 + half * 128],
                    af(S_RCL + half, 1), af(S_IDF, 128)),
                    reads=["SRCL", "SIDF"], writes=["p4b"])
            EMIT("dv", lambda e: e.tensor_copy(ab(RB1, 256, 0, 1),
                                               P[4][0:1, 4:260]),
                 reads=["p4b"], writes=["RB1"])
            EMIT("dv", lambda e: e.tensor_tensor(
                af(FS + 512, 256, 0, 1), P[4][0:1, 4:260],
                ab(RB1, 256, 0, 1), OP.subtract),
                reads=["p4b", "RB1"], writes=["RRES"])
            EMIT("dv", lambda e: e.tensor_copy(ab(RB2, 256, 0, 1),
                                               af(FS + 512, 256, 0, 1)),
                 reads=["RRES"], writes=["RB2"])
            EMIT("pe", lambda e: e.matmul(
                P[5][0:64, 0:256], ab(C_ONE, 64, 0, 1), ab(RB1, 256, 0, 1),
                start=True, stop=False),
                reads=["CONE", "RB1"], writes=["p5"])
            EMIT("pe", lambda e: e.matmul(
                P[5][0:64, 0:256], ab(C_ONE, 64, 0, 1), ab(RB2, 256, 0, 1),
                start=False, stop=True),
                reads=["CONE", "RB2"], writes=["p5"])
            EMIT("dv", lambda e: e.tensor_copy(af(FS, 256, 0, 64),
                                               P[5][0:64, 0:256]),
                 reads=["p5"], writes=["BCF"])
            pt_ = h // 2
            if h % 2 == 0:
                EMIT("dv", lambda e, pt_=pt_, mcq=mcq, AVP=AVP: e.tensor_tensor(
                    ab(YT + pt_ * 2048 + mcq * 256, 256, 0, 64),
                    AVP[0:64, 0:256], af(FS, 256, 0, 64), OP.mult),
                    reads=[avp, "BCF"], writes=[f"YT{pt_}_{mcq}"])
            else:
                EMIT("dv", lambda e, AVP=AVP: e.tensor_tensor(
                    ab(YSTG, 256, 0, 64), AVP[0:64, 0:256], af(FS, 256, 0, 64),
                    OP.mult),
                    reads=[avp, "BCF"], writes=["YSTG"])
                EMIT("sy", lambda e, pt_=pt_, mcq=mcq: e.dma_start(
                    Ab[64:128, YT + pt_ * 2048 + mcq * 256:
                       YT + pt_ * 2048 + mcq * 256 + 256],
                    ab(YSTG, 256, 0, 64)),
                    reads=["YSTG"], writes=[f"YT{pt_}_{mcq}"], inc=16)

        for i in range(len(units2)):
            l1_front(i)
            if i >= 1:
                l1_back(i - 1)
        if units2:
            l1_back(len(units2) - 1)

        barrier()
        for t in range(NT if KPH >= 7 else 0):
            readies = [f"YT{p_}_{t // 2}" for p_ in range(ND)]
            for dc in range(2):
                pb = f"p{4 + dc}"
                for k in range(ND):
                    EMIT("pe", lambda e, k=k, t=t, dc=dc: e.matmul(
                        P[4 + dc][:, :], ab(YT + k * 2048 + t * 128, 128),
                        ab(WC + k * 1024 + dc * 512, 512),
                        start=(k == 0), stop=(k == ND - 1)),
                        reads=readies + ["WC"], writes=[pb])
                EMIT("dv", lambda e, t=t, dc=dc: e.tensor_tensor(
                    Xt(t, dc * 512, 512), P[4 + dc][:, :], Xt(t, dc * 512, 512),
                    OP.add),
                    reads=[pb, f"X{t}"], writes=[f"X{t}"])
            layernorm(t, t % 2)

        barrier()
        CST = RB
        SNT = RB + 8192
        for (src1, src2, dst, dnm, useb) in (() if KPH < 8 else ((th_d, tn_d, CST, "CST", True),
                                             (ph_d, pn_d, SNT, "SNT", False))):
            for k in range(ND):
                for cc in range(2):
                    EMIT("sy", lambda e, k=k, cc=cc, src1=src1: e.dma_start(
                        af(FS, 512),
                        src1[k * 128:(k + 1) * 128, cc * 512:(cc + 1) * 512]),
                        writes=["TS1"], inc=16)
                    EMIT("sy", lambda e, k=k, cc=cc, src2=src2: e.dma_start(
                        af(FS + 512, 512),
                        src2[k * 128:(k + 1) * 128, cc * 512:(cc + 1) * 512]),
                        writes=["TS2"], inc=16)
                    EMIT("dv", lambda e: e.tensor_tensor(
                        af(FS, 512), af(FS, 512), af(FS + 512, 512), OP.add),
                        reads=["TS1", "TS2"], writes=["TS1"])
                    if useb:
                        EMIT("ac", lambda e, k=k, cc=cc, dst=dst: e.activation(
                            ab(dst + k * 1024 + cc * 512, 512), af(FS, 512),
                            AF.Sin, bias=col(S_HPI)),
                            reads=["TS1", "SHPI"], writes=[dnm])
                    else:
                        EMIT("ac", lambda e, k=k, cc=cc, dst=dst: e.activation(
                            ab(dst + k * 1024 + cc * 512, 512), af(FS, 512),
                            AF.Sin),
                            reads=["TS1"], writes=[dnm])
        for t in range(NT):
            for dc in range(2 if KPH >= 8 else 0):
                for k in range(ND):
                    EMIT("pe", lambda e, k=k, t=t, dc=dc: e.matmul(
                        P[0][:, :], XTk(k, t * 128, 128),
                        ab(CST + k * 1024 + dc * 512, 512),
                        start=(k == 0), stop=(k == ND - 1)),
                        reads=["CST"] + xtr(t * 128, 128), writes=["p0"])
                for k in range(ND):
                    EMIT("pe", lambda e, k=k, t=t, dc=dc: e.matmul(
                        P[1][:, :], XTk(k, t * 128, 128),
                        ab(SNT + k * 1024 + dc * 512, 512),
                        start=(k == 0), stop=(k == ND - 1)),
                        reads=["SNT"] + xtr(t * 128, 128), writes=["p1"])
                EMIT("ac", lambda e: e.activation(af(FS, 512), P[0][:, :], AF.Square),
                     reads=["p0"], writes=["RE2"])
                EMIT("ac", lambda e: e.activation(af(FS + 512, 512), P[1][:, :],
                                                  AF.Square),
                     reads=["p1"], writes=["IM2"])
                EMIT("dv", lambda e: e.tensor_tensor(
                    af(FS, 512), af(FS, 512), af(FS + 512, 512), OP.add),
                    reads=["RE2", "IM2"], writes=["RE2"])
                EMIT("ac", lambda e: e.activation(af(FS + 512, 512), af(FS, 512),
                                                  AF.Sqrt),
                     reads=["RE2"], writes=["IM2"])
                EMIT("dv", lambda e, t=t, dc=dc: e.tensor_tensor(
                    Xt(t, dc * 512, 512), af(FS + 512, 512), Xt(t, dc * 512, 512),
                    OP.add),
                    reads=["IM2", f"X{t}"], writes=[f"X{t}"])
            if KPH >= 8:
                layernorm(t, t % 2, rebuild=False)
            EMIT("sy", lambda e, t=t: e.dma_start(
                out_d[t * 128:(t + 1) * 128, :], Xt(t)),
                reads=[f"X{t}"], inc=16, sem="st")

        ops["sy"].append(lambda e: e.wait_ge(sems["st"], 16 * NT))

        @block.tensor
        def _(t_):
            for fn in ops["pe"]:
                fn(t_)

        @block.scalar
        def _(s_):
            for fn in ops["ac"]:
                fn(s_)

        @block.vector
        def _(v_):
            for fn in ops["dv"]:
                fn(v_)

        @block.gpsimd
        def _(g_):
            for fn in ops["gp"]:
                fn(g_)

        @block.sync
        def _(s_):
            for fn in ops["sy"]:
                fn(s_)

    return nc


def _host_prep(inp):
    f32 = np.float32
    d = {}
    Wq = np.asarray(inp["L0_attn_Wq"], f32)
    Wk = np.asarray(inp["L0_attn_Wk"], f32)
    wq_t = np.zeros((1024, 256), f32)
    wk_t = np.zeros((1024, 256), f32)
    bq_pad = np.zeros((256, 1), f32)
    bk_pad = np.zeros((256, 1), f32)
    bq = np.asarray(inp["L0_attn_bq"], f32)
    bk = np.asarray(inp["L0_attn_bk"], f32)
    for h in range(8):
        g, hh = h // 4, h % 4
        c0 = g * 128 + hh * 32
        wq_t[:, c0:c0 + 16] = Wq[h * 16:(h + 1) * 16, :].T
        wk_t[:, c0:c0 + 16] = Wk[h * 16:(h + 1) * 16, :].T
        bq_pad[c0:c0 + 16, 0] = bq[h * 16:(h + 1) * 16]
        bk_pad[c0:c0 + 16, 0] = bk[h * 16:(h + 1) * 16]
    d["wq_t"], d["wk_t"] = wq_t, wk_t
    d["bq_pad"], d["bk_pad"] = bq_pad, bk_pad
    d["wv_t"] = np.ascontiguousarray(np.asarray(inp["L0_attn_Wv"], f32).T)
    d["bv_r"] = np.asarray(inp["L0_attn_bv"], f32).reshape(1, 1024)
    d["gam"] = np.broadcast_to(
        np.asarray(inp["L0_attn_gamma"], f32).reshape(1, 1), (128, 1)).copy()
    d["wfc_t"] = np.ascontiguousarray(np.asarray(inp["L0_ffn_W_fc"], f32).T)
    d["wpj_t"] = np.ascontiguousarray(np.asarray(inp["L0_ffn_W_proj"], f32).T)
    d["wqd"] = np.asarray(inp["L1_attn_Wq_down"], f32)
    d["wkd"] = np.asarray(inp["L1_attn_Wk_down"], f32)
    d["wvd"] = np.asarray(inp["L1_attn_Wv_down"], f32)

    def upblock(c, e):
        U = np.concatenate([c, e], axis=1)
        out = np.zeros((128, 128), f32)
        out[0:64, 0:64] = U
        out[64:128, 64:128] = U
        return out

    def upswap(e):
        perm = np.arange(32)
        perm = perm + 1 - 2 * (perm % 2)
        Us = np.zeros((64, 64), f32)
        Us[:, 32:64] = e[:, perm]
        out = np.zeros((128, 128), f32)
        out[0:64, 0:64] = Us
        out[64:128, 64:128] = Us
        return out

    qc = np.asarray(inp["L1_attn_Wq_up_c"], f32)
    qe = np.asarray(inp["L1_attn_Wq_up_e"], f32)
    kc = np.asarray(inp["L1_attn_Wk_up_c"], f32)
    ke = np.asarray(inp["L1_attn_Wk_up_e"], f32)
    d["upq"] = upblock(qc, qe)
    d["upqs"] = upswap(qe)
    d["upk"] = upblock(kc, ke)
    d["upks"] = upswap(ke)
    d["wvu"] = np.asarray(inp["L1_attn_Wv_up"], f32)
    d["wc"] = np.asarray(inp["L1_attn_Wc"], f32)
    d["th"] = np.asarray(inp["L1_ffn_theta"], f32)
    d["tn"] = np.asarray(inp["L1_ffn_theta_noise"], f32)
    d["ph"] = np.asarray(inp["L1_ffn_phi"], f32)
    d["pn"] = np.asarray(inp["L1_ffn_phi_noise"], f32)
    d["identM"] = np.eye(128, dtype=f32)
    tpos = np.arange(T, dtype=np.float64)
    invf = 10000.0 ** (-np.arange(16, dtype=np.float64) / 16.0)
    ang = tpos[None, :] * invf[:, None]
    cosv = np.cos(ang)
    sinv = np.sin(ang)
    rc = np.zeros((128, T), np.float64)
    rs = np.zeros((128, T), np.float64)
    for r in range(128):
        j = r % 64
        if j < 32:
            rc[r] = 1.0
        else:
            i = j - 32
            rc[r] = cosv[i % 16]
            rs[r] = (-1.0 if i % 2 == 0 else 1.0) * sinv[i % 16]
    d["ropec"] = rc.astype(f32)
    d["ropes"] = rs.astype(f32)
    m = np.zeros((2, 128, 256), f32)
    for v in range(2):
        for nl in range(128):
            m[v, nl, :] = (128 * v + nl <= np.arange(256)).astype(f32)
    d["mask0"], d["mask1"] = m[0], m[1]
    return d


def kernel(**inputs):
    shared = _host_prep(inputs)
    x = np.ascontiguousarray(np.asarray(inputs["L0_attn_x"], np.float32))
    nc = build_nc()
    in_maps = [dict(shared, x=np.ascontiguousarray(x[i])) for i in range(8)]
    res = run_bass_kernel_spmd(nc, in_maps, core_ids=list(range(8)))
    out = np.stack([np.asarray(res.results[i]["out"], np.float32)
                    for i in range(8)])
    return out


# version 05
# speedup vs baseline: 1.0875x; 1.0117x over previous
import math
import os
import numpy as np
import concourse.bass as bass
import concourse.mybir as mybir
from concourse.bass_utils import run_bass_kernel_spmd
from concourse.alu_op_type import AluOpType
from contextlib import ExitStack

F32 = mybir.dt.float32
BF16 = mybir.dt.bfloat16
AF = mybir.ActivationFunctionType
AX = mybir.AxisListType.X
OP = AluOpType

T, D, NT, ND = 2048, 1024, 16, 8
KPH = int(os.environ.get("KPH", "99"))

TH = [0.24419428096993973, 0.4639691338428854, 0.6861859295255306,
      0.9059607823984763, 1.0839784132255623]

AF_W = 21504
XO = 0
FS = 16384
ST = 20480

AB_W = 54272
XTO = 0
RB = 16384
CB = 50688
C_ID = CB
C_ONE = CB + 128
C_BV = CB + 256
C_M0 = CB + 1280
C_M1 = CB + 1536
C_UPQ = CB + 1792
C_UPQS = CB + 1920
C_UPK = CB + 2048
C_UPKS = CB + 2176
C_UPV1 = CB + 2304
XBF = 53248

S_NEGMAX = ST + 0
S_R = ST + 2
S_RREC = ST + 4
S_GREC = ST + 6
S_SUM = ST + 8
S_SS = ST + 10
S_NMEAN = ST + 12
S_M2 = ST + 14
S_VAR = ST + 16
S_STD = ST + 18
S_RSTD = ST + 20
S_EPS = ST + 22
S_JUNK = ST + 23
S_GB = ST + 24
S_HPI = ST + 25
S_BQA = ST + 26
S_BQB = ST + 27
S_BKA = ST + 28
S_BKB = ST + 29
S_IDF = ST + 512
S_RCL = ST + 640


def build_nc():
    nc = bass.Bass("TRN2", target_bir_lowering=False)
    dr = {}

    def din(name, shape, dt=F32):
        dr[name] = nc.dram_tensor(name, shape, dt, kind="ExternalInput").ap()
        return dr[name]

    x_d = din("x", [T, D])
    wq_d = din("wq_t", [D, 256])
    bq_d = din("bq_pad", [256, 1])
    wk_d = din("wk_t", [D, 256])
    bk_d = din("bk_pad", [256, 1])
    wv_d = din("wv_t", [D, D])
    bv_d = din("bv_r", [1, D])
    gam_d = din("gam", [128, 1])
    wfc_d = din("wfc_t", [D, 4096])
    wpj_d = din("wpj_t", [4096, D])
    wqd_d = din("wqd", [D, D])
    wkd_d = din("wkd", [D, D])
    wvd_d = din("wvd", [D, D])
    upq_d = din("upq", [128, 128])
    upqs_d = din("upqs", [128, 128])
    upk_d = din("upk", [128, 128])
    upks_d = din("upks", [128, 128])
    wvu_d = din("wvu", [64, 64])
    wc_d = din("wc", [D, D])
    th_d = din("th", [D, D])
    tn_d = din("tn", [D, D])
    ph_d = din("ph", [D, D])
    pn_d = din("pn", [D, D])
    id_d = din("identM", [128, 128])
    rc_d = din("ropec", [128, T])
    rs_d = din("ropes", [128, T])
    m0_d = din("mask0", [128, 256])
    m1_d = din("mask1", [128, 256])
    qr_sp = nc.dram_tensor("qr_sp", [1024, T], BF16, kind="Internal").ap()
    kr_sp = nc.dram_tensor("kr_sp", [1024, T], BF16, kind="Internal").ap()
    lv_sp = nc.dram_tensor("lv_sp", [1024, T], BF16, kind="Internal").ap()
    wfcb_sp = nc.dram_tensor("wfcb_sp", [D, 4096], BF16, kind="Internal").ap()
    wpjb_sp = nc.dram_tensor("wpjb_sp", [4096, D], BF16, kind="Internal").ap()
    out_d = nc.dram_tensor("out", [T, D], F32, kind="ExternalOutput").ap()

    with ExitStack() as stk:
        E = stk.enter_context
        Af = E(nc.sbuf_tensor("arena_f", [128, AF_W], F32))
        Ab = E(nc.sbuf_tensor("arena_b", [128, AB_W], BF16))
        P = [E(nc.psum_tensor(f"p{i}", [128, 512], F32)) for i in range(6)]
        PT = [E(nc.psum_tensor(f"pt{i}", [128, 1024], BF16)) for i in range(2)]
        sems = {k: E(nc.semaphore("sem_" + k))
                for k in ("pe", "ac", "dv", "gp", "sy", "st")}
        block = E(nc.Block())

        ops = {k: [] for k in ("pe", "ac", "dv", "gp", "sy")}
        cnt = {k: 0 for k in ops}
        waited = {}
        ready = {}
        readers = {}

        def W(eng, prod, val):
            if val <= 0:
                return
            key = (eng, prod)
            if waited.get(key, -1) >= val:
                return
            waited[key] = val
            s = sems[prod]
            ops[eng].append(lambda e, s=s, v=val: e.wait_ge(s, v))

        def need(res, eng):
            r = ready.get(res)
            if r and r[0] != eng:
                W(eng, r[0], r[1])

        def acquire(res, eng):
            r = ready.get(res)
            if r and r[0] != eng:
                W(eng, r[0], r[1])
            for e2, v in readers.get(res, {}).items():
                if e2 != eng:
                    W(eng, e2, v)

        def EMIT(eng, fn, reads=(), writes=(), inc=1, sem=None):
            for r in reads:
                need(r, eng)
            for w in writes:
                acquire(w, eng)
            cnt[eng] += inc
            if inc:
                s = sems[sem if sem else eng]
                ops[eng].append(lambda e, f=fn, s=s, n=inc: f(e).then_inc(s, n))
            else:
                ops[eng].append(fn)
            for r in reads:
                readers.setdefault(r, {})[eng] = cnt[eng]
            for w in writes:
                ready[w] = (eng, cnt[eng])
                readers[w] = {}

        def MARK(eng, *names):
            for nm in names:
                ready[nm] = (eng, cnt[eng])
                readers[nm] = {}

        def barrier():
            for c in ops:
                for p in ops:
                    if p != c:
                        W(c, p, cnt[p])

        def af(off, w, p0=0, pn=128):
            return Af[p0:p0 + pn, off:off + w]

        def ab(off, w, p0=0, pn=128):
            return Ab[p0:p0 + pn, off:off + w]

        def Xt(t, c0=0, w=1024):
            return Af[:, XO + t * 1024 + c0: XO + t * 1024 + c0 + w]

        def XTk(k, c0, w):
            return Ab[:, XTO + k * 2048 + c0: XTO + k * 2048 + c0 + w]

        def col(o, n=1, p0=0, pn=128):
            return Af[p0:p0 + pn, o:o + n]

        def xtr(c0, w):
            t0, t1 = c0 // 128, (c0 + w - 1) // 128
            return [f"XT{t}" for t in range(t0, t1 + 1)]

        for t in range(NT):
            EMIT("sy", lambda e, t=t: e.dma_start(Xt(t), x_d[t * 128:(t + 1) * 128, :]),
                 writes=[f"X{t}"], inc=16)
        for (off, w, d_, nm, p0, pn) in (
                (C_ID, 128, id_d, "CID", 0, 128), (C_BV, 1024, bv_d, "CBV", 0, 1),
                (C_M0, 256, m0_d, "CM0", 0, 128), (C_M1, 256, m1_d, "CM1", 0, 128),
                (C_UPQ, 128, upq_d, "CUPQ", 0, 128), (C_UPQS, 128, upqs_d, "CUPQS", 0, 128),
                (C_UPK, 128, upk_d, "CUPK", 0, 128), (C_UPKS, 128, upks_d, "CUPKS", 0, 128),
                (C_UPV1, 64, wvu_d, "CUPV", 0, 64)):
            EMIT("gp", lambda e, off=off, w=w, d_=d_, p0=p0, pn=pn: e.dma_start(
                ab(off, w, p0, pn), d_), writes=[nm], inc=16)
        for (o, d_, nm) in ((S_GB, gam_d, "SGB"),):
            EMIT("sy", lambda e, o=o, d_=d_: e.dma_start(col(o), d_), writes=[nm], inc=16)
        EMIT("sy", lambda e: e.dma_start(af(S_IDF, 128), id_d), writes=["SIDF"], inc=16)
        for (o, d_, sl, nm) in ((S_BQA, bq_d, 0, "SBQA"), (S_BQB, bq_d, 1, "SBQB"),
                                (S_BKA, bk_d, 0, "SBKA"), (S_BKB, bk_d, 1, "SBKB")):
            EMIT("sy", lambda e, o=o, d_=d_, sl=sl: e.dma_start(
                col(o), d_[sl * 128:(sl + 1) * 128, :]), writes=[nm], inc=16)
        EMIT("dv", lambda e: e.memset(col(S_EPS), 1e-5), writes=["SEPS"])
        EMIT("dv", lambda e: e.memset(col(S_HPI), float(np.pi / 2)), writes=["SHPI"])
        EMIT("dv", lambda e: e.memset(ab(C_ONE, 128, 0, 1), 1.0), writes=["CONE"])

        def xt_build(t):
            EMIT("ac", lambda e, t=t: e.activation(ab(XBF, 1024), Xt(t), AF.Copy),
                 reads=[f"X{t}"], writes=["XBF"])
            for f in range(ND):
                EMIT("pe", lambda e, f=f: e.transpose(
                    PT[0][:, f * 128:(f + 1) * 128], ab(XBF + f * 128, 128),
                    ab(C_ID, 128)),
                    reads=["XBF", "CID"], writes=["pt0"])
            for f in range(ND):
                EMIT("dv", lambda e, f=f, t=t: e.tensor_copy(
                    XTk(f, t * 128, 128), PT[0][:, f * 128:(f + 1) * 128]),
                    reads=["pt0"], writes=[f"XT{t}"])

        def layernorm(t, s, rebuild=True):
            sq = af(FS + 1024, 1024)
            EMIT("ac", lambda e, t=t, s=s, sq=sq: e.activation(
                sq, Xt(t), AF.Square, accum_out=col(S_SS + s)),
                reads=[f"X{t}"], writes=["LND", f"SSS{s}"])
            EMIT("ac", lambda e, t=t, s=s: e.activation(
                sq, Xt(t), AF.Copy, accum_out=col(S_SUM + s)),
                reads=[f"X{t}"], writes=["LND", f"SSUM{s}"])
            EMIT("dv", lambda e, s=s: e.tensor_scalar(
                col(S_NMEAN + s), col(S_SUM + s), -1.0 / 1024.0, None, OP.mult),
                reads=[f"SSUM{s}"], writes=[f"SNM{s}"])
            EMIT("ac", lambda e, s=s: e.activation(
                col(S_M2 + s), col(S_NMEAN + s), AF.Square),
                reads=[f"SNM{s}"], writes=[f"SM2{s}"])
            EMIT("dv", lambda e, s=s: e.scalar_tensor_tensor(
                col(S_VAR + s), col(S_SS + s), 1.0 / 1024.0, col(S_M2 + s),
                OP.mult, OP.subtract),
                reads=[f"SSS{s}", f"SM2{s}"], writes=[f"SVAR{s}"])
            EMIT("ac", lambda e, s=s: e.activation(
                col(S_STD + s), col(S_VAR + s), AF.Sqrt, bias=col(S_EPS)),
                reads=[f"SVAR{s}", "SEPS"], writes=[f"SSTD{s}"])
            EMIT("dv", lambda e, s=s: e.reciprocal(col(S_RSTD + s), col(S_STD + s)),
                 reads=[f"SSTD{s}"], writes=[f"SRSTD{s}"], inc=0)
            EMIT("dv", lambda e: e.drain(), inc=0)
            EMIT("dv", lambda e, t=t, s=s: e.tensor_scalar(
                Xt(t), Xt(t), col(S_NMEAN + s), col(S_RSTD + s), OP.add, OP.mult),
                reads=[f"SNM{s}", f"SRSTD{s}"], writes=[f"X{t}"])
            if rebuild:
                xt_build(t)

        for t in range(NT):
            xt_build(t)

        _dummy = [None]
        VTOK = RB
        QTA, QTB, KTA, KTB = RB + 16384, RB + 18432, RB + 20480, RB + 22528
        WVS = RB + 24576
        WQS = RB + 28672
        WKS = RB + 30720
        PTP = [RB + 24576, RB + 26624]
        PTOK = RB + 28672
        OSB = RB + 30720

        for dvh in range(2 if KPH >= 1 else 0):
            for k in range(ND):
                EMIT("gp", lambda e, k=k, dvh=dvh: e.dma_start(
                    ab(WVS + k * 512, 512),
                    wv_d[k * 128:(k + 1) * 128, dvh * 512:(dvh + 1) * 512]),
                    writes=(["WVS"] if k == 0 else []), inc=16)
            MARK("gp", "WVS")
            for nt in range(NT):
                pb = f"p{nt % 2}"
                for k in range(ND):
                    EMIT("pe", lambda e, k=k, nt=nt: e.matmul(
                        P[nt % 2][:, :], XTk(k, nt * 128, 128), ab(WVS + k * 512, 512),
                        start=(k == 0), stop=False),
                        reads=["WVS"] + xtr(nt * 128, 128), writes=[pb])
                EMIT("pe", lambda e, nt=nt, dvh=dvh: e.matmul(
                    P[nt % 2][:, :], ab(C_ONE, 128, 0, 1),
                    ab(C_BV + dvh * 512, 512, 0, 1), start=False, stop=True),
                    reads=["CONE", "CBV"], writes=[pb])
                EMIT("ac", lambda e, nt=nt, dvh=dvh: e.activation(
                    ab(VTOK + nt * 1024 + dvh * 512, 512), P[nt % 2][:, :], AF.Copy),
                    reads=[pb], writes=[f"VT{nt}_{dvh}"])

        for k in range(ND if KPH >= 1 else 0):
            EMIT("gp", lambda e, k=k: e.dma_start(
                ab(WQS + k * 256, 256), wq_d[k * 128:(k + 1) * 128, :]),
                writes=(["WQS"] if k == 0 else []), inc=16)
        MARK("gp", "WQS")
        for k in range(ND if KPH >= 1 else 0):
            EMIT("gp", lambda e, k=k: e.dma_start(
                ab(WKS + k * 256, 256), wk_d[k * 128:(k + 1) * 128, :]),
                writes=(["WKS"] if k == 0 else []), inc=16)
        MARK("gp", "WKS")
        EMIT("gp", lambda e: e.dma_start(wfcb_sp, wfc_d), writes=["WFCB"], inc=16)
        EMIT("gp", lambda e: e.dma_start(wpjb_sp, wpj_d), writes=["WPJB"], inc=16)
        for (dst, wsrc, wnm, g, bq, nm) in (() if KPH < 1 else (
                (QTA, WQS, "WQS", 0, S_BQA, "QTA"), (QTB, WQS, "WQS", 1, S_BQB, "QTB"),
                (KTA, WKS, "WKS", 0, S_BKA, "KTA"), (KTB, WKS, "WKS", 1, S_BKB, "KTB"))):
            for mc in range(4):
                pb = f"p{mc % 2}"
                for k in range(ND):
                    EMIT("pe", lambda e, k=k, mc=mc, wsrc=wsrc, g=g: e.matmul(
                        P[mc % 2][:, :], ab(wsrc + k * 256 + g * 128, 128),
                        XTk(k, mc * 512, 512), start=(k == 0), stop=(k == ND - 1)),
                        reads=[wnm] + xtr(mc * 512, 512), writes=[pb])
                EMIT("dv", lambda e, mc=mc, dst=dst, bq=bq: e.tensor_scalar(
                    ab(dst + mc * 512, 512), P[mc % 2][:, :], col(bq), None, OP.add),
                    reads=[pb, "SBQA", "SBQB", "SBKA", "SBKB"], writes=[nm])

        barrier()

        units = [(h, mt) for h in range(8 if KPH >= 2 else 0)
                 for mt in range(NT)]

        def l0a_scores(u):
            h, mt = units[u]
            g = h // 4
            base = (h % 4) * 32
            qsrc = QTA if g == 0 else QTB
            ksrc = KTA if g == 0 else KTB
            qnm = "QTA" if g == 0 else "QTB"
            knm = "KTA" if g == 0 else "KTB"
            for c in range(4):
                pb = f"p{c % 2}"
                tp = (96, 0) if base == 96 else None
                EMIT("pe", lambda e, c=c, mt=mt, base=base, qsrc=qsrc, ksrc=ksrc,
                     tp=tp: e.matmul(
                         P[c % 2][:, :],
                         Ab[base:base + 32, qsrc + mt * 128:qsrc + mt * 128 + 128],
                         Ab[base:base + 32, ksrc + c * 512:ksrc + c * 512 + 512],
                         start=True, stop=True, tile_position=tp),
                     reads=[qnm, knm], writes=[pb])
                sp = mt % 2
                EMIT("ac", lambda e, c=c, sp=sp: e.activation(
                    af(FS + sp * 2048 + c * 512, 512), P[c % 2][:, :], AF.Prelu,
                    scale=0.25, alpha=0.2),
                    reads=[pb], writes=[f"SLK{sp}"])

        def l0a_tail(u):
            h, mt = units[u]
            s = mt % 2
            EMIT("dv", lambda e, s=s: e.reduce_max(
                col(S_NEGMAX + s), af(FS + s * 2048, 2048), axis=AX, negate=True,
                opt_output=False),
                reads=[f"SLK{s}"], writes=[f"SNX{s}"], inc=0)
            EMIT("dv", lambda e: e.drain(), inc=0)
            EMIT("dv", lambda e: e.memset(col(S_JUNK), 0.0), writes=[f"SNX{s}"])
            EMIT("ac", lambda e, s=s: e.activation(
                ab(PTOK, 2048), af(FS + s * 2048, 2048), AF.Exp,
                bias=col(S_NEGMAX + s), scale=1.0, accum_out=col(S_R + s)),
                reads=[f"SLK{s}", f"SNX{s}"], writes=["PTOK", f"SR{s}"])
            EMIT("dv", lambda e, s=s: e.reciprocal(col(S_RREC + s), col(S_R + s)),
                 reads=[f"SR{s}"], writes=[f"SRC{s}"], inc=0)
            EMIT("dv", lambda e, s=s: e.tensor_tensor(
                col(S_GREC + s), col(S_RREC + s), col(S_GB), OP.mult),
                reads=["SGB", f"SRC{s}"], writes=[f"SGR{s}"])

        def l0a_back(u):
            h, mt = units[u]
            s = mt % 2
            for bk in range(2):
                pt = f"pt{bk}"
                for i in range(8):
                    EMIT("pe", lambda e, bk=bk, i=i: e.transpose(
                        PT[bk][:, i * 128:(i + 1) * 128],
                        ab(PTOK + (bk * 8 + i) * 128, 128), ab(C_ID, 128)),
                        reads=["PTOK", "CID"], writes=[pt])
            for bk in range(2):
                EMIT("dv", lambda e, bk=bk, s=s: e.tensor_copy(
                    ab(PTP[s] + bk * 1024, 1024), PT[bk][:, :]),
                    reads=[f"pt{bk}"], writes=[f"PTP{s}"])
            for nt in range(NT):
                EMIT("pe", lambda e, nt=nt, h=h, s=s: e.matmul(
                    P[2][:, s * 128:s * 128 + 128],
                    ab(VTOK + nt * 1024 + h * 128, 128),
                    ab(PTP[s] + nt * 128, 128),
                    start=(nt == 0), stop=(nt == NT - 1)),
                    reads=[f"PTP{s}", f"VT{nt}_0", f"VT{nt}_1"], writes=["p2"])
            EMIT("dv", lambda e, s=s: e.tensor_copy(
                ab(OSB + s * 128, 128), P[2][:, s * 128:s * 128 + 128]),
                reads=["p2"], writes=[f"OSB{s}"])
            EMIT("pe", lambda e, s=s: e.transpose(
                PT[0][:, 0:128], ab(OSB + s * 128, 128), ab(C_ID, 128)),
                reads=[f"OSB{s}", "CID"], writes=["pt0"])
            EMIT("dv", lambda e, mt=mt, h=h, s=s: e.scalar_tensor_tensor(
                Xt(mt, h * 128, 128), PT[0][:, 0:128], col(S_GREC + s),
                Xt(mt, h * 128, 128), OP.mult, OP.add),
                reads=["pt0", f"SGR{s}", f"X{mt}"], writes=[f"X{mt}"])

        for u in range(len(units)):
            l0a_scores(u)
            if u >= 1:
                l0a_back(u - 1)
            l0a_tail(u)
        if units:
            l0a_back(len(units) - 1)

        barrier()
        for t in range(NT if KPH >= 3 else 0):
            layernorm(t, t % 2)

        barrier()
        HT = RB
        WFS = [RB + 16384, RB + 20480]
        WPS = [RB + 24576, RB + 25088]
        TMB = [RB + 25600, RB + 26112]

        for mcq in range(4 if KPH >= 4 else 0):
            for fch in range(32):
                buf = (fch // 4) % 2
                if fch % 4 == 0:
                    for k in range(ND):
                        EMIT("gp", lambda e, k=k, fch=fch, buf=buf: e.dma_start(
                            ab(WFS[buf] + k * 512, 512),
                            wfcb_sp[k * 128:(k + 1) * 128,
                                    fch * 128:fch * 128 + 512]),
                            reads=["WFCB"],
                            writes=([f"WFS{buf}"] if k == 0 else []), inc=16)
                    MARK("gp", f"WFS{buf}")
                pb = f"p{fch % 2}"
                for k in range(ND):
                    EMIT("pe", lambda e, k=k, fch=fch, mcq=mcq, buf=buf: e.matmul(
                        P[fch % 2][:, :],
                        ab(WFS[buf] + k * 512 + (fch % 4) * 128, 128),
                        XTk(k, mcq * 512, 512), start=(k == 0), stop=(k == ND - 1)),
                        reads=[f"WFS{buf}"] + xtr(mcq * 512, 512), writes=[pb])
                tb = fch % 2
                EMIT("dv", lambda e, fch=fch, tb=tb: e.tensor_scalar(
                    ab(TMB[tb], 512), P[fch % 2][:, :], 0.0, None, OP.max),
                    reads=[pb], writes=[f"TMB{tb}"])
                EMIT("ac", lambda e, fch=fch, tb=tb: e.activation(
                    ab(HT + fch * 512, 512), ab(TMB[tb], 512), AF.Square),
                    reads=[f"TMB{tb}"], writes=[f"HT{fch}"])
            for dc in range(2):
                for ft in range(32):
                    buf = ft % 2
                    EMIT("sy", lambda e, ft=ft, dc=dc, buf=buf: e.dma_start(
                        ab(WPS[buf], 512),
                        wpjb_sp[ft * 128:(ft + 1) * 128, dc * 512:(dc + 1) * 512]),
                        reads=["WPJB"], writes=[f"WPS{buf}"], inc=16)
                    for mt4 in range(4):
                        EMIT("pe", lambda e, ft=ft, mt4=mt4, buf=buf: e.matmul(
                            P[2 + mt4][:, :], ab(HT + ft * 512 + mt4 * 128, 128),
                            ab(WPS[buf], 512), start=(ft == 0), stop=(ft == 31)),
                            reads=[f"WPS{buf}", f"HT{ft}"], writes=[f"p{2 + mt4}"])
                for mt4 in range(4):
                    mt = mcq * 4 + mt4
                    pb = f"p{2 + mt4}"
                    a_, b_ = af(FS, 512), af(FS + 512, 512)
                    EMIT("dv", lambda e, mt4=mt4: e.tensor_scalar(
                        af(FS, 512), P[2 + mt4][:, :], TH[0], None, OP.is_gt),
                        reads=[pb], writes=["ACCA"])
                    for i, (src, dst, rs_, ws) in enumerate(
                            ((a_, b_, "ACCA", "ACCB"), (b_, a_, "ACCB", "ACCA"),
                             (a_, b_, "ACCA", "ACCB"), (b_, a_, "ACCB", "ACCA"))):
                        EMIT("dv", lambda e, mt4=mt4, i=i, src=src, dst=dst:
                             e.scalar_tensor_tensor(
                                 dst, P[2 + mt4][:, :], TH[i + 1], src,
                                 OP.is_gt, OP.add),
                             reads=[pb, rs_], writes=[ws])
                    EMIT("dv", lambda e, mt=mt, dc=dc: e.scalar_tensor_tensor(
                        Xt(mt, dc * 512, 512), af(FS, 512), 0.2,
                        Xt(mt, dc * 512, 512), OP.mult, OP.add),
                        reads=["ACCA", f"X{mt}"], writes=[f"X{mt}"])
            for mt4 in range(4):
                mt = mcq * 4 + mt4
                layernorm(mt, mt % 2)

        barrier()
        LATQ = [RB, RB + 2048]
        LATK = [RB + 4096, RB + 6144]
        LATV = [RB + 8192, RB + 10240]
        QTJ = [RB + 12288, RB + 14336]
        KTJ = [RB + 16384, RB + 18432]
        WDQ = [RB + 20480, RB + 21504]
        WDK = [RB + 22528, RB + 23552]
        WDV = [RB + 24576, RB + 25600]
        RPC = RB + 26624
        RPS = RB + 28672
        TP1 = RB + 30720
        TP2 = RB + 31232

        EMIT("gp", lambda e: e.dma_start(ab(RPC, 2048), rc_d), writes=["RPC"], inc=16)
        EMIT("gp", lambda e: e.dma_start(ab(RPS, 2048), rs_d), writes=["RPS"], inc=16)

        for j in range(8 if KPH >= 5 else 0):
            par = j % 2
            for (wd, dram, nm) in ((WDQ, wqd_d, "WDQ"), (WDK, wkd_d, "WDK"),
                                   (WDV, wvd_d, "WDV")):
                for k in range(ND):
                    EMIT("gp", lambda e, k=k, j=j, wd=wd, dram=dram, par=par:
                         e.dma_start(ab(wd[par] + k * 128, 128),
                                     dram[k * 128:(k + 1) * 128, j * 128:(j + 1) * 128]),
                         writes=([f"{nm}{par}"] if k == 0 else []), inc=16)
                MARK("gp", f"{nm}{par}")
            for (wd, lat, nm, lnm) in ((WDQ, LATQ, "WDQ", "LATQ"),
                                       (WDK, LATK, "WDK", "LATK"),
                                       (WDV, LATV, "WDV", "LATV")):
                for mc in range(4):
                    pb = f"p{mc % 2}"
                    for k in range(ND):
                        EMIT("pe", lambda e, k=k, mc=mc, wd=wd, par=par: e.matmul(
                            P[mc % 2][:, :], ab(wd[par] + k * 128, 128),
                            XTk(k, mc * 512, 512), start=(k == 0), stop=(k == ND - 1)),
                            reads=[f"{nm}{par}"] + xtr(mc * 512, 512), writes=[pb])
                    EMIT("ac", lambda e, mc=mc, lat=lat, par=par: e.activation(
                        ab(lat[par] + mc * 512, 512), P[mc % 2][:, :], AF.Copy),
                        reads=[pb], writes=[f"{lnm}{par}"])
            EMIT("sy", lambda e, j=j, par=par: e.dma_start(
                lv_sp[j * 128:(j + 1) * 128, :], ab(LATV[par], 2048)),
                reads=[f"LATV{par}"], inc=16)
            for (lat, lnm, upo, unm, upso, usnm, qt, qnm, spill) in (
                    (LATQ, "LATQ", C_UPQ, "CUPQ", C_UPQS, "CUPQS", QTJ, "QTJ", qr_sp),
                    (LATK, "LATK", C_UPK, "CUPK", C_UPKS, "CUPKS", KTJ, "KTJ", kr_sp)):
                for mc in range(4):
                    EMIT("pe", lambda e, mc=mc, lat=lat, upo=upo, par=par: e.matmul(
                        P[2][:, :], ab(upo, 128), ab(lat[par] + mc * 512, 512),
                        start=True, stop=True),
                        reads=[f"{lnm}{par}", unm], writes=["p2"])
                    EMIT("pe", lambda e, mc=mc, lat=lat, upso=upso, par=par: e.matmul(
                        P[3][:, :], ab(upso, 128), ab(lat[par] + mc * 512, 512),
                        start=True, stop=True),
                        reads=[f"{lnm}{par}", usnm], writes=["p3"])
                    EMIT("dv", lambda e, mc=mc: e.tensor_tensor(
                        ab(TP1, 512), P[2][:, :], ab(RPC + mc * 512, 512), OP.mult),
                        reads=["p2", "RPC"], writes=["TP1"])
                    EMIT("dv", lambda e, mc=mc: e.tensor_tensor(
                        ab(TP2, 512), P[3][:, :], ab(RPS + mc * 512, 512), OP.mult),
                        reads=["p3", "RPS"], writes=["TP2"])
                    EMIT("dv", lambda e, mc=mc, qt=qt, par=par: e.tensor_tensor(
                        ab(qt[par] + mc * 512, 512), ab(TP1, 512), ab(TP2, 512),
                        OP.add),
                        reads=["TP1", "TP2"], writes=[f"{qnm}{par}"])
                EMIT("sy", lambda e, j=j, qt=qt, par=par, spill=spill: e.dma_start(
                    spill[j * 128:(j + 1) * 128, :], ab(qt[par], 2048)),
                    reads=[f"{qnm}{par}"], inc=16)

        barrier()
        KTH = [XTO, XTO + 2048]
        QTH = [XTO + 4096, XTO + 6144]
        LVH = [XTO + 8192, XTO + 10240]
        PTL = XTO + 12288
        YT = RB
        VH = [RB + 16384, RB + 17424]
        PTMP = [RB + 18464, RB + 18720]
        YSTG = RB + 18976
        WC = RB + 19232
        RB1 = RB + 27424
        RB2 = RB + 27680

        for k in range(ND if KPH >= 6 else 0):
            EMIT("gp", lambda e, k=k: e.dma_start(
                ab(WC + k * 1024, 1024), wc_d[k * 128:(k + 1) * 128, :]),
                writes=(["WC"] if k == 0 else []), inc=16)
        MARK("gp", "WC")
        for par in range(2 if KPH >= 6 else 0):
            for nt in range(NT):
                EMIT("dv", lambda e, par=par, nt=nt: e.memset(
                    ab(VH[par] + nt * 65 + 64, 1), 1.0), writes=[f"VHO{par}"])

        units2 = []
        for h in range(16 if KPH >= 6 else 0):
            for mcq in range(8):
                units2.append((h, mcq))

        def l1_front(i):
            h, mcq = units2[i]
            par = h % 2
            avp = f"p{2 + i % 2}"
            if mcq == 0:
                EMIT("sy", lambda e, h=h, par=par: e.dma_start(
                    ab(LVH[par], 2048, 0, 64), lv_sp[h * 64:(h + 1) * 64, :]),
                    writes=[f"LVH{par}"], inc=16)
                EMIT("sy", lambda e, h=h, par=par: e.dma_start(
                    ab(KTH[par], 2048, 0, 64), kr_sp[h * 64:(h + 1) * 64, :]),
                    writes=[f"KTH{par}"], inc=16)
                EMIT("sy", lambda e, h=h, par=par: e.dma_start(
                    ab(QTH[par], 2048, 0, 64), qr_sp[h * 64:(h + 1) * 64, :]),
                    writes=[f"QTH{par}"], inc=16)
                for nt in range(NT):
                    pb = f"p{nt % 2}"
                    EMIT("pe", lambda e, nt=nt, par=par: e.matmul(
                        P[nt % 2][:, 0:64], ab(LVH[par] + nt * 128, 128, 0, 64),
                        ab(C_UPV1, 64, 0, 64), start=True, stop=True),
                        reads=[f"LVH{par}", "CUPV"], writes=[pb])
                    EMIT("dv", lambda e, nt=nt, par=par: e.tensor_copy(
                        ab(VH[par] + nt * 65, 64), P[nt % 2][:, 0:64]),
                        reads=[pb, f"VHO{par}"], writes=[f"VH{par}"])
            maxnt = 2 * mcq + 2
            for nt in range(maxnt):
                pb = f"p{nt % 2}"
                EMIT("pe", lambda e, nt=nt, mcq=mcq, par=par: e.matmul(
                    P[nt % 2][:, 0:256], ab(KTH[par] + nt * 128, 128, 0, 64),
                    ab(QTH[par] + mcq * 256, 256, 0, 64),
                    start=True, stop=True),
                    reads=[f"KTH{par}", f"QTH{par}"], writes=[pb])
                if nt < maxnt - 2:
                    EMIT("ac", lambda e, nt=nt: e.activation(
                        ab(PTL + nt * 256, 256), P[nt % 2][:, 0:256], AF.Exp,
                        scale=0.125),
                        reads=[pb], writes=[f"PTL{nt}"])
                else:
                    v = nt - 2 * mcq
                    pp = nt % 2
                    EMIT("ac", lambda e, pp=pp, nt=nt: e.activation(
                        ab(PTMP[pp], 256), P[nt % 2][:, 0:256], AF.Exp,
                        scale=0.125),
                        reads=[pb], writes=[f"PTMP{pp}"])
                    EMIT("dv", lambda e, nt=nt, v=v, pp=pp: e.tensor_tensor(
                        ab(PTL + nt * 256, 256), ab(PTMP[pp], 256),
                        ab(C_M0 if v == 0 else C_M1, 256), OP.mult),
                        reads=[f"PTMP{pp}", "CM0", "CM1"], writes=[f"PTL{nt}"])
            for nt in range(maxnt):
                EMIT("pe", lambda e, nt=nt, par=par, maxnt=maxnt, i=i: e.matmul(
                    P[2 + i % 2][0:65, 0:256], ab(VH[par] + nt * 65, 65),
                    ab(PTL + nt * 256, 256),
                    start=(nt == 0), stop=(nt == maxnt - 1)),
                    reads=[f"VH{par}", f"PTL{nt}"], writes=[avp])

        def l1_back(i):
            h, mcq = units2[i]
            avp = f"p{2 + i % 2}"
            AVP = P[2 + i % 2]
            EMIT("dv", lambda e, AVP=AVP: e.tensor_copy(
                af(FS + 768, 256, 0, 1), AVP[64:65, 0:256]),
                reads=[avp], writes=["RRAW"])
            for half in range(2):
                EMIT("pe", lambda e, half=half: e.transpose(
                    P[4][:, half:half + 1], af(FS + 768 + half * 128, 128, 0, 1),
                    af(S_IDF, 1, 0, 1)),
                    reads=["RRAW", "SIDF"], writes=["p4"])
            EMIT("dv", lambda e: e.reciprocal(af(S_RCL, 2), P[4][:, 0:2]),
                 reads=["p4"], writes=["SRCL"])
            for half in range(2):
                EMIT("pe", lambda e, half=half: e.transpose(
                    P[4][0:1, 4 + half * 128:132 + half * 128],
                    af(S_RCL + half, 1), af(S_IDF, 128)),
                    reads=["SRCL", "SIDF"], writes=["p4b"])
            EMIT("dv", lambda e: e.tensor_copy(ab(RB1, 256, 0, 1),
                                               P[4][0:1, 4:260]),
                 reads=["p4b"], writes=["RB1"])
            EMIT("dv", lambda e: e.tensor_tensor(
                af(FS + 512, 256, 0, 1), P[4][0:1, 4:260],
                ab(RB1, 256, 0, 1), OP.subtract),
                reads=["p4b", "RB1"], writes=["RRES"])
            EMIT("dv", lambda e: e.tensor_copy(ab(RB2, 256, 0, 1),
                                               af(FS + 512, 256, 0, 1)),
                 reads=["RRES"], writes=["RB2"])
            EMIT("pe", lambda e: e.matmul(
                P[5][0:64, 0:256], ab(C_ONE, 64, 0, 1), ab(RB1, 256, 0, 1),
                start=True, stop=False),
                reads=["CONE", "RB1"], writes=["p5"])
            EMIT("pe", lambda e: e.matmul(
                P[5][0:64, 0:256], ab(C_ONE, 64, 0, 1), ab(RB2, 256, 0, 1),
                start=False, stop=True),
                reads=["CONE", "RB2"], writes=["p5"])
            EMIT("dv", lambda e: e.tensor_copy(af(FS, 256, 0, 64),
                                               P[5][0:64, 0:256]),
                 reads=["p5"], writes=["BCF"])
            pt_ = h // 2
            if h % 2 == 0:
                EMIT("dv", lambda e, pt_=pt_, mcq=mcq, AVP=AVP: e.tensor_tensor(
                    ab(YT + pt_ * 2048 + mcq * 256, 256, 0, 64),
                    AVP[0:64, 0:256], af(FS, 256, 0, 64), OP.mult),
                    reads=[avp, "BCF"], writes=[f"YT{pt_}_{mcq}"])
            else:
                EMIT("dv", lambda e, AVP=AVP: e.tensor_tensor(
                    ab(YSTG, 256, 0, 64), AVP[0:64, 0:256], af(FS, 256, 0, 64),
                    OP.mult),
                    reads=[avp, "BCF"], writes=["YSTG"])
                EMIT("sy", lambda e, pt_=pt_, mcq=mcq: e.dma_start(
                    Ab[64:128, YT + pt_ * 2048 + mcq * 256:
                       YT + pt_ * 2048 + mcq * 256 + 256],
                    ab(YSTG, 256, 0, 64)),
                    reads=["YSTG"], writes=[f"YT{pt_}_{mcq}"], inc=16)

        for i in range(len(units2)):
            l1_front(i)
            if i >= 1:
                l1_back(i - 1)
        if units2:
            l1_back(len(units2) - 1)

        barrier()
        for t in range(NT if KPH >= 7 else 0):
            readies = [f"YT{p_}_{t // 2}" for p_ in range(ND)]
            for dc in range(2):
                pb = f"p{4 + dc}"
                for k in range(ND):
                    EMIT("pe", lambda e, k=k, t=t, dc=dc: e.matmul(
                        P[4 + dc][:, :], ab(YT + k * 2048 + t * 128, 128),
                        ab(WC + k * 1024 + dc * 512, 512),
                        start=(k == 0), stop=(k == ND - 1)),
                        reads=readies + ["WC"], writes=[pb])
                EMIT("dv", lambda e, t=t, dc=dc: e.tensor_tensor(
                    Xt(t, dc * 512, 512), P[4 + dc][:, :], Xt(t, dc * 512, 512),
                    OP.add),
                    reads=[pb, f"X{t}"], writes=[f"X{t}"])
            layernorm(t, t % 2)

        barrier()
        CST = RB
        SNT = RB + 8192
        for (src1, src2, dst, dnm, useb) in (() if KPH < 8 else ((th_d, tn_d, CST, "CST", True),
                                             (ph_d, pn_d, SNT, "SNT", False))):
            for k in range(ND):
                for cc in range(2):
                    EMIT("sy", lambda e, k=k, cc=cc, src1=src1: e.dma_start(
                        af(FS, 512),
                        src1[k * 128:(k + 1) * 128, cc * 512:(cc + 1) * 512]),
                        writes=["TS1"], inc=16)
                    EMIT("sy", lambda e, k=k, cc=cc, src2=src2: e.dma_start(
                        af(FS + 512, 512),
                        src2[k * 128:(k + 1) * 128, cc * 512:(cc + 1) * 512]),
                        writes=["TS2"], inc=16)
                    EMIT("dv", lambda e: e.tensor_tensor(
                        af(FS, 512), af(FS, 512), af(FS + 512, 512), OP.add),
                        reads=["TS1", "TS2"], writes=["TS1"])
                    if useb:
                        EMIT("ac", lambda e, k=k, cc=cc, dst=dst: e.activation(
                            ab(dst + k * 1024 + cc * 512, 512), af(FS, 512),
                            AF.Sin, bias=col(S_HPI)),
                            reads=["TS1", "SHPI"], writes=[dnm])
                    else:
                        EMIT("ac", lambda e, k=k, cc=cc, dst=dst: e.activation(
                            ab(dst + k * 1024 + cc * 512, 512), af(FS, 512),
                            AF.Sin),
                            reads=["TS1"], writes=[dnm])
        for t in range(NT):
            for dc in range(2 if KPH >= 8 else 0):
                for k in range(ND):
                    EMIT("pe", lambda e, k=k, t=t, dc=dc: e.matmul(
                        P[0][:, :], XTk(k, t * 128, 128),
                        ab(CST + k * 1024 + dc * 512, 512),
                        start=(k == 0), stop=(k == ND - 1)),
                        reads=["CST"] + xtr(t * 128, 128), writes=["p0"])
                for k in range(ND):
                    EMIT("pe", lambda e, k=k, t=t, dc=dc: e.matmul(
                        P[1][:, :], XTk(k, t * 128, 128),
                        ab(SNT + k * 1024 + dc * 512, 512),
                        start=(k == 0), stop=(k == ND - 1)),
                        reads=["SNT"] + xtr(t * 128, 128), writes=["p1"])
                EMIT("ac", lambda e: e.activation(af(FS, 512), P[0][:, :], AF.Square),
                     reads=["p0"], writes=["RE2"])
                EMIT("ac", lambda e: e.activation(af(FS + 512, 512), P[1][:, :],
                                                  AF.Square),
                     reads=["p1"], writes=["IM2"])
                EMIT("dv", lambda e: e.tensor_tensor(
                    af(FS, 512), af(FS, 512), af(FS + 512, 512), OP.add),
                    reads=["RE2", "IM2"], writes=["RE2"])
                EMIT("ac", lambda e: e.activation(af(FS + 512, 512), af(FS, 512),
                                                  AF.Sqrt),
                     reads=["RE2"], writes=["IM2"])
                EMIT("dv", lambda e, t=t, dc=dc: e.tensor_tensor(
                    Xt(t, dc * 512, 512), af(FS + 512, 512), Xt(t, dc * 512, 512),
                    OP.add),
                    reads=["IM2", f"X{t}"], writes=[f"X{t}"])
            if KPH >= 8:
                layernorm(t, t % 2, rebuild=False)
            EMIT("sy", lambda e, t=t: e.dma_start(
                out_d[t * 128:(t + 1) * 128, :], Xt(t)),
                reads=[f"X{t}"], inc=16, sem="st")

        ops["sy"].append(lambda e: e.wait_ge(sems["st"], 16 * NT))

        @block.tensor
        def _(t_):
            for fn in ops["pe"]:
                fn(t_)

        @block.scalar
        def _(s_):
            for fn in ops["ac"]:
                fn(s_)

        @block.vector
        def _(v_):
            for fn in ops["dv"]:
                fn(v_)

        @block.gpsimd
        def _(g_):
            for fn in ops["gp"]:
                fn(g_)

        @block.sync
        def _(s_):
            for fn in ops["sy"]:
                fn(s_)

    return nc


def _host_prep(inp):
    f32 = np.float32
    d = {}
    Wq = np.asarray(inp["L0_attn_Wq"], f32)
    Wk = np.asarray(inp["L0_attn_Wk"], f32)
    wq_t = np.zeros((1024, 256), f32)
    wk_t = np.zeros((1024, 256), f32)
    bq_pad = np.zeros((256, 1), f32)
    bk_pad = np.zeros((256, 1), f32)
    bq = np.asarray(inp["L0_attn_bq"], f32)
    bk = np.asarray(inp["L0_attn_bk"], f32)
    for h in range(8):
        g, hh = h // 4, h % 4
        c0 = g * 128 + hh * 32
        wq_t[:, c0:c0 + 16] = Wq[h * 16:(h + 1) * 16, :].T
        wk_t[:, c0:c0 + 16] = Wk[h * 16:(h + 1) * 16, :].T
        bq_pad[c0:c0 + 16, 0] = bq[h * 16:(h + 1) * 16]
        bk_pad[c0:c0 + 16, 0] = bk[h * 16:(h + 1) * 16]
    d["wq_t"], d["wk_t"] = wq_t, wk_t
    d["bq_pad"], d["bk_pad"] = bq_pad, bk_pad
    d["wv_t"] = np.ascontiguousarray(np.asarray(inp["L0_attn_Wv"], f32).T)
    d["bv_r"] = np.asarray(inp["L0_attn_bv"], f32).reshape(1, 1024)
    d["gam"] = np.broadcast_to(
        np.asarray(inp["L0_attn_gamma"], f32).reshape(1, 1), (128, 1)).copy()
    d["wfc_t"] = np.ascontiguousarray(np.asarray(inp["L0_ffn_W_fc"], f32).T)
    d["wpj_t"] = np.ascontiguousarray(np.asarray(inp["L0_ffn_W_proj"], f32).T)
    d["wqd"] = np.asarray(inp["L1_attn_Wq_down"], f32)
    d["wkd"] = np.asarray(inp["L1_attn_Wk_down"], f32)
    d["wvd"] = np.asarray(inp["L1_attn_Wv_down"], f32)

    def upblock(c, e):
        U = np.concatenate([c, e], axis=1)
        out = np.zeros((128, 128), f32)
        out[0:64, 0:64] = U
        out[64:128, 64:128] = U
        return out

    def upswap(e):
        perm = np.arange(32)
        perm = perm + 1 - 2 * (perm % 2)
        Us = np.zeros((64, 64), f32)
        Us[:, 32:64] = e[:, perm]
        out = np.zeros((128, 128), f32)
        out[0:64, 0:64] = Us
        out[64:128, 64:128] = Us
        return out

    qc = np.asarray(inp["L1_attn_Wq_up_c"], f32)
    qe = np.asarray(inp["L1_attn_Wq_up_e"], f32)
    kc = np.asarray(inp["L1_attn_Wk_up_c"], f32)
    ke = np.asarray(inp["L1_attn_Wk_up_e"], f32)
    d["upq"] = upblock(qc, qe)
    d["upqs"] = upswap(qe)
    d["upk"] = upblock(kc, ke)
    d["upks"] = upswap(ke)
    d["wvu"] = np.asarray(inp["L1_attn_Wv_up"], f32)
    d["wc"] = np.asarray(inp["L1_attn_Wc"], f32)
    d["th"] = np.asarray(inp["L1_ffn_theta"], f32)
    d["tn"] = np.asarray(inp["L1_ffn_theta_noise"], f32)
    d["ph"] = np.asarray(inp["L1_ffn_phi"], f32)
    d["pn"] = np.asarray(inp["L1_ffn_phi_noise"], f32)
    d["identM"] = np.eye(128, dtype=f32)
    tpos = np.arange(T, dtype=np.float64)
    invf = 10000.0 ** (-np.arange(16, dtype=np.float64) / 16.0)
    ang = tpos[None, :] * invf[:, None]
    cosv = np.cos(ang)
    sinv = np.sin(ang)
    rc = np.zeros((128, T), np.float64)
    rs = np.zeros((128, T), np.float64)
    for r in range(128):
        j = r % 64
        if j < 32:
            rc[r] = 1.0
        else:
            i = j - 32
            rc[r] = cosv[i % 16]
            rs[r] = (-1.0 if i % 2 == 0 else 1.0) * sinv[i % 16]
    d["ropec"] = rc.astype(f32)
    d["ropes"] = rs.astype(f32)
    m = np.zeros((2, 128, 256), f32)
    for v in range(2):
        for nl in range(128):
            m[v, nl, :] = (128 * v + nl <= np.arange(256)).astype(f32)
    d["mask0"], d["mask1"] = m[0], m[1]
    return d


def kernel(**inputs):
    shared = _host_prep(inputs)
    x = np.ascontiguousarray(np.asarray(inputs["L0_attn_x"], np.float32))
    nc = build_nc()
    in_maps = [dict(shared, x=np.ascontiguousarray(x[i])) for i in range(8)]
    res = run_bass_kernel_spmd(nc, in_maps, core_ids=list(range(8)))
    out = np.stack([np.asarray(res.results[i]["out"], np.float32)
                    for i in range(8)])
    return out


# version 06
# speedup vs baseline: 1.1639x; 1.0702x over previous
import math
import os
import numpy as np
import concourse.bass as bass
import concourse.mybir as mybir
from concourse.bass_utils import run_bass_kernel_spmd
from concourse.alu_op_type import AluOpType
from contextlib import ExitStack

F32 = mybir.dt.float32
BF16 = mybir.dt.bfloat16
AF = mybir.ActivationFunctionType
AX = mybir.AxisListType.X
OP = AluOpType

T, D, NT, ND = 2048, 1024, 16, 8
KPH = int(os.environ.get("KPH", "99"))

TH = [0.24419428096993973, 0.4639691338428854, 0.6861859295255306,
      0.9059607823984763, 1.0839784132255623]

AF_W = 21504
XO = 0
FS = 16384
ST = 20480

AB_W = 54272
XTO = 0
RB = 16384
CB = 50688
C_ID = CB
C_ONE = CB + 128
C_BV = CB + 256
C_M0 = CB + 1280
C_M1 = CB + 1536
C_UPQ = CB + 1792
C_UPQS = CB + 1920
C_UPK = CB + 2048
C_UPKS = CB + 2176
C_UPV1 = CB + 2304
XBF = 53248

S_NEGMAX = ST + 0
S_R = ST + 2
S_RREC = ST + 4
S_GREC = ST + 6
S_SUM = ST + 8
S_SS = ST + 10
S_NMEAN = ST + 12
S_M2 = ST + 14
S_VAR = ST + 16
S_STD = ST + 18
S_RSTD = ST + 20
S_EPS = ST + 22
S_JUNK = ST + 23
S_GB = ST + 24
S_HPI = ST + 25
S_BQA = ST + 26
S_BQB = ST + 27
S_BKA = ST + 28
S_BKB = ST + 29
S_IDF = ST + 512
S_RCL = ST + 640


def build_nc():
    nc = bass.Bass("TRN2", target_bir_lowering=False)
    dr = {}

    def din(name, shape, dt=F32):
        dr[name] = nc.dram_tensor(name, shape, dt, kind="ExternalInput").ap()
        return dr[name]

    x_d = din("x", [T, D])
    wq_d = din("wq_t", [D, 256])
    bq_d = din("bq_pad", [256, 1])
    wk_d = din("wk_t", [D, 256])
    bk_d = din("bk_pad", [256, 1])
    wv_d = din("wv_t", [D, D])
    bv_d = din("bv_r", [1, D])
    gam_d = din("gam", [128, 1])
    wfc_d = din("wfc_t", [D, 4096])
    wpj_d = din("wpj_t", [4096, D])
    wqd_d = din("wqd", [D, D])
    wkd_d = din("wkd", [D, D])
    wvd_d = din("wvd", [D, D])
    upq_d = din("upq", [128, 128])
    upqs_d = din("upqs", [128, 128])
    upk_d = din("upk", [128, 128])
    upks_d = din("upks", [128, 128])
    wvu_d = din("wvu", [64, 64])
    wc_d = din("wc", [D, D])
    th_d = din("th", [D, D])
    tn_d = din("tn", [D, D])
    ph_d = din("ph", [D, D])
    pn_d = din("pn", [D, D])
    id_d = din("identM", [128, 128])
    rc_d = din("ropec", [128, T])
    rs_d = din("ropes", [128, T])
    m0_d = din("mask0", [128, 256])
    m1_d = din("mask1", [128, 256])
    qr_sp = nc.dram_tensor("qr_sp", [1024, T], BF16, kind="Internal").ap()
    kr_sp = nc.dram_tensor("kr_sp", [1024, T], BF16, kind="Internal").ap()
    lv_sp = nc.dram_tensor("lv_sp", [1024, T], BF16, kind="Internal").ap()
    wfcb_sp = nc.dram_tensor("wfcb_sp", [D, 4096], BF16, kind="Internal").ap()
    wpjb_sp = nc.dram_tensor("wpjb_sp", [4096, D], BF16, kind="Internal").ap()
    out_d = nc.dram_tensor("out", [T, D], F32, kind="ExternalOutput").ap()

    with ExitStack() as stk:
        E = stk.enter_context
        Af = E(nc.sbuf_tensor("arena_f", [128, AF_W], F32))
        Ab = E(nc.sbuf_tensor("arena_b", [128, AB_W], BF16))
        P = [E(nc.psum_tensor(f"p{i}", [128, 512], F32)) for i in range(6)]
        PT = [E(nc.psum_tensor(f"pt{i}", [128, 1024], BF16)) for i in range(2)]
        sems = {k: E(nc.semaphore("sem_" + k))
                for k in ("pe", "ac", "dv", "gp", "sy", "st")}
        block = E(nc.Block())

        ops = {k: [] for k in ("pe", "ac", "dv", "gp", "sy")}
        cnt = {k: 0 for k in ops}
        waited = {}
        ready = {}
        readers = {}

        def W(eng, prod, val):
            if val <= 0:
                return
            key = (eng, prod)
            if waited.get(key, -1) >= val:
                return
            waited[key] = val
            s = sems[prod]
            ops[eng].append(lambda e, s=s, v=val: e.wait_ge(s, v))

        def need(res, eng):
            r = ready.get(res)
            if r and r[0] != eng:
                W(eng, r[0], r[1])

        def acquire(res, eng):
            r = ready.get(res)
            if r and r[0] != eng:
                W(eng, r[0], r[1])
            for e2, v in readers.get(res, {}).items():
                if e2 != eng:
                    W(eng, e2, v)

        def EMIT(eng, fn, reads=(), writes=(), inc=1, sem=None):
            for r in reads:
                need(r, eng)
            for w in writes:
                acquire(w, eng)
            cnt[eng] += inc
            if inc:
                s = sems[sem if sem else eng]
                ops[eng].append(lambda e, f=fn, s=s, n=inc: f(e).then_inc(s, n))
            else:
                ops[eng].append(fn)
            for r in reads:
                readers.setdefault(r, {})[eng] = cnt[eng]
            for w in writes:
                ready[w] = (eng, cnt[eng])
                readers[w] = {}

        def MARK(eng, *names):
            for nm in names:
                ready[nm] = (eng, cnt[eng])
                readers[nm] = {}

        def barrier():
            for c in ops:
                for p in ops:
                    if p != c:
                        W(c, p, cnt[p])

        def af(off, w, p0=0, pn=128):
            return Af[p0:p0 + pn, off:off + w]

        def ab(off, w, p0=0, pn=128):
            return Ab[p0:p0 + pn, off:off + w]

        def Xt(t, c0=0, w=1024):
            return Af[:, XO + t * 1024 + c0: XO + t * 1024 + c0 + w]

        def XTk(k, c0, w):
            return Ab[:, XTO + k * 2048 + c0: XTO + k * 2048 + c0 + w]

        def col(o, n=1, p0=0, pn=128):
            return Af[p0:p0 + pn, o:o + n]

        def xtr(c0, w):
            t0, t1 = c0 // 128, (c0 + w - 1) // 128
            return [f"XT{t}" for t in range(t0, t1 + 1)]

        for t in range(NT):
            EMIT("sy", lambda e, t=t: e.dma_start(Xt(t), x_d[t * 128:(t + 1) * 128, :]),
                 writes=[f"X{t}"], inc=16)
        for (off, w, d_, nm, p0, pn) in (
                (C_ID, 128, id_d, "CID", 0, 128), (C_BV, 1024, bv_d, "CBV", 0, 1),
                (C_M0, 256, m0_d, "CM0", 0, 128), (C_M1, 256, m1_d, "CM1", 0, 128),
                (C_UPQ, 128, upq_d, "CUPQ", 0, 128), (C_UPQS, 128, upqs_d, "CUPQS", 0, 128),
                (C_UPK, 128, upk_d, "CUPK", 0, 128), (C_UPKS, 128, upks_d, "CUPKS", 0, 128),
                (C_UPV1, 64, wvu_d, "CUPV", 0, 64)):
            EMIT("gp", lambda e, off=off, w=w, d_=d_, p0=p0, pn=pn: e.dma_start(
                ab(off, w, p0, pn), d_), writes=[nm], inc=16)
        for (o, d_, nm) in ((S_GB, gam_d, "SGB"),):
            EMIT("sy", lambda e, o=o, d_=d_: e.dma_start(col(o), d_), writes=[nm], inc=16)
        EMIT("sy", lambda e: e.dma_start(af(S_IDF, 128), id_d), writes=["SIDF"], inc=16)
        for (o, d_, sl, nm) in ((S_BQA, bq_d, 0, "SBQA"), (S_BQB, bq_d, 1, "SBQB"),
                                (S_BKA, bk_d, 0, "SBKA"), (S_BKB, bk_d, 1, "SBKB")):
            EMIT("sy", lambda e, o=o, d_=d_, sl=sl: e.dma_start(
                col(o), d_[sl * 128:(sl + 1) * 128, :]), writes=[nm], inc=16)
        EMIT("dv", lambda e: e.memset(col(S_EPS), 1e-5), writes=["SEPS"])
        EMIT("dv", lambda e: e.memset(col(S_HPI), float(np.pi / 2)), writes=["SHPI"])
        EMIT("dv", lambda e: e.memset(ab(C_ONE, 128, 0, 1), 1.0), writes=["CONE"])

        def xt_build(t):
            EMIT("ac", lambda e, t=t: e.activation(ab(XBF, 1024), Xt(t), AF.Copy),
                 reads=[f"X{t}"], writes=["XBF"])
            for f in range(ND):
                EMIT("pe", lambda e, f=f: e.transpose(
                    PT[0][:, f * 128:(f + 1) * 128], ab(XBF + f * 128, 128),
                    ab(C_ID, 128)),
                    reads=["XBF", "CID"], writes=["pt0"])
            for f in range(ND):
                EMIT("dv", lambda e, f=f, t=t: e.tensor_copy(
                    XTk(f, t * 128, 128), PT[0][:, f * 128:(f + 1) * 128]),
                    reads=["pt0"], writes=[f"XT{t}"])

        def layernorm(t, s, rebuild=True):
            sq = af(FS + 1024, 1024)
            EMIT("ac", lambda e, t=t, s=s, sq=sq: e.activation(
                sq, Xt(t), AF.Square, accum_out=col(S_SS + s)),
                reads=[f"X{t}"], writes=["LND", f"SSS{s}"])
            EMIT("ac", lambda e, t=t, s=s: e.activation(
                sq, Xt(t), AF.Copy, accum_out=col(S_SUM + s)),
                reads=[f"X{t}"], writes=["LND", f"SSUM{s}"])
            EMIT("dv", lambda e, s=s: e.tensor_scalar(
                col(S_NMEAN + s), col(S_SUM + s), -1.0 / 1024.0, None, OP.mult),
                reads=[f"SSUM{s}"], writes=[f"SNM{s}"])
            EMIT("ac", lambda e, s=s: e.activation(
                col(S_M2 + s), col(S_NMEAN + s), AF.Square),
                reads=[f"SNM{s}"], writes=[f"SM2{s}"])
            EMIT("dv", lambda e, s=s: e.scalar_tensor_tensor(
                col(S_VAR + s), col(S_SS + s), 1.0 / 1024.0, col(S_M2 + s),
                OP.mult, OP.subtract),
                reads=[f"SSS{s}", f"SM2{s}"], writes=[f"SVAR{s}"])
            EMIT("ac", lambda e, s=s: e.activation(
                col(S_STD + s), col(S_VAR + s), AF.Sqrt, bias=col(S_EPS)),
                reads=[f"SVAR{s}", "SEPS"], writes=[f"SSTD{s}"])
            EMIT("dv", lambda e, s=s: e.reciprocal(col(S_RSTD + s), col(S_STD + s)),
                 reads=[f"SSTD{s}"], writes=[f"SRSTD{s}"], inc=0)
            EMIT("dv", lambda e: e.drain(), inc=0)
            EMIT("dv", lambda e, t=t, s=s: e.tensor_scalar(
                Xt(t), Xt(t), col(S_NMEAN + s), col(S_RSTD + s), OP.add, OP.mult),
                reads=[f"SNM{s}", f"SRSTD{s}"], writes=[f"X{t}"])
            if rebuild:
                xt_build(t)

        for t in range(NT):
            xt_build(t)

        _dummy = [None]
        VTOK = RB
        QTA, QTB, KTA, KTB = RB + 16384, RB + 18432, RB + 20480, RB + 22528
        WVS = RB + 24576
        WQS = RB + 28672
        WKS = RB + 30720
        PTP = [RB + 24576, RB + 26624]
        PTOK = RB + 28672
        OSB = RB + 30720

        for dvh in range(2 if KPH >= 1 else 0):
            for k in range(ND):
                EMIT("gp", lambda e, k=k, dvh=dvh: e.dma_start(
                    ab(WVS + k * 512, 512),
                    wv_d[k * 128:(k + 1) * 128, dvh * 512:(dvh + 1) * 512]),
                    writes=(["WVS"] if k == 0 else []), inc=16)
            MARK("gp", "WVS")
            for nt in range(NT):
                pb = f"p{nt % 2}"
                for k in range(ND):
                    EMIT("pe", lambda e, k=k, nt=nt: e.matmul(
                        P[nt % 2][:, :], XTk(k, nt * 128, 128), ab(WVS + k * 512, 512),
                        start=(k == 0), stop=False),
                        reads=["WVS"] + xtr(nt * 128, 128), writes=[pb])
                EMIT("pe", lambda e, nt=nt, dvh=dvh: e.matmul(
                    P[nt % 2][:, :], ab(C_ONE, 128, 0, 1),
                    ab(C_BV + dvh * 512, 512, 0, 1), start=False, stop=True),
                    reads=["CONE", "CBV"], writes=[pb])
                EMIT("ac", lambda e, nt=nt, dvh=dvh: e.activation(
                    ab(VTOK + nt * 1024 + dvh * 512, 512), P[nt % 2][:, :], AF.Copy),
                    reads=[pb], writes=[f"VT{nt}_{dvh}"])

        for k in range(ND if KPH >= 1 else 0):
            EMIT("gp", lambda e, k=k: e.dma_start(
                ab(WQS + k * 256, 256), wq_d[k * 128:(k + 1) * 128, :]),
                writes=(["WQS"] if k == 0 else []), inc=16)
        MARK("gp", "WQS")
        for k in range(ND if KPH >= 1 else 0):
            EMIT("gp", lambda e, k=k: e.dma_start(
                ab(WKS + k * 256, 256), wk_d[k * 128:(k + 1) * 128, :]),
                writes=(["WKS"] if k == 0 else []), inc=16)
        MARK("gp", "WKS")
        EMIT("gp", lambda e: e.dma_start(wfcb_sp, wfc_d), writes=["WFCB"], inc=16)
        EMIT("gp", lambda e: e.dma_start(wpjb_sp, wpj_d), writes=["WPJB"], inc=16)
        for (dst, wsrc, wnm, g, bq, nm) in (() if KPH < 1 else (
                (QTA, WQS, "WQS", 0, S_BQA, "QTA"), (QTB, WQS, "WQS", 1, S_BQB, "QTB"),
                (KTA, WKS, "WKS", 0, S_BKA, "KTA"), (KTB, WKS, "WKS", 1, S_BKB, "KTB"))):
            for mc in range(4):
                pb = f"p{mc % 2}"
                for k in range(ND):
                    EMIT("pe", lambda e, k=k, mc=mc, wsrc=wsrc, g=g: e.matmul(
                        P[mc % 2][:, :], ab(wsrc + k * 256 + g * 128, 128),
                        XTk(k, mc * 512, 512), start=(k == 0), stop=(k == ND - 1)),
                        reads=[wnm] + xtr(mc * 512, 512), writes=[pb])
                EMIT("dv", lambda e, mc=mc, dst=dst, bq=bq: e.tensor_scalar(
                    ab(dst + mc * 512, 512), P[mc % 2][:, :], col(bq), None, OP.add),
                    reads=[pb, "SBQA", "SBQB", "SBKA", "SBKB"], writes=[nm])

        barrier()

        units = [(h, mt) for h in range(8 if KPH >= 2 else 0)
                 for mt in range(NT)]

        def l0a_scores(u):
            h, mt = units[u]
            g = h // 4
            base = (h % 4) * 32
            qsrc = QTA if g == 0 else QTB
            ksrc = KTA if g == 0 else KTB
            qnm = "QTA" if g == 0 else "QTB"
            knm = "KTA" if g == 0 else "KTB"
            for c in range(4):
                pb = f"p{c % 2}"
                tp = (96, 0) if base == 96 else None
                EMIT("pe", lambda e, c=c, mt=mt, base=base, qsrc=qsrc, ksrc=ksrc,
                     tp=tp: e.matmul(
                         P[c % 2][:, :],
                         Ab[base:base + 32, qsrc + mt * 128:qsrc + mt * 128 + 128],
                         Ab[base:base + 32, ksrc + c * 512:ksrc + c * 512 + 512],
                         start=True, stop=True, tile_position=tp),
                     reads=[qnm, knm], writes=[pb])
                sp = mt % 2
                EMIT("ac", lambda e, c=c, sp=sp: e.activation(
                    af(FS + sp * 2048 + c * 512, 512), P[c % 2][:, :], AF.Prelu,
                    scale=0.25, alpha=0.2),
                    reads=[pb], writes=[f"SLK{sp}"])

        def l0a_tail_a(u):
            h, mt = units[u]
            s = mt % 2
            EMIT("dv", lambda e, s=s: e.reduce_max(
                col(S_NEGMAX + s), af(FS + s * 2048, 2048), axis=AX, negate=True,
                opt_output=False),
                reads=[f"SLK{s}"], writes=[f"SNX{s}"], inc=0)
            EMIT("dv", lambda e: e.drain(), inc=0)
            EMIT("dv", lambda e: e.memset(col(S_JUNK), 0.0), writes=[f"SNX{s}"])

        def l0a_tail_b(u):
            h, mt = units[u]
            s = mt % 2
            EMIT("ac", lambda e, s=s: e.activation(
                ab(PTOK, 2048), af(FS + s * 2048, 2048), AF.Exp,
                bias=col(S_NEGMAX + s), scale=1.0, accum_out=col(S_R + s)),
                reads=[f"SLK{s}", f"SNX{s}"], writes=["PTOK", f"SR{s}"])
            EMIT("dv", lambda e, s=s: e.reciprocal(col(S_RREC + s), col(S_R + s)),
                 reads=[f"SR{s}"], writes=[f"SRC{s}"], inc=0)
            EMIT("dv", lambda e, s=s: e.tensor_tensor(
                col(S_GREC + s), col(S_RREC + s), col(S_GB), OP.mult),
                reads=["SGB", f"SRC{s}"], writes=[f"SGR{s}"])

        def l0a_back_a(u):
            h, mt = units[u]
            s = mt % 2
            for bk in range(2):
                pt = f"pt{bk}"
                for i in range(8):
                    EMIT("pe", lambda e, bk=bk, i=i: e.transpose(
                        PT[bk][:, i * 128:(i + 1) * 128],
                        ab(PTOK + (bk * 8 + i) * 128, 128), ab(C_ID, 128)),
                        reads=["PTOK", "CID"], writes=[pt])
            for bk in range(2):
                EMIT("dv", lambda e, bk=bk, s=s: e.tensor_copy(
                    ab(PTP[s] + bk * 1024, 1024), PT[bk][:, :]),
                    reads=[f"pt{bk}"], writes=[f"PTP{s}"])

        def l0a_back_b(u):
            h, mt = units[u]
            s = mt % 2
            for nt in range(NT):
                EMIT("pe", lambda e, nt=nt, h=h, s=s: e.matmul(
                    P[2][:, s * 128:s * 128 + 128],
                    ab(VTOK + nt * 1024 + h * 128, 128),
                    ab(PTP[s] + nt * 128, 128),
                    start=(nt == 0), stop=(nt == NT - 1)),
                    reads=[f"PTP{s}", f"VT{nt}_0", f"VT{nt}_1"], writes=["p2"])
            EMIT("dv", lambda e, s=s: e.tensor_copy(
                ab(OSB + s * 128, 128), P[2][:, s * 128:s * 128 + 128]),
                reads=["p2"], writes=[f"OSB{s}"])
            EMIT("pe", lambda e, s=s: e.transpose(
                PT[0][:, 0:128], ab(OSB + s * 128, 128), ab(C_ID, 128)),
                reads=[f"OSB{s}", "CID"], writes=["pt0"])
            EMIT("dv", lambda e, mt=mt, h=h, s=s: e.scalar_tensor_tensor(
                Xt(mt, h * 128, 128), PT[0][:, 0:128], col(S_GREC + s),
                Xt(mt, h * 128, 128), OP.mult, OP.add),
                reads=["pt0", f"SGR{s}", f"X{mt}"], writes=[f"X{mt}"])

        for u in range(len(units)):
            l0a_scores(u)
            if u >= 1:
                l0a_back_a(u - 1)
            l0a_tail_a(u)
            if u >= 1:
                l0a_back_b(u - 1)
            l0a_tail_b(u)
        if units:
            l0a_back_a(len(units) - 1)
            l0a_back_b(len(units) - 1)

        barrier()
        for t in range(NT if KPH >= 3 else 0):
            layernorm(t, t % 2)

        barrier()
        HT = RB
        WFS = [RB + 16384, RB + 20480]
        WPS = [RB + 24576, RB + 25088]
        TMB = [RB + 25600, RB + 26112]

        for mcq in range(4 if KPH >= 4 else 0):
            for fch in range(32):
                buf = (fch // 4) % 2
                if fch % 4 == 0:
                    for k in range(ND):
                        EMIT("gp", lambda e, k=k, fch=fch, buf=buf: e.dma_start(
                            ab(WFS[buf] + k * 512, 512),
                            wfcb_sp[k * 128:(k + 1) * 128,
                                    fch * 128:fch * 128 + 512]),
                            reads=["WFCB"],
                            writes=([f"WFS{buf}"] if k == 0 else []), inc=16)
                    MARK("gp", f"WFS{buf}")
                pb = f"p{fch % 2}"
                for k in range(ND):
                    EMIT("pe", lambda e, k=k, fch=fch, mcq=mcq, buf=buf: e.matmul(
                        P[fch % 2][:, :],
                        ab(WFS[buf] + k * 512 + (fch % 4) * 128, 128),
                        XTk(k, mcq * 512, 512), start=(k == 0), stop=(k == ND - 1)),
                        reads=[f"WFS{buf}"] + xtr(mcq * 512, 512), writes=[pb])
                tb = fch % 2
                EMIT("dv", lambda e, fch=fch, tb=tb: e.tensor_scalar(
                    ab(TMB[tb], 512), P[fch % 2][:, :], 0.0, None, OP.max),
                    reads=[pb], writes=[f"TMB{tb}"])
                EMIT("ac", lambda e, fch=fch, tb=tb: e.activation(
                    ab(HT + fch * 512, 512), ab(TMB[tb], 512), AF.Square),
                    reads=[f"TMB{tb}"], writes=[f"HT{fch}"])
            for dc in range(2):
                for ft in range(32):
                    buf = ft % 2
                    EMIT("sy", lambda e, ft=ft, dc=dc, buf=buf: e.dma_start(
                        ab(WPS[buf], 512),
                        wpjb_sp[ft * 128:(ft + 1) * 128, dc * 512:(dc + 1) * 512]),
                        reads=["WPJB"], writes=[f"WPS{buf}"], inc=16)
                    for mt4 in range(4):
                        EMIT("pe", lambda e, ft=ft, mt4=mt4, buf=buf: e.matmul(
                            P[2 + mt4][:, :], ab(HT + ft * 512 + mt4 * 128, 128),
                            ab(WPS[buf], 512), start=(ft == 0), stop=(ft == 31)),
                            reads=[f"WPS{buf}", f"HT{ft}"], writes=[f"p{2 + mt4}"])
                for mt4 in range(4):
                    mt = mcq * 4 + mt4
                    pb = f"p{2 + mt4}"
                    a_, b_ = af(FS, 512), af(FS + 512, 512)
                    EMIT("dv", lambda e, mt4=mt4: e.tensor_scalar(
                        af(FS, 512), P[2 + mt4][:, :], TH[0], None, OP.is_gt),
                        reads=[pb], writes=["ACCA"])
                    for i, (src, dst, rs_, ws) in enumerate(
                            ((a_, b_, "ACCA", "ACCB"), (b_, a_, "ACCB", "ACCA"),
                             (a_, b_, "ACCA", "ACCB"), (b_, a_, "ACCB", "ACCA"))):
                        EMIT("dv", lambda e, mt4=mt4, i=i, src=src, dst=dst:
                             e.scalar_tensor_tensor(
                                 dst, P[2 + mt4][:, :], TH[i + 1], src,
                                 OP.is_gt, OP.add),
                             reads=[pb, rs_], writes=[ws])
                    EMIT("dv", lambda e, mt=mt, dc=dc: e.scalar_tensor_tensor(
                        Xt(mt, dc * 512, 512), af(FS, 512), 0.2,
                        Xt(mt, dc * 512, 512), OP.mult, OP.add),
                        reads=["ACCA", f"X{mt}"], writes=[f"X{mt}"])
            for mt4 in range(4):
                mt = mcq * 4 + mt4
                layernorm(mt, mt % 2)

        barrier()
        LATQ = [RB, RB + 2048]
        LATK = [RB + 4096, RB + 6144]
        LATV = [RB + 8192, RB + 10240]
        QTJ = [RB + 12288, RB + 14336]
        KTJ = [RB + 16384, RB + 18432]
        WDQ = [RB + 20480, RB + 21504]
        WDK = [RB + 22528, RB + 23552]
        WDV = [RB + 24576, RB + 25600]
        RPC = RB + 26624
        RPS = RB + 28672
        TP1 = RB + 30720
        TP2 = RB + 31232

        EMIT("gp", lambda e: e.dma_start(ab(RPC, 2048), rc_d), writes=["RPC"], inc=16)
        EMIT("gp", lambda e: e.dma_start(ab(RPS, 2048), rs_d), writes=["RPS"], inc=16)

        for j in range(8 if KPH >= 5 else 0):
            par = j % 2
            for (wd, dram, nm) in ((WDQ, wqd_d, "WDQ"), (WDK, wkd_d, "WDK"),
                                   (WDV, wvd_d, "WDV")):
                for k in range(ND):
                    EMIT("gp", lambda e, k=k, j=j, wd=wd, dram=dram, par=par:
                         e.dma_start(ab(wd[par] + k * 128, 128),
                                     dram[k * 128:(k + 1) * 128, j * 128:(j + 1) * 128]),
                         writes=([f"{nm}{par}"] if k == 0 else []), inc=16)
                MARK("gp", f"{nm}{par}")
            for (wd, lat, nm, lnm) in ((WDQ, LATQ, "WDQ", "LATQ"),
                                       (WDK, LATK, "WDK", "LATK"),
                                       (WDV, LATV, "WDV", "LATV")):
                for mc in range(4):
                    pb = f"p{mc % 2}"
                    for k in range(ND):
                        EMIT("pe", lambda e, k=k, mc=mc, wd=wd, par=par: e.matmul(
                            P[mc % 2][:, :], ab(wd[par] + k * 128, 128),
                            XTk(k, mc * 512, 512), start=(k == 0), stop=(k == ND - 1)),
                            reads=[f"{nm}{par}"] + xtr(mc * 512, 512), writes=[pb])
                    EMIT("ac", lambda e, mc=mc, lat=lat, par=par: e.activation(
                        ab(lat[par] + mc * 512, 512), P[mc % 2][:, :], AF.Copy),
                        reads=[pb], writes=[f"{lnm}{par}"])
            EMIT("sy", lambda e, j=j, par=par: e.dma_start(
                lv_sp[j * 128:(j + 1) * 128, :], ab(LATV[par], 2048)),
                reads=[f"LATV{par}"], inc=16)
            for (lat, lnm, upo, unm, upso, usnm, qt, qnm, spill) in (
                    (LATQ, "LATQ", C_UPQ, "CUPQ", C_UPQS, "CUPQS", QTJ, "QTJ", qr_sp),
                    (LATK, "LATK", C_UPK, "CUPK", C_UPKS, "CUPKS", KTJ, "KTJ", kr_sp)):
                for mc in range(4):
                    EMIT("pe", lambda e, mc=mc, lat=lat, upo=upo, par=par: e.matmul(
                        P[2][:, :], ab(upo, 128), ab(lat[par] + mc * 512, 512),
                        start=True, stop=True),
                        reads=[f"{lnm}{par}", unm], writes=["p2"])
                    EMIT("pe", lambda e, mc=mc, lat=lat, upso=upso, par=par: e.matmul(
                        P[3][:, :], ab(upso, 128), ab(lat[par] + mc * 512, 512),
                        start=True, stop=True),
                        reads=[f"{lnm}{par}", usnm], writes=["p3"])
                    EMIT("dv", lambda e, mc=mc: e.tensor_tensor(
                        ab(TP1, 512), P[2][:, :], ab(RPC + mc * 512, 512), OP.mult),
                        reads=["p2", "RPC"], writes=["TP1"])
                    EMIT("dv", lambda e, mc=mc: e.tensor_tensor(
                        ab(TP2, 512), P[3][:, :], ab(RPS + mc * 512, 512), OP.mult),
                        reads=["p3", "RPS"], writes=["TP2"])
                    EMIT("dv", lambda e, mc=mc, qt=qt, par=par: e.tensor_tensor(
                        ab(qt[par] + mc * 512, 512), ab(TP1, 512), ab(TP2, 512),
                        OP.add),
                        reads=["TP1", "TP2"], writes=[f"{qnm}{par}"])
                EMIT("sy", lambda e, j=j, qt=qt, par=par, spill=spill: e.dma_start(
                    spill[j * 128:(j + 1) * 128, :], ab(qt[par], 2048)),
                    reads=[f"{qnm}{par}"], inc=16)

        barrier()
        KTH = [XTO, XTO + 2048]
        QTH = [XTO + 4096, XTO + 6144]
        LVH = [XTO + 8192, XTO + 10240]
        PTL = XTO + 12288
        YT = RB
        VH = [RB + 16384, RB + 17424]
        PTMP = [RB + 18464, RB + 18720]
        YSTG = RB + 18976
        WC = RB + 19232
        RB1 = RB + 27424
        RB2 = RB + 27680

        for k in range(ND if KPH >= 6 else 0):
            EMIT("gp", lambda e, k=k: e.dma_start(
                ab(WC + k * 1024, 1024), wc_d[k * 128:(k + 1) * 128, :]),
                writes=(["WC"] if k == 0 else []), inc=16)
        MARK("gp", "WC")
        for par in range(2 if KPH >= 6 else 0):
            for nt in range(NT):
                EMIT("dv", lambda e, par=par, nt=nt: e.memset(
                    ab(VH[par] + nt * 65 + 64, 1), 1.0), writes=[f"VHO{par}"])

        units2 = []
        for h in range(16 if KPH >= 6 else 0):
            for mcq in range(8):
                units2.append((h, mcq))

        def l1_front(i):
            h, mcq = units2[i]
            par = h % 2
            avp = f"p{2 + i % 2}"
            if mcq == 0:
                EMIT("sy", lambda e, h=h, par=par: e.dma_start(
                    ab(LVH[par], 2048, 0, 64), lv_sp[h * 64:(h + 1) * 64, :]),
                    writes=[f"LVH{par}"], inc=16)
                EMIT("sy", lambda e, h=h, par=par: e.dma_start(
                    ab(KTH[par], 2048, 0, 64), kr_sp[h * 64:(h + 1) * 64, :]),
                    writes=[f"KTH{par}"], inc=16)
                EMIT("sy", lambda e, h=h, par=par: e.dma_start(
                    ab(QTH[par], 2048, 0, 64), qr_sp[h * 64:(h + 1) * 64, :]),
                    writes=[f"QTH{par}"], inc=16)
                for nt in range(NT):
                    pb = f"p{nt % 2}"
                    EMIT("pe", lambda e, nt=nt, par=par: e.matmul(
                        P[nt % 2][:, 0:64], ab(LVH[par] + nt * 128, 128, 0, 64),
                        ab(C_UPV1, 64, 0, 64), start=True, stop=True),
                        reads=[f"LVH{par}", "CUPV"], writes=[pb])
                    EMIT("dv", lambda e, nt=nt, par=par: e.tensor_copy(
                        ab(VH[par] + nt * 65, 64), P[nt % 2][:, 0:64]),
                        reads=[pb, f"VHO{par}"], writes=[f"VH{par}"])
            maxnt = 2 * mcq + 2
            for nt in range(maxnt):
                pb = f"p{nt % 2}"
                EMIT("pe", lambda e, nt=nt, mcq=mcq, par=par: e.matmul(
                    P[nt % 2][:, 0:256], ab(KTH[par] + nt * 128, 128, 0, 64),
                    ab(QTH[par] + mcq * 256, 256, 0, 64),
                    start=True, stop=True),
                    reads=[f"KTH{par}", f"QTH{par}"], writes=[pb])
                if nt < maxnt - 2:
                    EMIT("ac", lambda e, nt=nt: e.activation(
                        ab(PTL + nt * 256, 256), P[nt % 2][:, 0:256], AF.Exp,
                        scale=0.125),
                        reads=[pb], writes=[f"PTL{nt}"])
                else:
                    v = nt - 2 * mcq
                    pp = nt % 2
                    EMIT("ac", lambda e, pp=pp, nt=nt: e.activation(
                        ab(PTMP[pp], 256), P[nt % 2][:, 0:256], AF.Exp,
                        scale=0.125),
                        reads=[pb], writes=[f"PTMP{pp}"])
                    EMIT("dv", lambda e, nt=nt, v=v, pp=pp: e.tensor_tensor(
                        ab(PTL + nt * 256, 256), ab(PTMP[pp], 256),
                        ab(C_M0 if v == 0 else C_M1, 256), OP.mult),
                        reads=[f"PTMP{pp}", "CM0", "CM1"], writes=[f"PTL{nt}"])
            for nt in range(maxnt):
                EMIT("pe", lambda e, nt=nt, par=par, maxnt=maxnt, i=i: e.matmul(
                    P[2 + i % 2][0:65, 0:256], ab(VH[par] + nt * 65, 65),
                    ab(PTL + nt * 256, 256),
                    start=(nt == 0), stop=(nt == maxnt - 1)),
                    reads=[f"VH{par}", f"PTL{nt}"], writes=[avp])

        def l1_back(i):
            h, mcq = units2[i]
            avp = f"p{2 + i % 2}"
            AVP = P[2 + i % 2]
            EMIT("dv", lambda e, AVP=AVP: e.tensor_copy(
                af(FS + 768, 256, 0, 1), AVP[64:65, 0:256]),
                reads=[avp], writes=["RRAW"])
            for half in range(2):
                EMIT("pe", lambda e, half=half: e.transpose(
                    P[4][:, half:half + 1], af(FS + 768 + half * 128, 128, 0, 1),
                    af(S_IDF, 1, 0, 1)),
                    reads=["RRAW", "SIDF"], writes=["p4"])
            EMIT("dv", lambda e: e.reciprocal(af(S_RCL, 2), P[4][:, 0:2]),
                 reads=["p4"], writes=["SRCL"])
            for half in range(2):
                EMIT("pe", lambda e, half=half: e.transpose(
                    P[4][0:1, 4 + half * 128:132 + half * 128],
                    af(S_RCL + half, 1), af(S_IDF, 128)),
                    reads=["SRCL", "SIDF"], writes=["p4b"])
            EMIT("dv", lambda e: e.tensor_copy(ab(RB1, 256, 0, 1),
                                               P[4][0:1, 4:260]),
                 reads=["p4b"], writes=["RB1"])
            EMIT("dv", lambda e: e.tensor_tensor(
                af(FS + 512, 256, 0, 1), P[4][0:1, 4:260],
                ab(RB1, 256, 0, 1), OP.subtract),
                reads=["p4b", "RB1"], writes=["RRES"])
            EMIT("dv", lambda e: e.tensor_copy(ab(RB2, 256, 0, 1),
                                               af(FS + 512, 256, 0, 1)),
                 reads=["RRES"], writes=["RB2"])
            EMIT("pe", lambda e: e.matmul(
                P[5][0:64, 0:256], ab(C_ONE, 64, 0, 1), ab(RB1, 256, 0, 1),
                start=True, stop=False),
                reads=["CONE", "RB1"], writes=["p5"])
            EMIT("pe", lambda e: e.matmul(
                P[5][0:64, 0:256], ab(C_ONE, 64, 0, 1), ab(RB2, 256, 0, 1),
                start=False, stop=True),
                reads=["CONE", "RB2"], writes=["p5"])
            EMIT("dv", lambda e: e.tensor_copy(af(FS, 256, 0, 64),
                                               P[5][0:64, 0:256]),
                 reads=["p5"], writes=["BCF"])
            pt_ = h // 2
            if h % 2 == 0:
                EMIT("dv", lambda e, pt_=pt_, mcq=mcq, AVP=AVP: e.tensor_tensor(
                    ab(YT + pt_ * 2048 + mcq * 256, 256, 0, 64),
                    AVP[0:64, 0:256], af(FS, 256, 0, 64), OP.mult),
                    reads=[avp, "BCF"], writes=[f"YT{pt_}_{mcq}"])
            else:
                EMIT("dv", lambda e, AVP=AVP: e.tensor_tensor(
                    ab(YSTG, 256, 0, 64), AVP[0:64, 0:256], af(FS, 256, 0, 64),
                    OP.mult),
                    reads=[avp, "BCF"], writes=["YSTG"])
                EMIT("sy", lambda e, pt_=pt_, mcq=mcq: e.dma_start(
                    Ab[64:128, YT + pt_ * 2048 + mcq * 256:
                       YT + pt_ * 2048 + mcq * 256 + 256],
                    ab(YSTG, 256, 0, 64)),
                    reads=["YSTG"], writes=[f"YT{pt_}_{mcq}"], inc=16)

        for i in range(len(units2)):
            l1_front(i)
            if i >= 1:
                l1_back(i - 1)
        if units2:
            l1_back(len(units2) - 1)

        barrier()
        for t in range(NT if KPH >= 7 else 0):
            readies = [f"YT{p_}_{t // 2}" for p_ in range(ND)]
            for dc in range(2):
                pb = f"p{4 + dc}"
                for k in range(ND):
                    EMIT("pe", lambda e, k=k, t=t, dc=dc: e.matmul(
                        P[4 + dc][:, :], ab(YT + k * 2048 + t * 128, 128),
                        ab(WC + k * 1024 + dc * 512, 512),
                        start=(k == 0), stop=(k == ND - 1)),
                        reads=readies + ["WC"], writes=[pb])
                EMIT("dv", lambda e, t=t, dc=dc: e.tensor_tensor(
                    Xt(t, dc * 512, 512), P[4 + dc][:, :], Xt(t, dc * 512, 512),
                    OP.add),
                    reads=[pb, f"X{t}"], writes=[f"X{t}"])
            layernorm(t, t % 2)

        barrier()
        CST = RB
        SNT = RB + 8192
        for (src1, src2, dst, dnm, useb) in (() if KPH < 8 else ((th_d, tn_d, CST, "CST", True),
                                             (ph_d, pn_d, SNT, "SNT", False))):
            for k in range(ND):
                for cc in range(2):
                    EMIT("sy", lambda e, k=k, cc=cc, src1=src1: e.dma_start(
                        af(FS, 512),
                        src1[k * 128:(k + 1) * 128, cc * 512:(cc + 1) * 512]),
                        writes=["TS1"], inc=16)
                    EMIT("sy", lambda e, k=k, cc=cc, src2=src2: e.dma_start(
                        af(FS + 512, 512),
                        src2[k * 128:(k + 1) * 128, cc * 512:(cc + 1) * 512]),
                        writes=["TS2"], inc=16)
                    EMIT("dv", lambda e: e.tensor_tensor(
                        af(FS, 512), af(FS, 512), af(FS + 512, 512), OP.add),
                        reads=["TS1", "TS2"], writes=["TS1"])
                    if useb:
                        EMIT("ac", lambda e, k=k, cc=cc, dst=dst: e.activation(
                            ab(dst + k * 1024 + cc * 512, 512), af(FS, 512),
                            AF.Sin, bias=col(S_HPI)),
                            reads=["TS1", "SHPI"], writes=[dnm])
                    else:
                        EMIT("ac", lambda e, k=k, cc=cc, dst=dst: e.activation(
                            ab(dst + k * 1024 + cc * 512, 512), af(FS, 512),
                            AF.Sin),
                            reads=["TS1"], writes=[dnm])
        for t in range(NT):
            for dc in range(2 if KPH >= 8 else 0):
                for k in range(ND):
                    EMIT("pe", lambda e, k=k, t=t, dc=dc: e.matmul(
                        P[0][:, :], XTk(k, t * 128, 128),
                        ab(CST + k * 1024 + dc * 512, 512),
                        start=(k == 0), stop=(k == ND - 1)),
                        reads=["CST"] + xtr(t * 128, 128), writes=["p0"])
                for k in range(ND):
                    EMIT("pe", lambda e, k=k, t=t, dc=dc: e.matmul(
                        P[1][:, :], XTk(k, t * 128, 128),
                        ab(SNT + k * 1024 + dc * 512, 512),
                        start=(k == 0), stop=(k == ND - 1)),
                        reads=["SNT"] + xtr(t * 128, 128), writes=["p1"])
                EMIT("ac", lambda e: e.activation(af(FS, 512), P[0][:, :], AF.Square),
                     reads=["p0"], writes=["RE2"])
                EMIT("ac", lambda e: e.activation(af(FS + 512, 512), P[1][:, :],
                                                  AF.Square),
                     reads=["p1"], writes=["IM2"])
                EMIT("dv", lambda e: e.tensor_tensor(
                    af(FS, 512), af(FS, 512), af(FS + 512, 512), OP.add),
                    reads=["RE2", "IM2"], writes=["RE2"])
                EMIT("ac", lambda e: e.activation(af(FS + 512, 512), af(FS, 512),
                                                  AF.Sqrt),
                     reads=["RE2"], writes=["IM2"])
                EMIT("dv", lambda e, t=t, dc=dc: e.tensor_tensor(
                    Xt(t, dc * 512, 512), af(FS + 512, 512), Xt(t, dc * 512, 512),
                    OP.add),
                    reads=["IM2", f"X{t}"], writes=[f"X{t}"])
            if KPH >= 8:
                layernorm(t, t % 2, rebuild=False)
            EMIT("sy", lambda e, t=t: e.dma_start(
                out_d[t * 128:(t + 1) * 128, :], Xt(t)),
                reads=[f"X{t}"], inc=16, sem="st")

        ops["sy"].append(lambda e: e.wait_ge(sems["st"], 16 * NT))

        @block.tensor
        def _(t_):
            for fn in ops["pe"]:
                fn(t_)

        @block.scalar
        def _(s_):
            for fn in ops["ac"]:
                fn(s_)

        @block.vector
        def _(v_):
            for fn in ops["dv"]:
                fn(v_)

        @block.gpsimd
        def _(g_):
            for fn in ops["gp"]:
                fn(g_)

        @block.sync
        def _(s_):
            for fn in ops["sy"]:
                fn(s_)

    return nc


def _host_prep(inp):
    f32 = np.float32
    d = {}
    Wq = np.asarray(inp["L0_attn_Wq"], f32)
    Wk = np.asarray(inp["L0_attn_Wk"], f32)
    wq_t = np.zeros((1024, 256), f32)
    wk_t = np.zeros((1024, 256), f32)
    bq_pad = np.zeros((256, 1), f32)
    bk_pad = np.zeros((256, 1), f32)
    bq = np.asarray(inp["L0_attn_bq"], f32)
    bk = np.asarray(inp["L0_attn_bk"], f32)
    for h in range(8):
        g, hh = h // 4, h % 4
        c0 = g * 128 + hh * 32
        wq_t[:, c0:c0 + 16] = Wq[h * 16:(h + 1) * 16, :].T
        wk_t[:, c0:c0 + 16] = Wk[h * 16:(h + 1) * 16, :].T
        bq_pad[c0:c0 + 16, 0] = bq[h * 16:(h + 1) * 16]
        bk_pad[c0:c0 + 16, 0] = bk[h * 16:(h + 1) * 16]
    d["wq_t"], d["wk_t"] = wq_t, wk_t
    d["bq_pad"], d["bk_pad"] = bq_pad, bk_pad
    d["wv_t"] = np.ascontiguousarray(np.asarray(inp["L0_attn_Wv"], f32).T)
    d["bv_r"] = np.asarray(inp["L0_attn_bv"], f32).reshape(1, 1024)
    d["gam"] = np.broadcast_to(
        np.asarray(inp["L0_attn_gamma"], f32).reshape(1, 1), (128, 1)).copy()
    d["wfc_t"] = np.ascontiguousarray(np.asarray(inp["L0_ffn_W_fc"], f32).T)
    d["wpj_t"] = np.ascontiguousarray(np.asarray(inp["L0_ffn_W_proj"], f32).T)
    d["wqd"] = np.asarray(inp["L1_attn_Wq_down"], f32)
    d["wkd"] = np.asarray(inp["L1_attn_Wk_down"], f32)
    d["wvd"] = np.asarray(inp["L1_attn_Wv_down"], f32)

    def upblock(c, e):
        U = np.concatenate([c, e], axis=1)
        out = np.zeros((128, 128), f32)
        out[0:64, 0:64] = U
        out[64:128, 64:128] = U
        return out

    def upswap(e):
        perm = np.arange(32)
        perm = perm + 1 - 2 * (perm % 2)
        Us = np.zeros((64, 64), f32)
        Us[:, 32:64] = e[:, perm]
        out = np.zeros((128, 128), f32)
        out[0:64, 0:64] = Us
        out[64:128, 64:128] = Us
        return out

    qc = np.asarray(inp["L1_attn_Wq_up_c"], f32)
    qe = np.asarray(inp["L1_attn_Wq_up_e"], f32)
    kc = np.asarray(inp["L1_attn_Wk_up_c"], f32)
    ke = np.asarray(inp["L1_attn_Wk_up_e"], f32)
    d["upq"] = upblock(qc, qe)
    d["upqs"] = upswap(qe)
    d["upk"] = upblock(kc, ke)
    d["upks"] = upswap(ke)
    d["wvu"] = np.asarray(inp["L1_attn_Wv_up"], f32)
    d["wc"] = np.asarray(inp["L1_attn_Wc"], f32)
    d["th"] = np.asarray(inp["L1_ffn_theta"], f32)
    d["tn"] = np.asarray(inp["L1_ffn_theta_noise"], f32)
    d["ph"] = np.asarray(inp["L1_ffn_phi"], f32)
    d["pn"] = np.asarray(inp["L1_ffn_phi_noise"], f32)
    d["identM"] = np.eye(128, dtype=f32)
    tpos = np.arange(T, dtype=np.float64)
    invf = 10000.0 ** (-np.arange(16, dtype=np.float64) / 16.0)
    ang = tpos[None, :] * invf[:, None]
    cosv = np.cos(ang)
    sinv = np.sin(ang)
    rc = np.zeros((128, T), np.float64)
    rs = np.zeros((128, T), np.float64)
    for r in range(128):
        j = r % 64
        if j < 32:
            rc[r] = 1.0
        else:
            i = j - 32
            rc[r] = cosv[i % 16]
            rs[r] = (-1.0 if i % 2 == 0 else 1.0) * sinv[i % 16]
    d["ropec"] = rc.astype(f32)
    d["ropes"] = rs.astype(f32)
    m = np.zeros((2, 128, 256), f32)
    for v in range(2):
        for nl in range(128):
            m[v, nl, :] = (128 * v + nl <= np.arange(256)).astype(f32)
    d["mask0"], d["mask1"] = m[0], m[1]
    return d


def kernel(**inputs):
    shared = _host_prep(inputs)
    x = np.ascontiguousarray(np.asarray(inputs["L0_attn_x"], np.float32))
    nc = build_nc()
    in_maps = [dict(shared, x=np.ascontiguousarray(x[i])) for i in range(8)]
    res = run_bass_kernel_spmd(nc, in_maps, core_ids=list(range(8)))
    out = np.stack([np.asarray(res.results[i]["out"], np.float32)
                    for i in range(8)])
    return out


# version 07
# speedup vs baseline: 1.1849x; 1.0181x over previous
import math
import os
import numpy as np
import concourse.bass as bass
import concourse.mybir as mybir
from concourse.bass_utils import run_bass_kernel_spmd
from concourse.alu_op_type import AluOpType
from contextlib import ExitStack

F32 = mybir.dt.float32
BF16 = mybir.dt.bfloat16
AF = mybir.ActivationFunctionType
AX = mybir.AxisListType.X
OP = AluOpType

T, D, NT, ND = 2048, 1024, 16, 8
KPH = int(os.environ.get("KPH", "99"))

TH = [0.24419428096993973, 0.4639691338428854, 0.6861859295255306,
      0.9059607823984763, 1.0839784132255623]

AF_W = 21504
XO = 0
FS = 16384
ST = 20480

AB_W = 54272
XTO = 0
RB = 16384
CB = 50688
C_ID = CB
C_ONE = CB + 128
C_BV = CB + 256
C_M0 = CB + 1280
C_M1 = CB + 1536
C_UPQ = CB + 1792
C_UPQS = CB + 1920
C_UPK = CB + 2048
C_UPKS = CB + 2176
C_UPV1 = CB + 2304
XBF = 53248

S_NEGMAX = ST + 0
S_R = ST + 2
S_RREC = ST + 4
S_GREC = ST + 6
S_SUM = ST + 8
S_SS = ST + 10
S_NMEAN = ST + 12
S_M2 = ST + 14
S_VAR = ST + 16
S_STD = ST + 18
S_RSTD = ST + 20
S_EPS = ST + 22
S_JUNK = ST + 23
S_GB = ST + 24
S_HPI = ST + 25
S_BQA = ST + 26
S_BQB = ST + 27
S_BKA = ST + 28
S_BKB = ST + 29
S_IDF = ST + 512
S_RCL = ST + 640


def build_nc():
    nc = bass.Bass("TRN2", target_bir_lowering=False)
    dr = {}

    def din(name, shape, dt=F32):
        dr[name] = nc.dram_tensor(name, shape, dt, kind="ExternalInput").ap()
        return dr[name]

    x_d = din("x", [T, D])
    wq_d = din("wq_t", [D, 256])
    bq_d = din("bq_pad", [256, 1])
    wk_d = din("wk_t", [D, 256])
    bk_d = din("bk_pad", [256, 1])
    wv_d = din("wv_t", [D, D])
    bv_d = din("bv_r", [1, D])
    gam_d = din("gam", [128, 1])
    wfc_d = din("wfc_t", [D, 4096])
    wpj_d = din("wpj_t", [4096, D])
    wqd_d = din("wqd", [D, D])
    wkd_d = din("wkd", [D, D])
    wvd_d = din("wvd", [D, D])
    upq_d = din("upq", [128, 128])
    upqs_d = din("upqs", [128, 128])
    upk_d = din("upk", [128, 128])
    upks_d = din("upks", [128, 128])
    wvu_d = din("wvu", [64, 64])
    wc_d = din("wc", [D, D])
    th_d = din("th", [D, D])
    tn_d = din("tn", [D, D])
    ph_d = din("ph", [D, D])
    pn_d = din("pn", [D, D])
    id_d = din("identM", [128, 128])
    rc_d = din("ropec", [128, T])
    rs_d = din("ropes", [128, T])
    m0_d = din("mask0", [128, 256])
    m1_d = din("mask1", [128, 256])
    qr_sp = nc.dram_tensor("qr_sp", [1024, T], BF16, kind="Internal").ap()
    kr_sp = nc.dram_tensor("kr_sp", [1024, T], BF16, kind="Internal").ap()
    lv_sp = nc.dram_tensor("lv_sp", [1024, T], BF16, kind="Internal").ap()
    wfcb_sp = nc.dram_tensor("wfcb_sp", [D, 4096], BF16, kind="Internal").ap()
    wpjb_sp = nc.dram_tensor("wpjb_sp", [4096, D], BF16, kind="Internal").ap()
    out_d = nc.dram_tensor("out", [T, D], F32, kind="ExternalOutput").ap()

    with ExitStack() as stk:
        E = stk.enter_context
        Af = E(nc.sbuf_tensor("arena_f", [128, AF_W], F32))
        Ab = E(nc.sbuf_tensor("arena_b", [128, AB_W], BF16))
        P = [E(nc.psum_tensor(f"p{i}", [128, 512], F32)) for i in range(6)]
        PT = [E(nc.psum_tensor(f"pt{i}", [128, 1024], BF16)) for i in range(2)]
        sems = {k: E(nc.semaphore("sem_" + k))
                for k in ("pe", "ac", "dv", "gp", "sy", "st")}
        block = E(nc.Block())

        ops = {k: [] for k in ("pe", "ac", "dv", "gp", "sy")}
        cnt = {k: 0 for k in ops}
        waited = {}
        ready = {}
        readers = {}

        def W(eng, prod, val):
            if val <= 0:
                return
            key = (eng, prod)
            if waited.get(key, -1) >= val:
                return
            waited[key] = val
            s = sems[prod]
            ops[eng].append(lambda e, s=s, v=val: e.wait_ge(s, v))

        def need(res, eng):
            r = ready.get(res)
            if r and r[0] != eng:
                W(eng, r[0], r[1])

        def acquire(res, eng):
            r = ready.get(res)
            if r and r[0] != eng:
                W(eng, r[0], r[1])
            for e2, v in readers.get(res, {}).items():
                if e2 != eng:
                    W(eng, e2, v)

        def EMIT(eng, fn, reads=(), writes=(), inc=1, sem=None):
            for r in reads:
                need(r, eng)
            for w in writes:
                acquire(w, eng)
            cnt[eng] += inc
            if inc:
                s = sems[sem if sem else eng]
                ops[eng].append(lambda e, f=fn, s=s, n=inc: f(e).then_inc(s, n))
            else:
                ops[eng].append(fn)
            for r in reads:
                readers.setdefault(r, {})[eng] = cnt[eng]
            for w in writes:
                ready[w] = (eng, cnt[eng])
                readers[w] = {}

        def MARK(eng, *names):
            for nm in names:
                ready[nm] = (eng, cnt[eng])
                readers[nm] = {}

        def barrier():
            for c in ops:
                for p in ops:
                    if p != c:
                        W(c, p, cnt[p])

        def af(off, w, p0=0, pn=128):
            return Af[p0:p0 + pn, off:off + w]

        def ab(off, w, p0=0, pn=128):
            return Ab[p0:p0 + pn, off:off + w]

        def Xt(t, c0=0, w=1024):
            return Af[:, XO + t * 1024 + c0: XO + t * 1024 + c0 + w]

        def XTk(k, c0, w):
            return Ab[:, XTO + k * 2048 + c0: XTO + k * 2048 + c0 + w]

        def col(o, n=1, p0=0, pn=128):
            return Af[p0:p0 + pn, o:o + n]

        def xtr(c0, w):
            t0, t1 = c0 // 128, (c0 + w - 1) // 128
            return [f"XT{t}" for t in range(t0, t1 + 1)]

        for t in range(NT):
            EMIT("sy", lambda e, t=t: e.dma_start(Xt(t), x_d[t * 128:(t + 1) * 128, :]),
                 writes=[f"X{t}"], inc=16)
        for (off, w, d_, nm, p0, pn) in (
                (C_ID, 128, id_d, "CID", 0, 128), (C_BV, 1024, bv_d, "CBV", 0, 1),
                (C_M0, 256, m0_d, "CM0", 0, 128), (C_M1, 256, m1_d, "CM1", 0, 128),
                (C_UPQ, 128, upq_d, "CUPQ", 0, 128), (C_UPQS, 128, upqs_d, "CUPQS", 0, 128),
                (C_UPK, 128, upk_d, "CUPK", 0, 128), (C_UPKS, 128, upks_d, "CUPKS", 0, 128),
                (C_UPV1, 64, wvu_d, "CUPV", 0, 64)):
            EMIT("gp", lambda e, off=off, w=w, d_=d_, p0=p0, pn=pn: e.dma_start(
                ab(off, w, p0, pn), d_), writes=[nm], inc=16)
        for (o, d_, nm) in ((S_GB, gam_d, "SGB"),):
            EMIT("sy", lambda e, o=o, d_=d_: e.dma_start(col(o), d_), writes=[nm], inc=16)
        EMIT("sy", lambda e: e.dma_start(af(S_IDF, 128), id_d), writes=["SIDF"], inc=16)
        for (o, d_, sl, nm) in ((S_BQA, bq_d, 0, "SBQA"), (S_BQB, bq_d, 1, "SBQB"),
                                (S_BKA, bk_d, 0, "SBKA"), (S_BKB, bk_d, 1, "SBKB")):
            EMIT("sy", lambda e, o=o, d_=d_, sl=sl: e.dma_start(
                col(o), d_[sl * 128:(sl + 1) * 128, :]), writes=[nm], inc=16)
        EMIT("dv", lambda e: e.memset(col(S_EPS), 1e-5), writes=["SEPS"])
        EMIT("dv", lambda e: e.memset(col(S_HPI), float(np.pi / 2)), writes=["SHPI"])
        EMIT("dv", lambda e: e.memset(ab(C_ONE, 128, 0, 1), 1.0), writes=["CONE"])

        def xt_build(t):
            EMIT("ac", lambda e, t=t: e.activation(ab(XBF, 1024), Xt(t), AF.Copy),
                 reads=[f"X{t}"], writes=["XBF"])
            for f in range(ND):
                EMIT("pe", lambda e, f=f: e.transpose(
                    PT[0][:, f * 128:(f + 1) * 128], ab(XBF + f * 128, 128),
                    ab(C_ID, 128)),
                    reads=["XBF", "CID"], writes=["pt0"])
            for f in range(ND):
                EMIT("dv", lambda e, f=f, t=t: e.tensor_copy(
                    XTk(f, t * 128, 128), PT[0][:, f * 128:(f + 1) * 128]),
                    reads=["pt0"], writes=[f"XT{t}"])

        def layernorm(t, s, rebuild=True):
            sq = af(FS + 1024, 1024)
            EMIT("ac", lambda e, t=t, s=s, sq=sq: e.activation(
                sq, Xt(t), AF.Square, accum_out=col(S_SS + s)),
                reads=[f"X{t}"], writes=["LND", f"SSS{s}"])
            EMIT("ac", lambda e, t=t, s=s: e.activation(
                sq, Xt(t), AF.Copy, accum_out=col(S_SUM + s)),
                reads=[f"X{t}"], writes=["LND", f"SSUM{s}"])
            EMIT("dv", lambda e, s=s: e.tensor_scalar(
                col(S_NMEAN + s), col(S_SUM + s), -1.0 / 1024.0, None, OP.mult),
                reads=[f"SSUM{s}"], writes=[f"SNM{s}"])
            EMIT("ac", lambda e, s=s: e.activation(
                col(S_M2 + s), col(S_NMEAN + s), AF.Square),
                reads=[f"SNM{s}"], writes=[f"SM2{s}"])
            EMIT("dv", lambda e, s=s: e.scalar_tensor_tensor(
                col(S_VAR + s), col(S_SS + s), 1.0 / 1024.0, col(S_M2 + s),
                OP.mult, OP.subtract),
                reads=[f"SSS{s}", f"SM2{s}"], writes=[f"SVAR{s}"])
            EMIT("ac", lambda e, s=s: e.activation(
                col(S_STD + s), col(S_VAR + s), AF.Sqrt, bias=col(S_EPS)),
                reads=[f"SVAR{s}", "SEPS"], writes=[f"SSTD{s}"])
            EMIT("dv", lambda e, s=s: e.reciprocal(col(S_RSTD + s), col(S_STD + s)),
                 reads=[f"SSTD{s}"], writes=[f"SRSTD{s}"], inc=0)
            EMIT("dv", lambda e: e.drain(), inc=0)
            EMIT("dv", lambda e, t=t, s=s: e.tensor_scalar(
                Xt(t), Xt(t), col(S_NMEAN + s), col(S_RSTD + s), OP.add, OP.mult),
                reads=[f"SNM{s}", f"SRSTD{s}"], writes=[f"X{t}"])
            if rebuild:
                xt_build(t)

        for t in range(NT):
            xt_build(t)

        _dummy = [None]
        VTOK = RB
        QTA, QTB, KTA, KTB = RB + 16384, RB + 18432, RB + 20480, RB + 22528
        WVS = RB + 24576
        WQS = RB + 28672
        WKS = RB + 30720
        PTP = [RB + 24576, RB + 26624]
        PTOK = RB + 28672
        OSB = RB + 30720

        for dvh in range(2 if KPH >= 1 else 0):
            for k in range(ND):
                EMIT("gp", lambda e, k=k, dvh=dvh: e.dma_start(
                    ab(WVS + k * 512, 512),
                    wv_d[k * 128:(k + 1) * 128, dvh * 512:(dvh + 1) * 512]),
                    writes=(["WVS"] if k == 0 else []), inc=16)
            MARK("gp", "WVS")
            for nt in range(NT):
                pb = f"p{nt % 2}"
                for k in range(ND):
                    EMIT("pe", lambda e, k=k, nt=nt: e.matmul(
                        P[nt % 2][:, :], XTk(k, nt * 128, 128), ab(WVS + k * 512, 512),
                        start=(k == 0), stop=False),
                        reads=["WVS"] + xtr(nt * 128, 128), writes=[pb])
                EMIT("pe", lambda e, nt=nt, dvh=dvh: e.matmul(
                    P[nt % 2][:, :], ab(C_ONE, 128, 0, 1),
                    ab(C_BV + dvh * 512, 512, 0, 1), start=False, stop=True),
                    reads=["CONE", "CBV"], writes=[pb])
                EMIT("ac", lambda e, nt=nt, dvh=dvh: e.activation(
                    ab(VTOK + nt * 1024 + dvh * 512, 512), P[nt % 2][:, :], AF.Copy),
                    reads=[pb], writes=[f"VT{nt}_{dvh}"])

        for k in range(ND if KPH >= 1 else 0):
            EMIT("gp", lambda e, k=k: e.dma_start(
                ab(WQS + k * 256, 256), wq_d[k * 128:(k + 1) * 128, :]),
                writes=(["WQS"] if k == 0 else []), inc=16)
        MARK("gp", "WQS")
        for k in range(ND if KPH >= 1 else 0):
            EMIT("gp", lambda e, k=k: e.dma_start(
                ab(WKS + k * 256, 256), wk_d[k * 128:(k + 1) * 128, :]),
                writes=(["WKS"] if k == 0 else []), inc=16)
        MARK("gp", "WKS")
        EMIT("gp", lambda e: e.dma_start(wfcb_sp, wfc_d), writes=["WFCB"], inc=16)
        EMIT("gp", lambda e: e.dma_start(wpjb_sp, wpj_d), writes=["WPJB"], inc=16)
        for (dst, wsrc, wnm, g, bq, nm) in (() if KPH < 1 else (
                (QTA, WQS, "WQS", 0, S_BQA, "QTA"), (QTB, WQS, "WQS", 1, S_BQB, "QTB"),
                (KTA, WKS, "WKS", 0, S_BKA, "KTA"), (KTB, WKS, "WKS", 1, S_BKB, "KTB"))):
            for mc in range(4):
                pb = f"p{mc % 2}"
                for k in range(ND):
                    EMIT("pe", lambda e, k=k, mc=mc, wsrc=wsrc, g=g: e.matmul(
                        P[mc % 2][:, :], ab(wsrc + k * 256 + g * 128, 128),
                        XTk(k, mc * 512, 512), start=(k == 0), stop=(k == ND - 1)),
                        reads=[wnm] + xtr(mc * 512, 512), writes=[pb])
                EMIT("dv", lambda e, mc=mc, dst=dst, bq=bq: e.tensor_scalar(
                    ab(dst + mc * 512, 512), P[mc % 2][:, :], col(bq), None, OP.add),
                    reads=[pb, "SBQA", "SBQB", "SBKA", "SBKB"], writes=[nm])

        barrier()

        units = [(h, mt) for h in range(8 if KPH >= 2 else 0)
                 for mt in range(NT)]

        def l0a_scores(u):
            h, mt = units[u]
            g = h // 4
            base = (h % 4) * 32
            qsrc = QTA if g == 0 else QTB
            ksrc = KTA if g == 0 else KTB
            qnm = "QTA" if g == 0 else "QTB"
            knm = "KTA" if g == 0 else "KTB"
            for c in range(4):
                pb = f"p{c % 2}"
                tp = (96, 0) if base == 96 else None
                EMIT("pe", lambda e, c=c, mt=mt, base=base, qsrc=qsrc, ksrc=ksrc,
                     tp=tp: e.matmul(
                         P[c % 2][:, :],
                         Ab[base:base + 32, qsrc + mt * 128:qsrc + mt * 128 + 128],
                         Ab[base:base + 32, ksrc + c * 512:ksrc + c * 512 + 512],
                         start=True, stop=True, tile_position=tp),
                     reads=[qnm, knm], writes=[pb])
                sp = mt % 2
                EMIT("ac", lambda e, c=c, sp=sp: e.activation(
                    af(FS + sp * 2048 + c * 512, 512), P[c % 2][:, :], AF.Prelu,
                    scale=0.25, alpha=0.2),
                    reads=[pb], writes=[f"SLK{sp}"])

        def l0a_tail_a(u):
            h, mt = units[u]
            s = mt % 2
            EMIT("dv", lambda e, s=s: e.reduce_max(
                col(S_NEGMAX + s), af(FS + s * 2048, 2048), axis=AX, negate=True,
                opt_output=False),
                reads=[f"SLK{s}"], writes=[f"SNX{s}"], inc=0)
            EMIT("dv", lambda e: e.drain(), inc=0)
            EMIT("dv", lambda e: e.memset(col(S_JUNK), 0.0), writes=[f"SNX{s}"])

        def l0a_tail_b(u):
            h, mt = units[u]
            s = mt % 2
            EMIT("ac", lambda e, s=s: e.activation(
                ab(PTOK, 2048), af(FS + s * 2048, 2048), AF.Exp,
                bias=col(S_NEGMAX + s), scale=1.0, accum_out=col(S_R + s)),
                reads=[f"SLK{s}", f"SNX{s}"], writes=["PTOK", f"SR{s}"])
            EMIT("dv", lambda e, s=s: e.reciprocal(col(S_RREC + s), col(S_R + s)),
                 reads=[f"SR{s}"], writes=[f"SRC{s}"], inc=0)
            EMIT("dv", lambda e, s=s: e.tensor_tensor(
                col(S_GREC + s), col(S_RREC + s), col(S_GB), OP.mult),
                reads=["SGB", f"SRC{s}"], writes=[f"SGR{s}"])

        def l0a_back_a(u):
            h, mt = units[u]
            s = mt % 2
            for bk in range(2):
                pt = f"pt{bk}"
                for i in range(8):
                    EMIT("pe", lambda e, bk=bk, i=i: e.transpose(
                        PT[bk][:, i * 128:(i + 1) * 128],
                        ab(PTOK + (bk * 8 + i) * 128, 128), ab(C_ID, 128)),
                        reads=["PTOK", "CID"], writes=[pt])
            for bk in range(2):
                EMIT("dv", lambda e, bk=bk, s=s: e.tensor_copy(
                    ab(PTP[s] + bk * 1024, 1024), PT[bk][:, :]),
                    reads=[f"pt{bk}"], writes=[f"PTP{s}"])

        def l0a_back_b(u):
            h, mt = units[u]
            s = mt % 2
            for nt in range(NT):
                EMIT("pe", lambda e, nt=nt, h=h, s=s: e.matmul(
                    P[2][:, s * 128:s * 128 + 128],
                    ab(VTOK + nt * 1024 + h * 128, 128),
                    ab(PTP[s] + nt * 128, 128),
                    start=(nt == 0), stop=(nt == NT - 1)),
                    reads=[f"PTP{s}", f"VT{nt}_0", f"VT{nt}_1"], writes=["p2"])
            EMIT("dv", lambda e, s=s: e.tensor_copy(
                ab(OSB + s * 128, 128), P[2][:, s * 128:s * 128 + 128]),
                reads=["p2"], writes=[f"OSB{s}"])
            EMIT("pe", lambda e, s=s: e.transpose(
                PT[0][:, 0:128], ab(OSB + s * 128, 128), ab(C_ID, 128)),
                reads=[f"OSB{s}", "CID"], writes=["pt0"])
            EMIT("dv", lambda e, mt=mt, h=h, s=s: e.scalar_tensor_tensor(
                Xt(mt, h * 128, 128), PT[0][:, 0:128], col(S_GREC + s),
                Xt(mt, h * 128, 128), OP.mult, OP.add),
                reads=["pt0", f"SGR{s}", f"X{mt}"], writes=[f"X{mt}"])

        for u in range(len(units)):
            l0a_scores(u)
            if u >= 1:
                l0a_back_a(u - 1)
            l0a_tail_a(u)
            if u >= 1:
                l0a_back_b(u - 1)
            l0a_tail_b(u)
        if units:
            l0a_back_a(len(units) - 1)
            l0a_back_b(len(units) - 1)

        barrier()
        for t in range(NT if KPH >= 3 else 0):
            layernorm(t, t % 2)

        barrier()
        HT = RB
        WFS = [RB + 16384, RB + 20480]
        WPS = [RB + 24576, RB + 25088]
        TMB = [RB + 25600, RB + 26112]

        for mcq in range(4 if KPH >= 4 else 0):
            for fch in range(32):
                buf = (fch // 4) % 2
                if fch % 4 == 0:
                    for k in range(ND):
                        EMIT("gp", lambda e, k=k, fch=fch, buf=buf: e.dma_start(
                            ab(WFS[buf] + k * 512, 512),
                            wfcb_sp[k * 128:(k + 1) * 128,
                                    fch * 128:fch * 128 + 512]),
                            reads=["WFCB"],
                            writes=([f"WFS{buf}"] if k == 0 else []), inc=16)
                    MARK("gp", f"WFS{buf}")
                pb = f"p{fch % 2}"
                for k in range(ND):
                    EMIT("pe", lambda e, k=k, fch=fch, mcq=mcq, buf=buf: e.matmul(
                        P[fch % 2][:, :],
                        ab(WFS[buf] + k * 512 + (fch % 4) * 128, 128),
                        XTk(k, mcq * 512, 512), start=(k == 0), stop=(k == ND - 1)),
                        reads=[f"WFS{buf}"] + xtr(mcq * 512, 512), writes=[pb])
                tb = fch % 2
                EMIT("dv", lambda e, fch=fch, tb=tb: e.tensor_scalar(
                    ab(TMB[tb], 512), P[fch % 2][:, :], 0.0, None, OP.max),
                    reads=[pb], writes=[f"TMB{tb}"])
                EMIT("ac", lambda e, fch=fch, tb=tb: e.activation(
                    ab(HT + fch * 512, 512), ab(TMB[tb], 512), AF.Square),
                    reads=[f"TMB{tb}"], writes=[f"HT{fch}"])
            for dc in range(2):
                for ft in range(32):
                    buf = ft % 2
                    EMIT("sy", lambda e, ft=ft, dc=dc, buf=buf: e.dma_start(
                        ab(WPS[buf], 512),
                        wpjb_sp[ft * 128:(ft + 1) * 128, dc * 512:(dc + 1) * 512]),
                        reads=["WPJB"], writes=[f"WPS{buf}"], inc=16)
                    for mt4 in range(4):
                        EMIT("pe", lambda e, ft=ft, mt4=mt4, buf=buf: e.matmul(
                            P[2 + mt4][:, :], ab(HT + ft * 512 + mt4 * 128, 128),
                            ab(WPS[buf], 512), start=(ft == 0), stop=(ft == 31)),
                            reads=[f"WPS{buf}", f"HT{ft}"], writes=[f"p{2 + mt4}"])
                for mt4 in range(4):
                    mt = mcq * 4 + mt4
                    pb = f"p{2 + mt4}"
                    a_, b_ = af(FS, 512), af(FS + 512, 512)
                    EMIT("dv", lambda e, mt4=mt4: e.tensor_scalar(
                        af(FS, 512), P[2 + mt4][:, :], TH[0], None, OP.is_gt),
                        reads=[pb], writes=["ACCA"])
                    for i, (src, dst, rs_, ws) in enumerate(
                            ((a_, b_, "ACCA", "ACCB"), (b_, a_, "ACCB", "ACCA"),
                             (a_, b_, "ACCA", "ACCB"), (b_, a_, "ACCB", "ACCA"))):
                        EMIT("dv", lambda e, mt4=mt4, i=i, src=src, dst=dst:
                             e.scalar_tensor_tensor(
                                 dst, P[2 + mt4][:, :], TH[i + 1], src,
                                 OP.is_gt, OP.add),
                             reads=[pb, rs_], writes=[ws])
                    EMIT("dv", lambda e, mt=mt, dc=dc: e.scalar_tensor_tensor(
                        Xt(mt, dc * 512, 512), af(FS, 512), 0.2,
                        Xt(mt, dc * 512, 512), OP.mult, OP.add),
                        reads=["ACCA", f"X{mt}"], writes=[f"X{mt}"])
            for mt4 in range(4):
                mt = mcq * 4 + mt4
                layernorm(mt, mt % 2)

        barrier()
        LATQ = [RB, RB + 2048]
        LATK = [RB + 4096, RB + 6144]
        LATV = [RB + 8192, RB + 10240]
        QTJ = [RB + 12288, RB + 14336]
        KTJ = [RB + 16384, RB + 18432]
        WDQ = [RB + 20480, RB + 21504]
        WDK = [RB + 22528, RB + 23552]
        WDV = [RB + 24576, RB + 25600]
        RPC = RB + 26624
        RPS = RB + 28672
        TP1 = RB + 30720
        TP2 = RB + 31232

        EMIT("gp", lambda e: e.dma_start(ab(RPC, 2048), rc_d), writes=["RPC"], inc=16)
        EMIT("gp", lambda e: e.dma_start(ab(RPS, 2048), rs_d), writes=["RPS"], inc=16)

        for j in range(8 if KPH >= 5 else 0):
            par = j % 2
            for (wd, dram, nm) in ((WDQ, wqd_d, "WDQ"), (WDK, wkd_d, "WDK"),
                                   (WDV, wvd_d, "WDV")):
                for k in range(ND):
                    EMIT("gp", lambda e, k=k, j=j, wd=wd, dram=dram, par=par:
                         e.dma_start(ab(wd[par] + k * 128, 128),
                                     dram[k * 128:(k + 1) * 128, j * 128:(j + 1) * 128]),
                         writes=([f"{nm}{par}"] if k == 0 else []), inc=16)
                MARK("gp", f"{nm}{par}")
            for (wd, lat, nm, lnm) in ((WDQ, LATQ, "WDQ", "LATQ"),
                                       (WDK, LATK, "WDK", "LATK"),
                                       (WDV, LATV, "WDV", "LATV")):
                for mc in range(4):
                    pb = f"p{mc % 2}"
                    for k in range(ND):
                        EMIT("pe", lambda e, k=k, mc=mc, wd=wd, par=par: e.matmul(
                            P[mc % 2][:, :], ab(wd[par] + k * 128, 128),
                            XTk(k, mc * 512, 512), start=(k == 0), stop=(k == ND - 1)),
                            reads=[f"{nm}{par}"] + xtr(mc * 512, 512), writes=[pb])
                    EMIT("ac", lambda e, mc=mc, lat=lat, par=par: e.activation(
                        ab(lat[par] + mc * 512, 512), P[mc % 2][:, :], AF.Copy),
                        reads=[pb], writes=[f"{lnm}{par}"])
            EMIT("sy", lambda e, j=j, par=par: e.dma_start(
                lv_sp[j * 128:(j + 1) * 128, :], ab(LATV[par], 2048)),
                reads=[f"LATV{par}"], inc=16)
            for (lat, lnm, upo, unm, upso, usnm, qt, qnm, spill) in (
                    (LATQ, "LATQ", C_UPQ, "CUPQ", C_UPQS, "CUPQS", QTJ, "QTJ", qr_sp),
                    (LATK, "LATK", C_UPK, "CUPK", C_UPKS, "CUPKS", KTJ, "KTJ", kr_sp)):
                for mc in range(4):
                    EMIT("pe", lambda e, mc=mc, lat=lat, upo=upo, par=par: e.matmul(
                        P[2][:, :], ab(upo, 128), ab(lat[par] + mc * 512, 512),
                        start=True, stop=True),
                        reads=[f"{lnm}{par}", unm], writes=["p2"])
                    EMIT("pe", lambda e, mc=mc, lat=lat, upso=upso, par=par: e.matmul(
                        P[3][:, :], ab(upso, 128), ab(lat[par] + mc * 512, 512),
                        start=True, stop=True),
                        reads=[f"{lnm}{par}", usnm], writes=["p3"])
                    EMIT("dv", lambda e, mc=mc: e.tensor_tensor(
                        ab(TP1, 512), P[2][:, :], ab(RPC + mc * 512, 512), OP.mult),
                        reads=["p2", "RPC"], writes=["TP1"])
                    EMIT("dv", lambda e, mc=mc: e.tensor_tensor(
                        ab(TP2, 512), P[3][:, :], ab(RPS + mc * 512, 512), OP.mult),
                        reads=["p3", "RPS"], writes=["TP2"])
                    EMIT("dv", lambda e, mc=mc, qt=qt, par=par: e.tensor_tensor(
                        ab(qt[par] + mc * 512, 512), ab(TP1, 512), ab(TP2, 512),
                        OP.add),
                        reads=["TP1", "TP2"], writes=[f"{qnm}{par}"])
                EMIT("sy", lambda e, j=j, qt=qt, par=par, spill=spill: e.dma_start(
                    spill[j * 128:(j + 1) * 128, :], ab(qt[par], 2048)),
                    reads=[f"{qnm}{par}"], inc=16)

        barrier()
        KTH = [XTO, XTO + 2048]
        QTH = [XTO + 4096, XTO + 6144]
        LVH = [XTO + 8192, XTO + 10240]
        PTL = [XTO + 12288, RB + 27936]
        YT = RB
        VH = [RB + 16384, RB + 17424]
        PTMP = [RB + 18464, RB + 18720]
        YSTG = RB + 18976
        WC = RB + 19232
        RB1 = RB + 27424
        RB2 = RB + 27680

        for k in range(ND if KPH >= 6 else 0):
            EMIT("gp", lambda e, k=k: e.dma_start(
                ab(WC + k * 1024, 1024), wc_d[k * 128:(k + 1) * 128, :]),
                writes=(["WC"] if k == 0 else []), inc=16)
        MARK("gp", "WC")
        for par in range(2 if KPH >= 6 else 0):
            for nt in range(NT):
                EMIT("dv", lambda e, par=par, nt=nt: e.memset(
                    ab(VH[par] + nt * 65 + 64, 1), 1.0), writes=[f"VHO{par}"])

        units2 = []
        for h in range(16 if KPH >= 6 else 0):
            for mcq in range(8):
                units2.append((h, mcq))

        def l1_front(i):
            h, mcq = units2[i]
            par = h % 2
            avp = f"p{2 + i % 2}"
            if mcq == 0:
                EMIT("sy", lambda e, h=h, par=par: e.dma_start(
                    ab(LVH[par], 2048, 0, 64), lv_sp[h * 64:(h + 1) * 64, :]),
                    writes=[f"LVH{par}"], inc=16)
                EMIT("sy", lambda e, h=h, par=par: e.dma_start(
                    ab(KTH[par], 2048, 0, 64), kr_sp[h * 64:(h + 1) * 64, :]),
                    writes=[f"KTH{par}"], inc=16)
                EMIT("sy", lambda e, h=h, par=par: e.dma_start(
                    ab(QTH[par], 2048, 0, 64), qr_sp[h * 64:(h + 1) * 64, :]),
                    writes=[f"QTH{par}"], inc=16)
                for nt in range(NT):
                    pb = f"p{nt % 2}"
                    EMIT("pe", lambda e, nt=nt, par=par: e.matmul(
                        P[nt % 2][:, 0:64], ab(LVH[par] + nt * 128, 128, 0, 64),
                        ab(C_UPV1, 64, 0, 64), start=True, stop=True),
                        reads=[f"LVH{par}", "CUPV"], writes=[pb])
                    EMIT("dv", lambda e, nt=nt, par=par: e.tensor_copy(
                        ab(VH[par] + nt * 65, 64), P[nt % 2][:, 0:64]),
                        reads=[pb, f"VHO{par}"], writes=[f"VH{par}"])
            maxnt = 2 * mcq + 2
            pl = PTL[i % 2]
            for nt in range(maxnt):
                pb = f"p{nt % 2}"
                EMIT("pe", lambda e, nt=nt, mcq=mcq, par=par: e.matmul(
                    P[nt % 2][:, 0:256], ab(KTH[par] + nt * 128, 128, 0, 64),
                    ab(QTH[par] + mcq * 256, 256, 0, 64),
                    start=True, stop=True),
                    reads=[f"KTH{par}", f"QTH{par}"], writes=[pb])
                if nt < maxnt - 2:
                    EMIT("ac", lambda e, nt=nt, pl=pl: e.activation(
                        ab(pl + nt * 256, 256), P[nt % 2][:, 0:256], AF.Exp,
                        scale=0.125),
                        reads=[pb], writes=[f"PTL{i % 2}_{nt}"])
                else:
                    v = nt - 2 * mcq
                    pp = nt % 2
                    EMIT("ac", lambda e, pp=pp, nt=nt: e.activation(
                        ab(PTMP[pp], 256), P[nt % 2][:, 0:256], AF.Exp,
                        scale=0.125),
                        reads=[pb], writes=[f"PTMP{pp}"])
                    EMIT("dv", lambda e, nt=nt, v=v, pp=pp, pl=pl: e.tensor_tensor(
                        ab(pl + nt * 256, 256), ab(PTMP[pp], 256),
                        ab(C_M0 if v == 0 else C_M1, 256), OP.mult),
                        reads=[f"PTMP{pp}", "CM0", "CM1"], writes=[f"PTL{i % 2}_{nt}"])

        def l1_av(i):
            h, mcq = units2[i]
            par = h % 2
            avp = f"p{2 + i % 2}"
            maxnt = 2 * mcq + 2
            pl = PTL[i % 2]
            for nt in range(maxnt):
                EMIT("pe", lambda e, nt=nt, par=par, maxnt=maxnt, i=i, pl=pl: e.matmul(
                    P[2 + i % 2][0:65, 0:256], ab(VH[par] + nt * 65, 65),
                    ab(pl + nt * 256, 256),
                    start=(nt == 0), stop=(nt == maxnt - 1)),
                    reads=[f"VH{par}", f"PTL{i % 2}_{nt}"], writes=[avp])

        def l1_back(i):
            h, mcq = units2[i]
            avp = f"p{2 + i % 2}"
            AVP = P[2 + i % 2]
            EMIT("dv", lambda e, AVP=AVP: e.tensor_copy(
                af(FS + 768, 256, 0, 1), AVP[64:65, 0:256]),
                reads=[avp], writes=["RRAW"])
            for half in range(2):
                EMIT("pe", lambda e, half=half: e.transpose(
                    P[4][:, half:half + 1], af(FS + 768 + half * 128, 128, 0, 1),
                    af(S_IDF, 1, 0, 1)),
                    reads=["RRAW", "SIDF"], writes=["p4"])
            EMIT("dv", lambda e: e.reciprocal(af(S_RCL, 2), P[4][:, 0:2]),
                 reads=["p4"], writes=["SRCL"])
            for half in range(2):
                EMIT("pe", lambda e, half=half: e.transpose(
                    P[4][0:1, 4 + half * 128:132 + half * 128],
                    af(S_RCL + half, 1), af(S_IDF, 128)),
                    reads=["SRCL", "SIDF"], writes=["p4b"])
            EMIT("dv", lambda e: e.tensor_copy(ab(RB1, 256, 0, 1),
                                               P[4][0:1, 4:260]),
                 reads=["p4b"], writes=["RB1"])
            EMIT("dv", lambda e: e.tensor_tensor(
                af(FS + 512, 256, 0, 1), P[4][0:1, 4:260],
                ab(RB1, 256, 0, 1), OP.subtract),
                reads=["p4b", "RB1"], writes=["RRES"])
            EMIT("dv", lambda e: e.tensor_copy(ab(RB2, 256, 0, 1),
                                               af(FS + 512, 256, 0, 1)),
                 reads=["RRES"], writes=["RB2"])
            EMIT("pe", lambda e: e.matmul(
                P[5][0:64, 0:256], ab(C_ONE, 64, 0, 1), ab(RB1, 256, 0, 1),
                start=True, stop=False),
                reads=["CONE", "RB1"], writes=["p5"])
            EMIT("pe", lambda e: e.matmul(
                P[5][0:64, 0:256], ab(C_ONE, 64, 0, 1), ab(RB2, 256, 0, 1),
                start=False, stop=True),
                reads=["CONE", "RB2"], writes=["p5"])
            EMIT("dv", lambda e: e.tensor_copy(af(FS, 256, 0, 64),
                                               P[5][0:64, 0:256]),
                 reads=["p5"], writes=["BCF"])
            pt_ = h // 2
            if h % 2 == 0:
                EMIT("dv", lambda e, pt_=pt_, mcq=mcq, AVP=AVP: e.tensor_tensor(
                    ab(YT + pt_ * 2048 + mcq * 256, 256, 0, 64),
                    AVP[0:64, 0:256], af(FS, 256, 0, 64), OP.mult),
                    reads=[avp, "BCF"], writes=[f"YT{pt_}_{mcq}"])
            else:
                EMIT("dv", lambda e, AVP=AVP: e.tensor_tensor(
                    ab(YSTG, 256, 0, 64), AVP[0:64, 0:256], af(FS, 256, 0, 64),
                    OP.mult),
                    reads=[avp, "BCF"], writes=["YSTG"])
                EMIT("sy", lambda e, pt_=pt_, mcq=mcq: e.dma_start(
                    Ab[64:128, YT + pt_ * 2048 + mcq * 256:
                       YT + pt_ * 2048 + mcq * 256 + 256],
                    ab(YSTG, 256, 0, 64)),
                    reads=["YSTG"], writes=[f"YT{pt_}_{mcq}"], inc=16)

        for i in range(len(units2)):
            l1_front(i)
            if i >= 1:
                l1_av(i - 1)
            if i >= 2:
                l1_back(i - 2)
        if units2:
            l1_av(len(units2) - 1)
            l1_back(len(units2) - 2)
            l1_back(len(units2) - 1)

        barrier()
        for t in range(NT if KPH >= 7 else 0):
            readies = [f"YT{p_}_{t // 2}" for p_ in range(ND)]
            for dc in range(2):
                pb = f"p{4 + dc}"
                for k in range(ND):
                    EMIT("pe", lambda e, k=k, t=t, dc=dc: e.matmul(
                        P[4 + dc][:, :], ab(YT + k * 2048 + t * 128, 128),
                        ab(WC + k * 1024 + dc * 512, 512),
                        start=(k == 0), stop=(k == ND - 1)),
                        reads=readies + ["WC"], writes=[pb])
                EMIT("dv", lambda e, t=t, dc=dc: e.tensor_tensor(
                    Xt(t, dc * 512, 512), P[4 + dc][:, :], Xt(t, dc * 512, 512),
                    OP.add),
                    reads=[pb, f"X{t}"], writes=[f"X{t}"])
            layernorm(t, t % 2)

        barrier()
        CST = RB
        SNT = RB + 8192
        for (src1, src2, dst, dnm, useb) in (() if KPH < 8 else ((th_d, tn_d, CST, "CST", True),
                                             (ph_d, pn_d, SNT, "SNT", False))):
            for k in range(ND):
                for cc in range(2):
                    EMIT("sy", lambda e, k=k, cc=cc, src1=src1: e.dma_start(
                        af(FS, 512),
                        src1[k * 128:(k + 1) * 128, cc * 512:(cc + 1) * 512]),
                        writes=["TS1"], inc=16)
                    EMIT("sy", lambda e, k=k, cc=cc, src2=src2: e.dma_start(
                        af(FS + 512, 512),
                        src2[k * 128:(k + 1) * 128, cc * 512:(cc + 1) * 512]),
                        writes=["TS2"], inc=16)
                    EMIT("dv", lambda e: e.tensor_tensor(
                        af(FS, 512), af(FS, 512), af(FS + 512, 512), OP.add),
                        reads=["TS1", "TS2"], writes=["TS1"])
                    if useb:
                        EMIT("ac", lambda e, k=k, cc=cc, dst=dst: e.activation(
                            ab(dst + k * 1024 + cc * 512, 512), af(FS, 512),
                            AF.Sin, bias=col(S_HPI)),
                            reads=["TS1", "SHPI"], writes=[dnm])
                    else:
                        EMIT("ac", lambda e, k=k, cc=cc, dst=dst: e.activation(
                            ab(dst + k * 1024 + cc * 512, 512), af(FS, 512),
                            AF.Sin),
                            reads=["TS1"], writes=[dnm])
        for t in range(NT):
            for dc in range(2 if KPH >= 8 else 0):
                for k in range(ND):
                    EMIT("pe", lambda e, k=k, t=t, dc=dc: e.matmul(
                        P[0][:, :], XTk(k, t * 128, 128),
                        ab(CST + k * 1024 + dc * 512, 512),
                        start=(k == 0), stop=(k == ND - 1)),
                        reads=["CST"] + xtr(t * 128, 128), writes=["p0"])
                for k in range(ND):
                    EMIT("pe", lambda e, k=k, t=t, dc=dc: e.matmul(
                        P[1][:, :], XTk(k, t * 128, 128),
                        ab(SNT + k * 1024 + dc * 512, 512),
                        start=(k == 0), stop=(k == ND - 1)),
                        reads=["SNT"] + xtr(t * 128, 128), writes=["p1"])
                EMIT("ac", lambda e: e.activation(af(FS, 512), P[0][:, :], AF.Square),
                     reads=["p0"], writes=["RE2"])
                EMIT("ac", lambda e: e.activation(af(FS + 512, 512), P[1][:, :],
                                                  AF.Square),
                     reads=["p1"], writes=["IM2"])
                EMIT("dv", lambda e: e.tensor_tensor(
                    af(FS, 512), af(FS, 512), af(FS + 512, 512), OP.add),
                    reads=["RE2", "IM2"], writes=["RE2"])
                EMIT("ac", lambda e: e.activation(af(FS + 512, 512), af(FS, 512),
                                                  AF.Sqrt),
                     reads=["RE2"], writes=["IM2"])
                EMIT("dv", lambda e, t=t, dc=dc: e.tensor_tensor(
                    Xt(t, dc * 512, 512), af(FS + 512, 512), Xt(t, dc * 512, 512),
                    OP.add),
                    reads=["IM2", f"X{t}"], writes=[f"X{t}"])
            if KPH >= 8:
                layernorm(t, t % 2, rebuild=False)
            EMIT("sy", lambda e, t=t: e.dma_start(
                out_d[t * 128:(t + 1) * 128, :], Xt(t)),
                reads=[f"X{t}"], inc=16, sem="st")

        ops["sy"].append(lambda e: e.wait_ge(sems["st"], 16 * NT))

        @block.tensor
        def _(t_):
            for fn in ops["pe"]:
                fn(t_)

        @block.scalar
        def _(s_):
            for fn in ops["ac"]:
                fn(s_)

        @block.vector
        def _(v_):
            for fn in ops["dv"]:
                fn(v_)

        @block.gpsimd
        def _(g_):
            for fn in ops["gp"]:
                fn(g_)

        @block.sync
        def _(s_):
            for fn in ops["sy"]:
                fn(s_)

    return nc


def _host_prep(inp):
    f32 = np.float32
    d = {}
    Wq = np.asarray(inp["L0_attn_Wq"], f32)
    Wk = np.asarray(inp["L0_attn_Wk"], f32)
    wq_t = np.zeros((1024, 256), f32)
    wk_t = np.zeros((1024, 256), f32)
    bq_pad = np.zeros((256, 1), f32)
    bk_pad = np.zeros((256, 1), f32)
    bq = np.asarray(inp["L0_attn_bq"], f32)
    bk = np.asarray(inp["L0_attn_bk"], f32)
    for h in range(8):
        g, hh = h // 4, h % 4
        c0 = g * 128 + hh * 32
        wq_t[:, c0:c0 + 16] = Wq[h * 16:(h + 1) * 16, :].T
        wk_t[:, c0:c0 + 16] = Wk[h * 16:(h + 1) * 16, :].T
        bq_pad[c0:c0 + 16, 0] = bq[h * 16:(h + 1) * 16]
        bk_pad[c0:c0 + 16, 0] = bk[h * 16:(h + 1) * 16]
    d["wq_t"], d["wk_t"] = wq_t, wk_t
    d["bq_pad"], d["bk_pad"] = bq_pad, bk_pad
    d["wv_t"] = np.ascontiguousarray(np.asarray(inp["L0_attn_Wv"], f32).T)
    d["bv_r"] = np.asarray(inp["L0_attn_bv"], f32).reshape(1, 1024)
    d["gam"] = np.broadcast_to(
        np.asarray(inp["L0_attn_gamma"], f32).reshape(1, 1), (128, 1)).copy()
    d["wfc_t"] = np.ascontiguousarray(np.asarray(inp["L0_ffn_W_fc"], f32).T)
    d["wpj_t"] = np.ascontiguousarray(np.asarray(inp["L0_ffn_W_proj"], f32).T)
    d["wqd"] = np.asarray(inp["L1_attn_Wq_down"], f32)
    d["wkd"] = np.asarray(inp["L1_attn_Wk_down"], f32)
    d["wvd"] = np.asarray(inp["L1_attn_Wv_down"], f32)

    def upblock(c, e):
        U = np.concatenate([c, e], axis=1)
        out = np.zeros((128, 128), f32)
        out[0:64, 0:64] = U
        out[64:128, 64:128] = U
        return out

    def upswap(e):
        perm = np.arange(32)
        perm = perm + 1 - 2 * (perm % 2)
        Us = np.zeros((64, 64), f32)
        Us[:, 32:64] = e[:, perm]
        out = np.zeros((128, 128), f32)
        out[0:64, 0:64] = Us
        out[64:128, 64:128] = Us
        return out

    qc = np.asarray(inp["L1_attn_Wq_up_c"], f32)
    qe = np.asarray(inp["L1_attn_Wq_up_e"], f32)
    kc = np.asarray(inp["L1_attn_Wk_up_c"], f32)
    ke = np.asarray(inp["L1_attn_Wk_up_e"], f32)
    d["upq"] = upblock(qc, qe)
    d["upqs"] = upswap(qe)
    d["upk"] = upblock(kc, ke)
    d["upks"] = upswap(ke)
    d["wvu"] = np.asarray(inp["L1_attn_Wv_up"], f32)
    d["wc"] = np.asarray(inp["L1_attn_Wc"], f32)
    d["th"] = np.asarray(inp["L1_ffn_theta"], f32)
    d["tn"] = np.asarray(inp["L1_ffn_theta_noise"], f32)
    d["ph"] = np.asarray(inp["L1_ffn_phi"], f32)
    d["pn"] = np.asarray(inp["L1_ffn_phi_noise"], f32)
    d["identM"] = np.eye(128, dtype=f32)
    tpos = np.arange(T, dtype=np.float64)
    invf = 10000.0 ** (-np.arange(16, dtype=np.float64) / 16.0)
    ang = tpos[None, :] * invf[:, None]
    cosv = np.cos(ang)
    sinv = np.sin(ang)
    rc = np.zeros((128, T), np.float64)
    rs = np.zeros((128, T), np.float64)
    for r in range(128):
        j = r % 64
        if j < 32:
            rc[r] = 1.0
        else:
            i = j - 32
            rc[r] = cosv[i % 16]
            rs[r] = (-1.0 if i % 2 == 0 else 1.0) * sinv[i % 16]
    d["ropec"] = rc.astype(f32)
    d["ropes"] = rs.astype(f32)
    m = np.zeros((2, 128, 256), f32)
    for v in range(2):
        for nl in range(128):
            m[v, nl, :] = (128 * v + nl <= np.arange(256)).astype(f32)
    d["mask0"], d["mask1"] = m[0], m[1]
    return d


def kernel(**inputs):
    shared = _host_prep(inputs)
    x = np.ascontiguousarray(np.asarray(inputs["L0_attn_x"], np.float32))
    nc = build_nc()
    in_maps = [dict(shared, x=np.ascontiguousarray(x[i])) for i in range(8)]
    res = run_bass_kernel_spmd(nc, in_maps, core_ids=list(range(8)))
    out = np.stack([np.asarray(res.results[i]["out"], np.float32)
                    for i in range(8)])
    return out
